# version 12
# baseline (speedup 1.0000x reference)
import numpy as np
import ml_dtypes
import concourse.bass as bass
import concourse.mybir as mybir
from concourse.bass_utils import run_bass_kernel_spmd

F32 = mybir.dt.float32
F32R = mybir.dt.float32r
BF16 = mybir.dt.bfloat16
AF = mybir.ActivationFunctionType
ALU = mybir.AluOpType

H = 4096
OCB = 67
KCB = 32
ICB = 88
FFB = 32
NI = 11008
EPS = 1e-5
TPC = 1024
HT = 512
NCH = 4
QQ = 128
SEM_CAP = 40000


SAME_ENG_THRESH = 16


class Sched:
    ENGS = ("sync", "gpsimd", "vector", "scalar", "tensor")

    def __init__(self):
        self.ops = {e: [] for e in Sched.ENGS}
        self.last_w = {}
        self.readers = {}

    def op(self, eng, fn, reads=(), writes=(), deps=(), dma=False):
        waits = {}

        def add_dep(h):
            if h is None:
                return
            e, i = h
            kk = (e, self.ops[e][i]["dma"])
            waits[kk] = max(waits.get(kk, -1), i)

        for k in reads:
            if k in self.last_w:
                add_dep(self.last_w[k])
        for k in writes:
            if k in self.last_w:
                add_dep(self.last_w[k])
            for r in self.readers.get(k, ()):
                add_dep(r)
        for d in deps:
            add_dep(d)
        n = len(self.ops[eng])
        kc = (eng, False)
        if kc in waits:
            ti = waits[kc]
            if dma or eng == "tensor" or n - ti > SAME_ENG_THRESH:
                waits.pop(kc)
        idx = n
        self.ops[eng].append(dict(fn=fn, waits=waits, dma=dma))
        h = (eng, idx)
        for k in reads:
            self.readers.setdefault(k, []).append(h)
        for k in writes:
            self.last_w[k] = h
            self.readers[k] = []
        return h

    NLANE = 8

    def finalize(self, blk, sems, dsems):
        targets = {e: set() for e in Sched.ENGS}
        for e in Sched.ENGS:
            for i, o in enumerate(self.ops[e]):
                for (te, _td), ti in o["waits"].items():
                    targets[te].add(ti)
        seg_of, val_of, nseg = {}, {}, {}
        for e in Sched.ENGS:
            seg, val, ndma = 0, 0, 0
            for i, o in enumerate(self.ops[e]):
                if o["dma"]:
                    lane = ndma % Sched.NLANE
                    o["inc"] = ("dma", lane, 16 * (ndma // Sched.NLANE + 1))
                    seg_of[(e, i)] = lane
                    val_of[(e, i)] = 16 * (ndma // Sched.NLANE + 1)
                    ndma += 1
                    continue
                if i in targets[e]:
                    inc = 1
                    if val + inc > SEM_CAP:
                        seg += 1
                        val = 0
                    val += inc
                    o["inc"] = ("eng", seg, inc)
                else:
                    o["inc"] = None
                seg_of[(e, i)] = seg
                val_of[(e, i)] = val
            nseg[e] = seg + 1
        for e in Sched.ENGS:
            assert len(sems[e]) >= nseg[e], (e, nseg[e])

        def make_body(e):
            def body(eng):
                guar = {}
                for i, o in enumerate(self.ops[e]):
                    for (te, _td), ti in sorted(o["waits"].items()):
                        tdma = self.ops[te][ti]["dma"]
                        tseg, tval = seg_of[(te, ti)], val_of[(te, ti)]
                        assert tval > 0, (e, i, te, ti)
                        key = (te, "d" if tdma else "e", tseg)
                        if guar.get(key, -1) >= tval:
                            continue
                        sem = dsems[te][tseg] if tdma else sems[te][tseg]
                        eng.wait_ge(sem, tval)
                        guar[key] = tval
                        if not tdma:
                            for sgg in range(tseg):
                                guar[(te, "e", sgg)] = 1 << 30
                    inst = o["fn"](eng)
                    if o["inc"] is not None:
                        kind, sg, inc_or_val = o["inc"]
                        assert inst is not None, f"{e}:{i} inc w/o inst"
                        if kind == "dma":
                            inst.then_inc(dsems[e][sg], 16)
                        else:
                            inst.then_inc(sems[e][sg], inc_or_val)
            return body

        blk.sync(make_body("sync"))
        blk.gpsimd(make_body("gpsimd"))
        blk.vector(make_body("vector"))
        blk.scalar(make_body("scalar"))
        blk.tensor(make_body("tensor"))


def build_nc(dbg=False):
    nc = bass.Bass("TRN2", target_bir_lowering=False)

    def din(name, shape, dt=F32):
        return nc.dram_tensor(name, shape, dt, kind="ExternalInput").ap()

    xT = din("xT", [H, TPC])
    xPT = din("xPT", [H, TPC])
    Wip = din("Wip", [OCB, 128, KCB * 128])
    Wop = din("Wop", [FFB, 128, KCB * 128])
    WgT = din("WgT", [ICB, 128, KCB * 128])
    WuT = din("WuT", [ICB, 128, KCB * 128])
    WdT = din("WdT", [ICB, 128, FFB * 128])
    identf = din("identf", [128, 128])
    Umat = din("Umat", [128, 128])
    maskA = din("maskA", [128, 128])
    lnw_i = din("lnw_i", [128, 32])
    nw_i = din("nw_i", [128, 32])
    mup_i = din("mup_i", [128, OCB])
    dtb_i = din("dtb_i", [64, 1])
    alog_i = din("alog_i", [64, 1])
    Dexp_i = din("Dexp_i", [128, 32])
    convw_i = din("convw_i", [128, 136])

    out = nc.dram_tensor("out", [H, TPC], F32, kind="ExternalOutput").ap()
    DB = {}
    if dbg:
        for nm, shape in [("d_dts", [64, TPC]), ("d_h", [128, 4096]),
                          ("d_g0", [128, 4096]), ("d_attn", [128, 16384]),
                          ("d_scal", [1, 16]), ("d_gn", [128, 16384]),
                          ("d_xr", [128, 16384]),
                          ("d_sil", [128, 17536]), ("d_gall", [128, 8 * 4096]),
                          ("d_hM", [128, 2 * 4096])]:
            DB[nm] = nc.dram_tensor(nm, shape, F32, kind="ExternalOutput").ap()

    x2scr = nc.dram_tensor("x2scr", [KCB, 128, TPC], F32, kind="Internal").ap()
    intscr = nc.dram_tensor("intscr", [ICB, 128, TPC], BF16, kind="Internal").ap()
    qdscr = nc.dram_tensor("qdscr", [ICB, 128, FFB * 128], BF16, kind="Internal").ap()
    srowscr = nc.dram_tensor("srowscr", [64, 128], F32, kind="Internal").ap()
    zscr = nc.dram_tensor("zscr", [8, 128, 32, 128], BF16, kind="Internal").ap()
    foscr = nc.dram_tensor("foscr", [FFB, 128, TPC], BF16, kind="Internal").ap()

    ones1 = nc.alloc_sbuf_tensor("ones1", [1, 128], F32).ap()
    onesb = nc.alloc_sbuf_tensor("onesb", [128, 1], BF16).ap()
    onesf = nc.alloc_sbuf_tensor("onesf", [128, 1], F32).ap()
    AR = nc.alloc_sbuf_tensor("AR", [128, 106112], BF16).ap()
    ARf = AR.bitcast(F32)
    ARr = AR.bitcast(F32R)
    ptr = [0]

    def rgn(n):
        n = (n + 31) & ~31
        c0 = ptr[0]
        ptr[0] += n
        assert ptr[0] <= 106112, f"arena overflow {ptr[0]}"
        return c0

    def bf(c0, n, r0=0, nr=128):
        return AR[r0:r0 + nr, c0:c0 + n]

    def f32(c0, n, r0=0, nr=128):
        return ARf[r0:r0 + nr, c0 // 2:c0 // 2 + n]

    def f32r(c0, n, r0=0, nr=128):
        return ARr[r0:r0 + nr, c0 // 2:c0 // 2 + n]

    c_hT = rgn(8192)
    c_pool = rgn(12544)
    c_ib = rgn(128)
    c_if = rgn(256)
    c_U = rgn(256)
    c_mask = rgn(256)
    c_cw = rgn(272)
    c_mup = rgn(160)
    c_lnw = rgn(64)
    c_nw = rgn(64)
    c_Dx = rgn(64)
    c_sm = rgn(64)
    c_acc = rgn(256)
    c_halo = rgn(112)
    c_xr = rgn(16384)
    c_xbcs = rgn(17536)
    c_zs = rgn(8192)
    c_Xs = rgn(4096)
    c_g = rgn(4096)
    c_Wb = rgn(8192)
    c_dt = rgn(2048)
    c_ch = rgn(2432)
    c_Dm = rgn(512)
    c_E = rgn(256)
    c_Mt = rgn(256)
    c_Cx = rgn(256)
    c_Bw = rgn(256)
    c_Gt = rgn(128)
    c_hdec = rgn(2048)
    c_yb = rgn(2048)
    c_cv = rgn(2048)
    c_sq = rgn(1024)
    c_xt = rgn(2048)
    c_hbf = rgn(2048)
    c_srw = rgn(4160)
    c_onesA = rgn(256)
    c_abs = rgn(2048)
    assert ptr[0] <= 106112, ptr[0]

    c_gn = c_xr
    c_attn = c_xbcs
    c_ln3 = c_xr
    assert c_xbcs == c_xr + 16384 and c_zs == c_xbcs + 17536
    c_int = c_xr
    c_wq = c_zs
    c_qq = c_Wb
    c_qt = c_g
    c_wdq = c_Xs
    c_wdo = c_hbf
    assert c_g == c_Xs + 4096 and c_Wb == c_g + 4096
    c_qd = c_hbf

    def smc(i, r0=0, nr=128):
        return ARf[r0:r0 + nr, c_sm // 2 + i:c_sm // 2 + i + 1]

    sm_dtb = smc(0, 0, 64)
    sm_alog = smc(1, 0, 64)
    sm_Acol = smc(2, 0, 64)
    sm_T = [smc(3), smc(4), smc(5)]
    sm_negT = [smc(11), smc(12), smc(13)]
    sm_eta = smc(6)

    PL = {"sa": (0, 0), "sb": (0, 512), "t1": (0, 1024), "t2": (0, 1536),
          "al": (0, 2048), "be": (0, 2560),
          "mx": (0, 3200), "r3": (0, 4224), "mr3": (0, 5248)}

    def pool(nm, n=1024, off=0):
        r, cb = PL[nm]
        base = c_pool // 2 + cb + off
        return ARf[r:r + 1, base:base + n]

    def oA(r0):
        return ARf[r0:r0 + 1, c_onesA // 2:c_onesA // 2 + 128]

    def pbase(nm):
        return PL[nm][0]

    def chs(off, n, r0=0, nr=128):
        return ARf[r0:r0 + nr, c_ch // 2 + off:c_ch // 2 + off + n]

    sfm = lambda s: chs(s * 128, 128, 0, 64)
    wfm = lambda s: chs(256 + s * 128, 128, 0, 64)
    dtAf = chs(512, 128, 0, 64)
    lndf = chs(640, 128, 0, 64)
    dtATM = lambda s: chs(768, 64)
    vT = lambda s: chs(832 + s * 64, 64)
    wT = lambda s: chs(960 + s * 64, 64)
    B_TM = lambda s: bf(c_ch + 2176 + s * 128, 128)
    dts_h = f32(c_dt, 512, 0, 64)
    dtraw_h = f32(c_dt + 1024, 512, 0, 64)

    srow = lambda s: ARf[32 * s:32 * s + 1, c_srw // 2:c_srw // 2 + 1024]
    erow = lambda s: ARf[32 * s:32 * s + 1,
                         c_srw // 2 + 1024:c_srw // 2 + 2048]
    crow = lambda s: ARf[32 * s:32 * s + 1,
                         c_srw // 2 + 2048:c_srw // 2 + 2048 + 16]

    pf = [nc.alloc_psum_tensor(f"pf{i}", [128, 512], F32).ap() for i in range(6)]
    pb = [nc.alloc_psum_tensor(f"pb{i}", [128, 1024], BF16).ap() for i in range(2)]
    pbf = [p.bitcast(F32) for p in pb]

    acc = f32(c_acc, 128)
    S = Sched()

    def dma(q, dst, src, reads=(), writes=(), deps=()):
        return S.op(q, lambda e, d=dst, sr=src: e.dma_start(out=d, in_=sr),
                    reads=reads, writes=writes, deps=deps, dma=True)

    dma("sync", f32(c_U, 128), Umat, writes=["cU"])
    dma("sync", f32(c_mask, 128), maskA, writes=["cMk"])
    dma("sync", f32(c_cw, 136), convw_i, writes=["ccw"])
    dma("sync", f32(c_mup, 67), mup_i, writes=["cmup"])
    dma("sync", f32(c_lnw, 32), lnw_i, writes=["clnw"])
    dma("sync", f32(c_nw, 32), nw_i, writes=["cnw"])
    dma("sync", f32(c_Dx, 32), Dexp_i, writes=["cD"])
    dma("sync", sm_dtb, dtb_i, writes=["cdtb"])
    dma("sync", sm_alog, alog_i, writes=["cal"])
    dma("sync", f32(c_if, 128), identf, writes=["cIf"])
    dma("gpsimd", bf(c_ib, 128), identf, writes=["cIb"])
    eps_ap = smc(14)
    m1_ap = smc(15)
    nc.const_aps.register(F32, EPS, eps_ap)
    nc.const_aps.register(F32, -1.0, m1_ap)
    S.op("vector", lambda e: e.memset(eps_ap, EPS), writes=["constz"])
    S.op("vector", lambda e: e.memset(m1_ap, -1.0), writes=["constz"])
    S.op("vector", lambda e: e.memset(ones1, 1.0), writes=["o1"])
    S.op("vector", lambda e: e.memset(f32(c_onesA, 128), 1.0), writes=["o1"])
    S.op("vector", lambda e: e.memset(onesb, 1.0), writes=["ob"])
    S.op("vector", lambda e: e.memset(onesf, 1.0), writes=["of"])
    S.op("vector", lambda e: e.memset(f32(c_hT, 4096), 0.0),
         writes=[f"hT{g}" for g in range(8)])
    S.op("vector", lambda e: e.memset(bf(c_halo, 102), 0.0), writes=["halo"])
    S.op("scalar", lambda e: e.activation(out=sm_Acol, in_=sm_alog, func=AF.Exp),
         reads=["cal", "constz"], writes=["cA"])
    S.op("vector", lambda e: e.tensor_scalar_mul(out=sm_Acol, in0=sm_Acol,
                                                 scalar1=-1.0),
         reads=["cA"], writes=["cA"])

    absin = [f32(c_abs, 512), f32(c_abs + 1024, 512)]
    absout = bf(c_abs + 2048, 512)
    accn = [0]
    bg1, bg2 = [], []

    def emit_abs(mat, ib, j, si):
        sl = accn[0] % 2
        col = accn[0] % 88
        accn[0] += 1
        dma("scalar", absin[sl], mat[ib][:, j * 512:(j + 1) * 512],
            writes=[f"ai{sl}"])
        S.op("scalar",
             lambda e, sl=sl, col=col: e.activation(
                 out=absout, in_=absin[sl], func=AF.Abs,
                 accum_out=acc[:, col:col + 1]),
             reads=[f"ai{sl}"], writes=["ao", f"acc{col}"])
        if accn[0] % 88 == 0:
            S.op("vector", lambda e, si=si: e.reduce_sum(
                out=acc[:, 88 + si:89 + si], in_=acc[:, 0:88],
                axis=mybir.AxisListType.X),
                reads=[f"acc{k}" for k in range(88)], writes=[f"sum{si}"])

    _si = [0]
    for mat in (WgT, WuT, WdT):
        for ib in range(ICB):
            for j in range(8):
                bg1.append((emit_abs, mat, ib, j, (_si[0]) // 88))
                _si[0] += 1

    def emit_wdq(ib, j):
        sl = ib % 2
        wqa = f32(c_wdq + sl * 2048, 1024)
        qta = bf(c_wdo, 1024)
        wk = [f"wdqa{sl}"] + (["XsT"] if (ib == 0 and j == 0) else [])
        dma("sync", wqa, WdT[ib][:, j * 1024:(j + 1) * 1024], writes=wk)
        wk2 = ["wdqo"] + (["hbf0", "hbf1"] if (ib == 0 and j == 0) else [])
        S.op("scalar", lambda e, wqa=wqa, qta=qta: e.activation(
            out=qta, in_=wqa, func=AF.Sign, bias=sm_T[2]),
            reads=[f"wdqa{sl}", "Tc2"], writes=wk2)
        S.op("scalar", lambda e, wqa=wqa: e.activation(
            out=bf(c_qt, 1024), in_=wqa, func=AF.Sign, bias=sm_negT[2]),
            reads=[f"wdqa{sl}", "nTc2"],
            writes=["qt"] + (["gbuf"] if (ib == 0 and j == 0) else []))
        S.op("vector", lambda e, qta=qta: e.tensor_add(
            out=qta, in0=qta, in1=bf(c_qt, 1024)),
            reads=["wdqo", "qt"], writes=["wdqo"])
        dma("sync", qdscr[ib][:, j * 1024:(j + 1) * 1024], qta,
            reads=["wdqo"], writes=[f"qd{ib}_{j}"])

    def pump(n=1):
        for _ in range(n):
            if bg1:
                fn, *a = bg1.pop(0)
                fn(*a)
            elif bg2:
                fn, *a = bg2.pop(0)
                fn(*a)

    def stats_half(xsrc, hh, with_rms, statkey):
        for fc in range(KCB):
            sl = fc % 2
            xt = f32(c_xt + sl * 1024, 512)
            dma("sync", xt, xsrc[fc * 128:(fc + 1) * 128, hh * HT:(hh + 1) * HT],
                writes=[f"xt{sl}"])
            S.op("tensor", lambda e, xt=xt, fc=fc: e.matmul(
                out=pf[4][0:1, 0:512], lhsT=onesf, rhs=xt,
                start=(fc == 0), stop=(fc == KCB - 1)),
                reads=[f"xt{sl}", "of"], writes=["pf4"])
            sq = bf(c_sq + (fc % 2) * 512, 512)
            S.op("scalar", lambda e, xt=xt, sq=sq: e.activation(
                out=sq, in_=xt, func=AF.Square),
                reads=[f"xt{sl}"], writes=[f"sq{fc % 2}"])
            S.op("tensor", lambda e, sq=sq, fc=fc: e.matmul(
                out=pf[5][0:1, 0:512], lhsT=onesb, rhs=sq,
                start=(fc == 0), stop=(fc == KCB - 1)),
                reads=[f"sq{fc % 2}", "ob"], writes=["pf5"])
        S.op("vector", lambda e: e.tensor_copy(out=pool("sa", 512),
                                               in_=pf[4][0:1, 0:512]),
             reads=["pf4"], writes=["p_sa"])
        S.op("vector", lambda e: e.tensor_copy(out=pool("sb", 512),
                                               in_=pf[5][0:1, 0:512]),
             reads=["pf5"], writes=["p_sb"])
        S.op("vector", lambda e: e.tensor_scalar_mul(out=pool("t1", 512),
                                                     in0=pool("sa", 512),
                                                     scalar1=1.0 / H),
             reads=["p_sa"], writes=["p_t1"])
        S.op("vector", lambda e: e.tensor_scalar_mul(out=pool("sb", 512),
                                                     in0=pool("sb", 512),
                                                     scalar1=1.0 / H),
             reads=["p_sb"], writes=["p_sb"])
        S.op("vector", lambda e: e.tensor_mul(out=pool("t2", 512),
                                              in0=pool("t1", 512),
                                              in1=pool("t1", 512)),
             reads=["p_t1"], writes=["p_t2"])
        S.op("vector", lambda e: e.tensor_sub(out=pool("sb", 512),
                                              in0=pool("sb", 512),
                                              in1=pool("t2", 512)),
             reads=["p_sb", "p_t2"], writes=["p_sb"])
        S.op("scalar", lambda e: e.activation(out=pool("t2", 512),
                                              in_=pool("sb", 512),
                                              func=AF.Sqrt, bias=EPS),
             reads=["p_sb"], writes=["p_t2"])
        S.op("vector", lambda e: e.reciprocal(out=pool("t2", 512),
                                              in_=pool("t2", 512)),
             reads=["p_t2"], writes=["p_t2"])
        if with_rms:
            S.op("vector", lambda e: e.tensor_mul(out=pool("sa", 512),
                                                  in0=pool("t2", 512),
                                                  in1=pool("t2", 512)),
                 reads=["p_t2"], writes=["p_sa"])
            S.op("vector", lambda e: e.tensor_mul(out=pool("sa", 512),
                                                  in0=pool("sa", 512),
                                                  in1=pool("sb", 512)),
                 reads=["p_sa", "p_sb"], writes=["p_sa"])
            S.op("scalar", lambda e: e.activation(out=pool("sa", 512),
                                                  in_=pool("sa", 512),
                                                  func=AF.Sqrt, bias=EPS),
                 reads=["p_sa"], writes=["p_sa"])
            S.op("vector", lambda e: e.reciprocal(out=pool("sa", 512),
                                                  in_=pool("sa", 512)),
                 reads=["p_sa"], writes=["p_sa"])
            S.op("vector", lambda e: e.tensor_mul(out=pool("al", 512),
                                                  in0=pool("t2", 512),
                                                  in1=pool("sa", 512)),
                 reads=["p_t2", "p_sa"], writes=["p_al"])
        else:
            S.op("vector", lambda e: e.tensor_copy(out=pool("al", 512),
                                                   in_=pool("t2", 512)),
                 reads=["p_t2"], writes=["p_al"])
        S.op("vector", lambda e: e.tensor_mul(out=pool("be", 512),
                                              in0=pool("t1", 512),
                                              in1=pool("al", 512)),
             reads=["p_t1", "p_al"], writes=["p_be"])
        if statkey == "mx":
            S.op("vector", lambda e, hh=hh: e.tensor_copy(
                out=pool("mx", 512, hh * 512), in_=pool("t1", 512)),
                reads=["p_t1"], writes=[f"p_mx{hh}"])

    def bcast2(rowA, rowB, readsA, readsB, baseA=0, baseB=0):
        S.op("tensor", lambda e: e.matmul(out=pf[2][:, 0:512], lhsT=oA(baseA),
                                          rhs=rowA, start=True, stop=True),
             reads=list(readsA) + ["o1"], writes=["pf2"])
        S.op("tensor", lambda e: e.matmul(out=pf[3][:, 0:512], lhsT=oA(baseB),
                                          rhs=rowB, start=True, stop=True),
             reads=list(readsB) + ["o1"], writes=["pf3"])

    def xr_half(xsrc, hh):
        bcast2(pool("al", 512), pool("be", 512), ["p_al"], ["p_be"])
        for fc in range(KCB):
            sl = fc % 2
            xt = f32(c_xt + sl * 1024, 512)
            dma("sync", xt, xsrc[fc * 128:(fc + 1) * 128, hh * HT:(hh + 1) * HT],
                writes=[f"xt{sl}"])
            cvt = f32(c_cv + (fc % 2) * 1024, 512)
            S.op("vector", lambda e, xt=xt, cvt=cvt: e.tensor_tensor(
                out=cvt, in0=xt, in1=pf[2][:, 0:512], op=ALU.mult),
                reads=[f"xt{sl}", "pf2"], writes=[f"cv{fc % 2}"])
            S.op("vector", lambda e, cvt=cvt: e.tensor_tensor(
                out=cvt, in0=cvt, in1=pf[3][:, 0:512], op=ALU.subtract),
                reads=[f"cv{fc % 2}", "pf3"], writes=[f"cv{fc % 2}"])
            wx = ["xr"] + (["gn"] if fc == 0 else [])
            S.op("vector", lambda e, cvt=cvt, fc=fc: e.tensor_scalar_mul(
                out=bf(c_xr + fc * 512, 512), in0=cvt,
                scalar1=f32(c_lnw, 32)[:, fc:fc + 1]),
                reads=[f"cv{fc % 2}", "clnw"], writes=wx)
            pump(1)

    def sweep_inproj(blocks, hh):
        for bi, oc in enumerate(blocks):
            wsl = bi % 2
            wb = bf(c_Wb + wsl * 4096, 4096)
            dma("gpsimd", wb, Wip[oc], writes=[f"wb{wsl}"])
            bank = bi % 2
            for kc in range(KCB):
                S.op("tensor", lambda e, wb=wb, kc=kc, bank=bank: e.matmul(
                    out=pf[bank][:, 0:512], lhsT=wb[:, kc * 128:(kc + 1) * 128],
                    rhs=bf(c_xr + kc * 512, 512), start=(kc == 0),
                    stop=(kc == KCB - 1)),
                    reads=[f"wb{wsl}", "xr"], writes=[f"pf{bank}"])
            mupc = f32(c_mup, 67)[:, oc:oc + 1]
            if oc < 32:
                zev = bf(c_zs + (oc % 2) * 512, 512)
                S.op("scalar", lambda e, bank=bank, mupc=mupc, zev=zev: e.activation(
                    out=zev, in_=pf[bank][:, 0:512],
                    func=AF.Silu, scale=mupc),
                    reads=[f"pf{bank}", "cmup"], writes=[f"zev{oc % 2}", "zsl0"])
                for c2 in range(NCH):
                    dma("sync", zscr[hh * NCH + c2, :, oc, :],
                        zev[:, c2 * 128:(c2 + 1) * 128],
                        reads=[f"zev{oc % 2}"], writes=[f"zd{hh}_{c2}"])
            elif oc < 66:
                cc = oc - 32
                S.op("scalar", lambda e, cc=cc, bank=bank, mupc=mupc: e.activation(
                    out=bf(c_xbcs + cc * 515 + 3, 512), in_=pf[bank][:, 0:512],
                    func=AF.Copy, scale=mupc),
                    reads=[f"pf{bank}", "cmup"],
                    writes=[f"xbc{cc}", f"sil{cc}"] + (["attn"] if cc == 0 else []))
            else:
                S.op("scalar", lambda e, bank=bank: e.activation(
                    out=dtraw_h, in_=pf[bank][0:64, 0:512], func=AF.Copy,
                    scale=ARf[0:64, c_mup // 2 + oc:c_mup // 2 + oc + 1]),
                    reads=[f"pf{bank}", "cmup"], writes=["dtraw"])
            pump(1)

    def conv_half(hh):
        xb3 = AR[:, c_xbcs:c_xbcs + 17510].rearrange("p (c t) -> p c t", t=515)
        hl3 = bf(c_halo, 102).rearrange("p (c t) -> p c t", t=3)
        S.op("vector", lambda e: e.tensor_copy(out=xb3[:, :, 0:3], in_=hl3),
             reads=["halo"] + [f"xbc{cc}" for cc in range(34)],
             writes=[f"xbch{cc}" for cc in range(34)])
        S.op("vector", lambda e: e.tensor_copy(out=hl3, in_=xb3[:, :, 512:515]),
             reads=[f"xbc{cc}" for cc in range(34)], writes=["halo"])
        for cc in range(34):
            sl = cc % 2
            cvt = f32(c_cv + sl * 1024, 512)

            def stripe(k, cc=cc):
                return bf(c_xbcs + cc * 515 + k, 512)

            def wcol(k, cc=cc):
                return f32(c_cw, 136)[:, cc * 4 + k:cc * 4 + k + 1]

            S.op("vector", lambda e, cvt=cvt, cc=cc: e.tensor_scalar_mul(
                out=cvt, in0=stripe(0, cc), scalar1=wcol(0, cc)),
                reads=[f"xbc{cc}", f"xbch{cc}", "ccw"], writes=[f"cv{sl}"])
            for k in range(1, 4):
                S.op("vector", lambda e, cvt=cvt, cc=cc, k=k: e.scalar_tensor_tensor(
                    out=cvt, in0=stripe(k, cc), scalar=wcol(k, cc), in1=cvt,
                    op0=ALU.mult, op1=ALU.add),
                    reads=[f"xbc{cc}", f"xbch{cc}", "ccw", f"cv{sl}"],
                    writes=[f"cv{sl}"])
            S.op("scalar", lambda e, cvt=cvt, cc=cc: e.activation(
                out=bf(c_xbcs + cc * 515 + 3, 512), in_=cvt, func=AF.Silu),
                reads=[f"cv{sl}"], writes=[f"sil{cc}", f"xbc{cc}"])
            pump(1)

    def dt_half(hh):
        ta = f32(c_cv, 512, 0, 64)
        tb = f32(c_cv + 1024, 512, 0, 64)
        tc = f32(c_yb, 512, 0, 64)
        S.op("vector", lambda e: e.tensor_scalar_add(out=ta, in0=dtraw_h,
                                                     scalar1=sm_dtb),
             reads=["dtraw", "cdtb"], writes=["cv0"])
        S.op("vector", lambda e: e.tensor_scalar_max(out=tb, in0=ta, scalar1=0.0),
             reads=["cv0"], writes=["cv1"])
        S.op("scalar", lambda e: e.activation(out=tc, in_=ta, func=AF.Abs),
             reads=["cv0"], writes=["yb0"])
        S.op("scalar", lambda e: e.activation(out=tc, in_=tc, func=AF.Exp,
                                              scale=-1.0),
             reads=["yb0"], writes=["yb0"])
        S.op("scalar", lambda e: e.activation(out=tc, in_=tc, func=AF.Ln, bias=1.0),
             reads=["yb0"], writes=["yb0"])
        S.op("vector", lambda e: e.tensor_add(out=dts_h, in0=tb, in1=tc),
             reads=["cv1", "yb0"], writes=["dts"])

    def chunk2(hh, c2, is_R):
        t0 = c2 * QQ
        dts = dts_h[:, t0:t0 + QQ]
        csl = c2 % 2
        ta = f32(c_cv, 512, 0, 64)[:, 0:128]
        tb = f32(c_cv + 1024, 512, 0, 64)[:, 0:128]
        tc = f32(c_yb, 512, 0, 64)[:, 0:128]

        S.op("vector", lambda e: e.tensor_scalar_mul(out=dtAf, in0=dts,
                                                     scalar1=sm_Acol),
             reads=["dts", "cA"], writes=["dtA"])
        S.op("tensor", lambda e: e.transpose(out=pf[5][:, 0:64], in_=dtAf,
                                             identity=f32(c_if, 128)[0:64, 0:64]),
             reads=["dtA", "cIf", "pf5"], writes=["pf5"])
        S.op("vector", lambda e, csl=csl: e.tensor_copy(out=dtATM(csl),
                                                        in_=pf[5][:, 0:64]),
             reads=["pf5"], writes=["dtATM"])
        S.op("tensor", lambda e, csl=csl: e.matmul(
            out=pf[5][0:64, 64:192], lhsT=dtATM(csl), rhs=f32(c_U, 128),
            start=True, stop=True),
            reads=["dtATM", "cU", "pf5"], writes=["pf5"])
        S.op("vector", lambda e, csl=csl: e.tensor_copy(out=sfm(csl),
                                                        in_=pf[5][0:64, 64:192]),
             reads=["pf5"], writes=["sfm"])
        dma("sync", srowscr, sfm(csl), reads=["sfm"], writes=["srowd"])
        S.op("vector", lambda e, csl=csl: e.tensor_scalar(
            out=ta, in0=sfm(csl), scalar1=sfm(csl)[:, 127:128], scalar2=None,
            op0=ALU.subtract),
            reads=["sfm"], writes=["cv0"])
        S.op("scalar", lambda e: e.activation(out=tb, in_=ta, func=AF.Exp,
                                              scale=-1.0),
             reads=["cv0"], writes=["cv1"])
        S.op("vector", lambda e, csl=csl: e.tensor_mul(out=wfm(csl), in0=tb,
                                                       in1=dts),
             reads=["cv1", "dts"], writes=["wfm"])
        S.op("tensor", lambda e, csl=csl: e.transpose(
            out=pf[5][:, 192:256], in_=wfm(csl),
            identity=f32(c_if, 128)[0:64, 0:64]),
            reads=["wfm", "cIf", "pf5"], writes=["pf5"])
        S.op("vector", lambda e, csl=csl: e.tensor_copy(out=wT(csl),
                                                        in_=pf[5][:, 192:256]),
             reads=["pf5"], writes=["wTk"])
        if not is_R:
            S.op("scalar", lambda e, dts=dts: e.activation(out=lndf, in_=dts,
                                                           func=AF.Ln),
                 reads=["dts"], writes=["lndtk"])
            S.op("vector", lambda e, csl=csl: e.tensor_sub(out=tc, in0=lndf,
                                                           in1=sfm(csl)),
                 reads=["lndtk", "sfm"], writes=["yb0"])
            S.op("tensor", lambda e: e.transpose(
                out=pf[5][:, 256:320], in_=tc,
                identity=f32(c_if, 128)[0:64, 0:64]),
                reads=["yb0", "cIf", "pf5"], writes=["pf5"])
            S.op("vector", lambda e, csl=csl: e.tensor_copy(out=vT(csl),
                                                            in_=pf[5][:, 256:320]),
                 reads=["pf5"], writes=["vTk"])
        if not is_R:
            zsl_ = bf(c_zs + csl * 4096, 4096)
            dma("sync", zsl_, zscr[hh * NCH + c2],
                reads=[f"zd{hh}_{c2}"], writes=[f"zsl{csl}"])
        Bfm = bf(c_xbcs + 32 * 515 + 3 + t0, 128)
        Cfm = bf(c_xbcs + 33 * 515 + 3 + t0, 128)
        S.op("tensor", lambda e, Bfm=Bfm: e.transpose(
            out=pb[0][:, 0:128], in_=Bfm, identity=bf(c_ib, 128)),
            reads=["sil32", "cIb", "pb0"], writes=["pb0"])
        S.op("vector", lambda e, csl=csl: e.tensor_copy(out=B_TM(csl),
                                                        in_=pb[0][:, 0:128]),
             reads=["pb0"], writes=["BTM"])
        if not is_R:
            S.op("tensor", lambda e, Bfm=Bfm, Cfm=Cfm: e.matmul(
                out=pf[0][:, 0:128], lhsT=Bfm, rhs=Cfm, start=True, stop=True),
                reads=["sil32", "sil33", "pf0"], writes=["pf0"])
            S.op("vector", lambda e, csl=csl: e.tensor_copy(
                out=bf(c_Gt, 128), in_=pf[0][:, 0:128]),
                reads=["pf0"], writes=["GtB"])
        for fc in range(KCB):
            pbank = (fc // 8) % 2
            S.op("tensor", lambda e, fc=fc, pbank=pbank, t0=t0: e.transpose(
                out=pb[pbank][:, (fc % 8) * 128:(fc % 8 + 1) * 128],
                in_=bf(c_xbcs + fc * 515 + 3 + t0, 128), identity=bf(c_ib, 128)),
                reads=[f"sil{fc}", "cIb", f"pb{pbank}"], writes=[f"pb{pbank}"])
            if fc % 8 == 7:
                S.op("vector", lambda e, fc=fc, pbank=pbank: e.tensor_copy(
                    out=bf(c_Xs + (fc - 7) * 128, 1024),
                    in_=pb[pbank][:, 0:1024]),
                    reads=[f"pb{pbank}"], writes=["XsT"])

        for g8 in range(8):
            ssl = g8 % 2
            sr = srow(ssl)
            cr = crow(ssl)
            dma("sync", sr, srowscr[g8 * 8:(g8 + 1) * 8, :],
                reads=["srowd"], writes=[f"srow{ssl}"])
            S.op("scalar", lambda e, sr=sr, cr=cr: e.activation(
                out=cr[0:1, 0:8], in_=sr[0:1, 127::128], func=AF.Exp),
                reads=[f"srow{ssl}"], writes=[f"crow{ssl}"])
            S.op("tensor", lambda e, cr=cr, ssl=ssl: e.matmul(
                out=pf[5][:, 0:512], lhsT=oA(32 * ssl),
                rhs=cr[0:1, 0:8].rearrange("a (b c) -> a b c", c=1)
                .broadcast_to([1, 8, 64]),
                start=True, stop=True),
                reads=[f"crow{ssl}", "o1", "pf5"], writes=["pf5"])
            S.op("vector", lambda e, g8=g8: e.tensor_tensor(
                out=f32(c_hdec + (g8 % 2) * 1024, 512),
                in0=f32(c_hT, 4096)[:, g8 * 512:(g8 + 1) * 512],
                in1=pf[5][:, 0:512], op=ALU.mult),
                reads=[f"hT{g8}", "pf5"], writes=[f"hdec{g8 % 2}"])
            ybank = g8 % 2
            if not is_R:
                S.op("scalar", lambda e, sr=sr, ssl=ssl: e.activation(
                    out=erow(ssl), in_=sr, func=AF.Exp),
                    reads=[f"srow{ssl}"], writes=[f"erow{ssl}"])
                S.op("vector", lambda e, g8=g8, ssl=ssl: e.tensor_copy(
                    out=bf(c_hbf + ssl * 1024, 512),
                    in_=f32(c_hT, 4096)[:, g8 * 512:(g8 + 1) * 512]),
                    reads=[f"hT{g8}"], writes=[f"hbf{ssl}"])
                for j4 in range(2):
                    S.op("tensor", lambda e, sr=sr, j4=j4, ssl=ssl: e.matmul(
                        out=pf[3][:, 0:512], lhsT=oA(32 * ssl),
                        rhs=sr[0:1, j4 * 512:(j4 + 1) * 512],
                        start=True, stop=True),
                        reads=[f"srow{ssl}", "o1"], writes=["pf3"])
                    S.op("tensor", lambda e, j4=j4, ssl=ssl: e.matmul(
                        out=pf[4][:, 0:512], lhsT=oA(32 * ssl),
                        rhs=erow(ssl)[0:1, j4 * 512:(j4 + 1) * 512],
                        start=True, stop=True),
                        reads=[f"erow{ssl}", "o1"], writes=["pf4"])
                    for jj in range(4):
                        j8 = j4 * 4 + jj
                        r = g8 * 8 + j8
                        q = j8 // 2
                        pcol = (j8 % 2) * 64
                        msl = j8 % 2
                        scol = jj * 128
                        S.op("vector", lambda e, r=r, csl=csl, msl=msl,
                             scol=scol: e.scalar_tensor_tensor(
                            out=f32(c_Dm + msl * 256, 128),
                            in0=pf[3][:, scol:scol + 128],
                            scalar=vT(csl)[:, r:r + 1], in1=f32(c_mask, 128),
                            op0=ALU.add, op1=ALU.add),
                            reads=["pf3", "vTk", "cMk"], writes=[f"Dm{msl}"])
                        S.op("scalar", lambda e, msl=msl: e.activation(
                            out=bf(c_E + msl * 128, 128),
                            in_=f32(c_Dm + msl * 256, 128), func=AF.Exp),
                            reads=[f"Dm{msl}"], writes=[f"E{msl}"])
                        S.op("vector", lambda e, msl=msl: e.tensor_mul(
                            out=bf(c_Mt + msl * 128, 128),
                            in0=bf(c_E + msl * 128, 128),
                            in1=bf(c_Gt, 128)),
                            reads=[f"E{msl}", "GtB"], writes=[f"Mt{msl}"])
                        S.op("vector", lambda e, msl=msl, scol=scol,
                             Cfm=Cfm: e.tensor_tensor(
                            out=bf(c_Cx + msl * 128, 128), in0=Cfm,
                            in1=pf[4][:, scol:scol + 128], op=ALU.mult),
                            reads=["sil33", "pf4"], writes=[f"Cx{msl}"])
                        S.op("tensor", lambda e, r=r, msl=msl, q=q, pcol=pcol,
                             ybank=ybank: e.matmul(
                            out=pf[ybank][pcol:pcol + 64, q * 128:(q + 1) * 128],
                            lhsT=bf(c_Xs + r * 64, 64),
                            rhs=bf(c_Mt + msl * 128, 128),
                            start=True, stop=False, tile_position=(0, pcol)),
                            reads=["XsT", f"Mt{msl}"], writes=[f"pf{ybank}"])
                        S.op("tensor", lambda e, j8=j8, msl=msl, q=q, pcol=pcol,
                             ssl=ssl, ybank=ybank: e.matmul(
                            out=pf[ybank][pcol:pcol + 64, q * 128:(q + 1) * 128],
                            lhsT=bf(c_hbf + ssl * 1024, 512)[:, j8 * 64:(j8 + 1) * 64],
                            rhs=bf(c_Cx + msl * 128, 128),
                            start=False, stop=(j8 == 7), tile_position=(0, pcol)),
                            reads=[f"hbf{ssl}", f"Cx{msl}"], writes=[f"pf{ybank}"])
            S.op("tensor", lambda e, g8=g8: e.matmul(
                out=pf[2][:, 0:512], lhsT=f32(c_if, 128),
                rhs=f32(c_hdec + (g8 % 2) * 1024, 512),
                start=True, stop=False),
                reads=[f"hdec{g8 % 2}", "cIf", "pf2kk"], writes=["pf2"])
            for j8 in range(8):
                r = g8 * 8 + j8
                msl = j8 % 2
                S.op("vector", lambda e, r=r, msl=msl, csl=csl: e.tensor_scalar_mul(
                    out=bf(c_Bw + msl * 128, 128), in0=B_TM(csl),
                    scalar1=wT(csl)[:, r:r + 1]),
                    reads=["BTM", "wTk"], writes=[f"Bw{msl}"])
                S.op("tensor", lambda e, r=r, msl=msl, j8=j8, g8=g8: e.matmul(
                    out=pf[2][:, j8 * 64:(j8 + 1) * 64],
                    lhsT=bf(c_Bw + msl * 128, 128),
                    rhs=bf(c_Xs + r * 64, 64),
                    start=False, stop=(j8 == 7)),
                    reads=[f"Bw{msl}", "XsT"], writes=["pf2"])
            S.op("vector", lambda e, g8=g8: e.tensor_copy(
                out=f32(c_hT, 4096)[:, g8 * 512:(g8 + 1) * 512],
                in_=pf[2][:, 0:512]),
                reads=["pf2"], writes=[f"hT{g8}", "pf2kk"])
            if not is_R:
                for q in range(4):
                    fc = g8 * 4 + q
                    S.op("vector", lambda e, fc=fc, q=q, t0=t0, g8=g8,
                         ybank=ybank: e.scalar_tensor_tensor(
                        out=f32(c_yb + (g8 % 2) * 1024, 512)[:, q * 128:(q + 1) * 128],
                        in0=bf(c_xbcs + fc * 515 + 3 + t0, 128),
                        scalar=f32(c_Dx, 32)[:, fc:fc + 1],
                        in1=pf[ybank][:, q * 128:(q + 1) * 128],
                        op0=ALU.mult, op1=ALU.add),
                        reads=[f"sil{fc}", "cD", f"pf{ybank}"],
                        writes=[f"yb{g8 % 2}"])
                S.op("vector", lambda e, g8=g8, csl=csl: e.tensor_tensor(
                    out=bf(c_g + g8 * 512, 512),
                    in0=f32(c_yb + (g8 % 2) * 1024, 512),
                    in1=bf(c_zs + csl * 4096 + g8 * 512, 512), op=ALU.mult),
                    reads=[f"yb{g8 % 2}", f"zsl{csl}"], writes=["gbuf"])
            pump(1)
        if is_R:
            return
        for i8 in range(8):
            sq = bf(c_sq + (i8 % 2) * 512, 512)
            S.op("scalar", lambda e, i8=i8, sq=sq: e.activation(
                out=sq, in_=bf(c_g + i8 * 512, 512), func=AF.Square),
                reads=["gbuf"], writes=[f"sq{i8 % 2}"])
            S.op("tensor", lambda e, sq=sq, i8=i8: e.matmul(
                out=pf[0][0:1, 0:512], lhsT=onesb, rhs=sq,
                start=(i8 == 0), stop=(i8 == 7)),
                reads=[f"sq{i8 % 2}", "ob"], writes=["pf0"])
        r2 = pool("be", 128, 512)
        S.op("vector", lambda e: e.tensor_copy(out=pool("t1", 512),
                                               in_=pf[0][0:1, 0:512]),
             reads=["pf0"], writes=["p_t1c"])
        S.op("vector", lambda e: e.tensor_add(out=pool("t2", 128),
                                              in0=pool("t1", 128),
                                              in1=pool("t1", 128, 128)),
             reads=["p_t1c"], writes=["p_t2c"])
        S.op("vector", lambda e: e.tensor_add(out=pool("t2", 128, 128),
                                              in0=pool("t1", 128, 256),
                                              in1=pool("t1", 128, 384)),
             reads=["p_t1c"], writes=["p_t2c"])
        S.op("vector", lambda e, r2=r2: e.tensor_add(out=r2,
                                                     in0=pool("t2", 128),
                                                     in1=pool("t2", 128, 128)),
             reads=["p_t2c"], writes=["p_r2"])
        S.op("vector", lambda e, r2=r2: e.tensor_scalar_mul(out=r2, in0=r2,
                                                            scalar1=1.0 / H),
             reads=["p_r2"], writes=["p_r2"])
        S.op("scalar", lambda e, r2=r2: e.activation(out=r2, in_=r2,
                                                     func=AF.Sqrt, bias=EPS),
             reads=["p_r2"], writes=["p_r2"])
        S.op("vector", lambda e, r2=r2: e.reciprocal(out=r2, in_=r2),
             reads=["p_r2"], writes=["p_r2"])
        S.op("tensor", lambda e, r2=r2: e.matmul(out=pf[0][:, 0:128],
                                                 lhsT=ones1, rhs=r2,
                                                 start=True, stop=True),
             reads=["p_r2", "o1"], writes=["pf0"])
        for fc in range(KCB):
            S.op("vector", lambda e, fc=fc: e.tensor_tensor(
                out=f32(c_cv + (fc % 2) * 1024, 512)[:, 0:128],
                in0=bf(c_g + fc * 128, 128), in1=pf[0][:, 0:128], op=ALU.mult),
                reads=["gbuf", "pf0"], writes=[f"cv{fc % 2}"])
            wg_ = ["gn"] + (["xr"] if (fc == 0 and c2 == 0) else [])
            S.op("vector", lambda e, fc=fc, t0=t0: e.tensor_scalar_mul(
                out=bf(c_gn + fc * 512 + t0, 128),
                in0=f32(c_cv + (fc % 2) * 1024, 512)[:, 0:128],
                scalar1=f32(c_nw, 32)[:, fc:fc + 1]),
                reads=[f"cv{fc % 2}", "cnw"], writes=wg_)

    def postln_scalars(s1_src, s2_src, s1_reads, s2_reads):
        S.op("vector", lambda e: e.tensor_copy(out=pool("sa", 512), in_=s1_src),
             reads=s1_reads, writes=["p_sa"])
        S.op("vector", lambda e: e.tensor_copy(out=pool("sb", 512), in_=s2_src),
             reads=s2_reads, writes=["p_sb"])
        S.op("vector", lambda e: e.tensor_scalar_mul(out=pool("t1", 512),
                                                     in0=pool("sa", 512),
                                                     scalar1=1.0 / H),
             reads=["p_sa"], writes=["p_t1"])
        S.op("vector", lambda e: e.tensor_scalar_mul(out=pool("sb", 512),
                                                     in0=pool("sb", 512),
                                                     scalar1=1.0 / H),
             reads=["p_sb"], writes=["p_sb"])
        S.op("vector", lambda e: e.tensor_mul(out=pool("t2", 512),
                                              in0=pool("t1", 512),
                                              in1=pool("t1", 512)),
             reads=["p_t1"], writes=["p_t2"])
        S.op("vector", lambda e: e.tensor_sub(out=pool("sb", 512),
                                              in0=pool("sb", 512),
                                              in1=pool("t2", 512)),
             reads=["p_sb", "p_t2"], writes=["p_sb"])
        S.op("scalar", lambda e: e.activation(out=pool("t2", 512),
                                              in_=pool("sb", 512),
                                              func=AF.Sqrt, bias=EPS),
             reads=["p_sb"], writes=["p_t2"])
        S.op("vector", lambda e: e.reciprocal(out=pool("t2", 512),
                                              in_=pool("t2", 512)),
             reads=["p_t2"], writes=["p_t2"])
        S.op("vector", lambda e: e.tensor_mul(out=pool("t1", 512),
                                              in0=pool("t1", 512),
                                              in1=pool("t2", 512)),
             reads=["p_t1", "p_t2"], writes=["p_t1"])

    def outproj_half(hh):
        for oc in range(FFB):
            wsl = oc % 2
            wb = bf(c_Wb + wsl * 4096, 4096)
            dma("gpsimd", wb, Wop[oc], writes=[f"wb{wsl}"])
            bank = oc % 2
            for kc in range(KCB):
                S.op("tensor", lambda e, wb=wb, kc=kc, bank=bank: e.matmul(
                    out=pf[bank][:, 0:512], lhsT=wb[:, kc * 128:(kc + 1) * 128],
                    rhs=bf(c_gn + kc * 512, 512), start=(kc == 0),
                    stop=(kc == KCB - 1)),
                    reads=[f"wb{wsl}", "gn"], writes=[f"pf{bank}"])
            wka = ["attn"]
            if oc == 0:
                wka += [f"sil{cc}" for cc in range(34)] +                        [f"xbc{cc}" for cc in range(34)] +                        [f"xbch{cc}" for cc in range(34)]
            S.op("scalar", lambda e, oc=oc, bank=bank: e.activation(
                out=bf(c_attn + oc * 512, 512), in_=pf[bank][:, 0:512],
                func=AF.Copy),
                reads=[f"pf{bank}"], writes=wka)
            S.op("tensor", lambda e, oc=oc: e.matmul(
                out=pf[4][0:1, 0:512], lhsT=onesb,
                rhs=bf(c_attn + oc * 512, 512),
                start=(oc == 0), stop=(oc == FFB - 1)),
                reads=["attn", "ob"], writes=["pf4"])
            sq = bf(c_sq + (oc % 2) * 512, 512)
            S.op("scalar", lambda e, oc=oc, sq=sq: e.activation(
                out=sq, in_=bf(c_attn + oc * 512, 512), func=AF.Square),
                reads=["attn"], writes=[f"sq{oc % 2}"])
            S.op("tensor", lambda e, sq=sq, oc=oc: e.matmul(
                out=pf[5][0:1, 0:512], lhsT=onesb, rhs=sq,
                start=(oc == 0), stop=(oc == FFB - 1)),
                reads=[f"sq{oc % 2}", "ob"], writes=["pf5"])
            pump(1)
        postln_scalars(pf[4][0:1, 0:512], pf[5][0:1, 0:512], ["pf4"], ["pf5"])
        bcast2(pool("t2", 512), pool("t1", 512), ["p_t2"], ["p_t1"])
        for fc in range(KCB):
            sl = fc % 2
            xt = f32(c_xt + sl * 1024, 512)
            dma("sync", xt, xT[fc * 128:(fc + 1) * 128, hh * HT:(hh + 1) * HT],
                writes=[f"xt{sl}"])
            cvt = f32(c_cv + (fc % 2) * 1024, 512)
            S.op("vector", lambda e, fc=fc, cvt=cvt: e.tensor_tensor(
                out=cvt, in0=bf(c_attn + fc * 512, 512), in1=pf[2][:, 0:512],
                op=ALU.mult),
                reads=["attn", "pf2"], writes=[f"cv{fc % 2}"])
            S.op("vector", lambda e, cvt=cvt: e.tensor_tensor(
                out=cvt, in0=cvt, in1=pf[3][:, 0:512], op=ALU.subtract),
                reads=[f"cv{fc % 2}", "pf3"], writes=[f"cv{fc % 2}"])
            S.op("vector", lambda e, cvt=cvt, xt=xt: e.tensor_add(
                out=cvt, in0=cvt, in1=xt),
                reads=[f"cv{fc % 2}", f"xt{sl}"], writes=[f"cv{fc % 2}"])
            dma("sync", x2scr[fc, :, hh * HT:(hh + 1) * HT], cvt,
                reads=[f"cv{fc % 2}"], writes=[f"x2_{fc}_{hh}"])
            sq = bf(c_sq + (fc % 2) * 512, 512)
            S.op("scalar", lambda e, cvt=cvt, sq=sq: e.activation(
                out=sq, in_=cvt, func=AF.Square),
                reads=[f"cv{fc % 2}"], writes=[f"sq{fc % 2}"])
            S.op("tensor", lambda e, sq=sq, fc=fc: e.matmul(
                out=pf[4][0:1, 0:512], lhsT=onesb, rhs=sq,
                start=(fc == 0), stop=(fc == KCB - 1)),
                reads=[f"sq{fc % 2}", "ob"], writes=["pf4"])
            pump(1)
        r3 = pool("r3", 512, hh * 512)
        mr3 = pool("mr3", 512, hh * 512)
        S.op("vector", lambda e: e.tensor_copy(out=pool("sa", 512),
                                               in_=pf[4][0:1, 0:512]),
             reads=["pf4"], writes=["p_sa"])
        S.op("vector", lambda e: e.tensor_scalar_mul(out=pool("t1", 512),
                                                     in0=pool("sa", 512),
                                                     scalar1=1.0 / H),
             reads=["p_sa"], writes=["p_t1"])
        S.op("vector", lambda e, hh=hh: e.tensor_mul(
            out=pool("t2", 512), in0=pool("mx", 512, hh * 512),
            in1=pool("mx", 512, hh * 512)),
            reads=[f"p_mx{hh}"], writes=["p_t2"])
        S.op("vector", lambda e: e.tensor_sub(out=pool("t1", 512),
                                              in0=pool("t1", 512),
                                              in1=pool("t2", 512)),
             reads=["p_t1", "p_t2"], writes=["p_t1"])
        S.op("scalar", lambda e, r3=r3: e.activation(out=r3, in_=pool("t1", 512),
                                                     func=AF.Sqrt, bias=EPS),
             reads=["p_t1"], writes=["p_r3"])
        S.op("vector", lambda e, r3=r3: e.reciprocal(out=r3, in_=r3),
             reads=["p_r3"], writes=["p_r3"])
        S.op("vector", lambda e, r3=r3, mr3=mr3, hh=hh: e.tensor_mul(
            out=mr3, in0=pool("mx", 512, hh * 512), in1=r3),
            reads=["p_r3", f"p_mx{hh}"], writes=["p_mr3"])

    def emit_scales():
        for mi in range(3):
            red = ARf[:, c_acc // 2 + 116 + mi:c_acc // 2 + 117 + mi]
            S.op("vector", lambda e, red=red, mi=mi: e.reduce_sum(
                out=red, in_=acc[:, 88 + 8 * mi:88 + 8 * mi + 8],
                axis=mybir.AxisListType.X),
                reads=[f"sum{k}" for k in range(8 * mi, 8 * mi + 8)],
                writes=[f"red{mi}"])
            S.op("tensor", lambda e, red=red, mi=mi: e.matmul(
                out=pf[4][0:1, mi:mi + 1], lhsT=red, rhs=onesf,
                start=(mi == 0), stop=(mi == 2)),
                reads=[f"red{mi}", "of"], writes=["pf4"])
        S.op("vector", lambda e: e.tensor_copy(out=pool("t1", 3),
                                               in_=pf[4][0:1, 0:3]),
             reads=["pf4"], writes=["p_sc"])
        S.op("vector", lambda e: e.tensor_scalar_mul(
            out=pool("t1", 3, 4), in0=pool("t1", 3), scalar1=1.0 / (NI * H)),
            reads=["p_sc"], writes=["p_sc"])
        S.op("vector", lambda e: e.tensor_scalar(
            out=pool("t1", 3, 8), in0=pool("t1", 3, 4), scalar1=EPS, scalar2=0.5,
            op0=ALU.add, op1=ALU.mult),
            reads=["p_sc"], writes=["p_sc"])
        S.op("vector", lambda e: e.tensor_mul(
            out=pool("t1", 1, 12), in0=pool("t1", 1, 4), in1=pool("t1", 1, 5)),
            reads=["p_sc"], writes=["p_sc"])
        S.op("vector", lambda e: e.tensor_mul(
            out=pool("t1", 1, 12), in0=pool("t1", 1, 12), in1=pool("t1", 1, 6)),
            reads=["p_sc"], writes=["p_sc"])
        S.op("vector", lambda e: e.tensor_scalar_mul(
            out=pool("t1", 1, 12), in0=pool("t1", 1, 12), scalar1=1.0 / 8.0),
            reads=["p_sc"], writes=["p_sc"])
        for mi in range(3):
            S.op("tensor", lambda e, mi=mi: e.matmul(
                out=pf[4][:, 8 + mi:9 + mi], lhsT=ones1,
                rhs=pool("t1", 1, 8 + mi), start=True, stop=True),
                reads=["p_sc", "o1"], writes=["pf4"])
        S.op("tensor", lambda e: e.matmul(
            out=pf[4][:, 12:13], lhsT=ones1, rhs=pool("t1", 1, 12),
            start=True, stop=True),
            reads=["p_sc", "o1"], writes=["pf4"])
        for mi in range(3):
            S.op("vector", lambda e, mi=mi: e.tensor_copy(
                out=sm_T[mi], in_=pf[4][:, 8 + mi:9 + mi]),
                reads=["pf4"], writes=[f"Tc{mi}", "Tcs"])
            S.op("vector", lambda e, mi=mi: e.tensor_scalar_mul(
                out=sm_negT[mi], in0=sm_T[mi], scalar1=-1.0),
                reads=[f"Tc{mi}"], writes=[f"nTc{mi}", "Tcs"])
        S.op("vector", lambda e: e.tensor_copy(out=sm_eta, in_=pf[4][:, 12:13]),
             reads=["pf4"], writes=["eta"])
        if dbg:
            dma("sync", DB["d_scal"], pool("t1", 16), reads=["p_sc"],
                writes=["dscal"])

    for hh in range(2):
        stats_half(xPT, hh, True, "")
        xr_half(xPT, hh)
        sweep_inproj(list(range(32, OCB)), hh)
        conv_half(hh)
        dt_half(hh)
        for c2 in range(NCH):
            chunk2(hh, c2, is_R=True)
    if dbg:
        dma("sync", DB["d_h"], f32(c_hT, 4096),
            reads=[f"hT{g}" for g in range(8)], writes=["dh"])

    for hh in range(2):
        stats_half(xT, hh, True, "mx")
        xr_half(xT, hh)
        sweep_inproj(list(range(32, OCB)), hh)
        conv_half(hh)
        dt_half(hh)
        sweep_inproj(list(range(0, 32)), hh)
        if dbg and hh == 0:
            dma("gpsimd", DB["d_sil"], bf(c_xbcs, 17536),
                reads=[f"sil{cc}" for cc in range(34)], writes=["dsil"])
            dma("gpsimd", DB["d_xr"], bf(c_xr, 16384), reads=["xr"], writes=["dxr"])
        for c2 in range(NCH):
            chunk2(hh, c2, is_R=False)
            if dbg and hh == 0 and c2 == 0:
                dma("gpsimd", DB["d_g0"], bf(c_g, 4096), reads=["gbuf"],
                    writes=["dg0"])
            if dbg:
                dma("gpsimd", DB["d_gall"][:, (hh * NCH + c2) * 4096:
                                           (hh * NCH + c2 + 1) * 4096],
                    bf(c_g, 4096), reads=["gbuf"], writes=["dgall"])
        if dbg:
            dma("sync", DB["d_hM"][:, hh * 4096:(hh + 1) * 4096],
                f32(c_hT, 4096), reads=[f"hT{g}" for g in range(8)],
                writes=["dhM"])
        outproj_half(hh)
        if dbg and hh == 0:
            dma("gpsimd", DB["d_attn"], bf(c_attn, 16384), reads=["attn"],
                writes=["datt"])
            dma("gpsimd", DB["d_gn"], bf(c_gn, 16384), reads=["gn"], writes=["dgn"])
    if dbg:
        dma("sync", DB["d_dts"][:, 512:1024], dts_h, reads=["dts"],
            writes=["ddts"])

    while bg1:
        pump(4)
    while bg1:
        pump(1)
    emit_scales()
    S.op("vector", lambda e: e.engine_nop(),
         writes=["qt", "gbuf", "wdqa0", "wdqa1", "XsT", "wdqo", "hbf0", "hbf1",
                 "wqg0", "wqg1", "wqu0", "wqu1", "zev0", "zev1", "zsl0", "zsl1",
                 "q_wqg", "q_wqu", "wb0", "wb1", "xr", "attn", "gn", "halo", "ln3",
                 "cv0", "cv1", "it0", "it1", "yb0", "yb1"] +
                [f"sil{cc}" for cc in range(34)] +
                [f"xbc{cc}" for cc in range(34)] +
                [f"xbch{cc}" for cc in range(34)])
    for ib in range(ICB):
        for j in range(4):
            bg2.append((emit_wdq, ib, j))

    for hh in range(2):
        S.op("tensor", lambda e, hh=hh: e.matmul(
            out=pf[2][:, 0:512], lhsT=ones1,
            rhs=pool("r3", 512, hh * 512), start=True, stop=True),
            reads=["p_r3", "o1"], writes=["pf2"])
        S.op("tensor", lambda e, hh=hh: e.matmul(
            out=pf[3][:, 0:512], lhsT=ones1,
            rhs=pool("mr3", 512, hh * 512), start=True, stop=True),
            reads=["p_mr3", "o1"], writes=["pf3"])
        for fc in range(KCB):
            sl = fc % 2
            xt = f32(c_xt + sl * 1024, 512)
            dma("sync", xt, x2scr[fc, :, hh * HT:(hh + 1) * HT],
                reads=[f"x2_{fc}_{hh}"], writes=[f"xt{sl}"])
            cvt = f32(c_cv + (fc % 2) * 1024, 512)
            S.op("vector", lambda e, xt=xt, cvt=cvt: e.tensor_tensor(
                out=cvt, in0=xt, in1=pf[2][:, 0:512], op=ALU.mult),
                reads=[f"xt{sl}", "pf2"], writes=[f"cv{fc % 2}"])
            wkeys = ["ln3"]
            S.op("vector", lambda e, cvt=cvt, fc=fc, hh=hh: e.tensor_tensor(
                out=bf(c_ln3 + fc * 1024 + hh * 512, 512), in0=cvt,
                in1=pf[3][:, 0:512], op=ALU.subtract),
                reads=[f"cv{fc % 2}", "pf3"], writes=wkeys)
            pump(1)

    def quant_into(dst, WT, ib, Tb, nTb, rk):
        base = c_wq if rk == "wqg" else c_wq + 4096
        for j in range(4):
            wp = f32(base + (j % 2) * 2048, 1024)
            dma("sync", wp, WT[ib][:, j * 1024:(j + 1) * 1024],
                writes=[f"{rk}{j % 2}"])
            qtp = bf(c_yb, 1024)
            S.op("scalar", lambda e, wp=wp, j=j, dst=dst, Tb=Tb: e.activation(
                out=dst[:, j * 1024:(j + 1) * 1024], in_=wp, func=AF.Sign,
                bias=Tb),
                reads=[f"{rk}{j % 2}", "Tcs"], writes=["q_" + rk])
            S.op("scalar", lambda e, wp=wp, qtp=qtp, nTb=nTb: e.activation(
                out=qtp, in_=wp, func=AF.Sign, bias=nTb),
                reads=[f"{rk}{j % 2}", "Tcs"], writes=["qt"])
            S.op("vector", lambda e, j=j, dst=dst, qtp=qtp: e.tensor_add(
                out=dst[:, j * 1024:(j + 1) * 1024],
                in0=dst[:, j * 1024:(j + 1) * 1024], in1=qtp),
                reads=["q_" + rk, "qt"], writes=["q_" + rk])

    qg = bf(c_qq, 4096)
    qu = bf(c_qq + 4096, 4096)
    for ib in range(ICB):
        quant_into(qg, WgT, ib, sm_T[0], sm_negT[0], "wqg")
        for tt in range(2):
            for kc in range(KCB):
                S.op("tensor", lambda e, kc=kc, tt=tt: e.matmul(
                    out=pf[tt][:, 0:512], lhsT=qg[:, kc * 128:(kc + 1) * 128],
                    rhs=bf(c_ln3 + kc * 1024 + tt * 512, 512),
                    start=(kc == 0), stop=(kc == KCB - 1)),
                    reads=["q_wqg", "ln3"], writes=[f"pf{tt}"])
        quant_into(qu, WuT, ib, sm_T[1], sm_negT[1], "wqu")
        for tt in range(2):
            for kc in range(KCB):
                S.op("tensor", lambda e, kc=kc, tt=tt: e.matmul(
                    out=pf[2 + tt][:, 0:512], lhsT=qu[:, kc * 128:(kc + 1) * 128],
                    rhs=bf(c_ln3 + kc * 1024 + tt * 512, 512),
                    start=(kc == 0), stop=(kc == KCB - 1)),
                    reads=["q_wqu", "ln3"], writes=[f"pf{2 + tt}"])
            gt = f32(c_cv + tt * 1024, 512)
            S.op("scalar", lambda e, gt=gt, tt=tt: e.activation(
                out=gt, in_=pf[tt][:, 0:512], func=AF.Copy),
                reads=[f"pf{tt}"], writes=[f"cv{tt}"])
            it = bf(c_hdec + tt * 1024, 512)
            S.op("vector", lambda e, gt=gt, it=it, tt=tt: e.tensor_tensor(
                out=it, in0=gt, in1=pf[2 + tt][:, 0:512], op=ALU.mult),
                reads=[f"cv{tt}", f"pf{2 + tt}"], writes=[f"it{tt}"])
            dma("sync", intscr[ib, :, tt * HT:(tt + 1) * HT], it,
                reads=[f"it{tt}"], writes=[f"int{ib}_{tt}"])
        pump(2)
    while bg2:
        pump(4)

    S.op("vector", lambda e: e.engine_nop(),
         writes=["wqg0", "wqg1", "wqu0", "wqu1", "q_wqg", "q_wqu", "qt", "ln3",
                 "XsT", "gbuf", "wb0", "wb1", "wdqa0", "wdqa1", "wdqo",
                 "hbf0", "hbf1", "zev0", "zev1", "zsl0", "zsl1", "xr", "attn",
                 "gn", "cv0", "cv1", "it0", "it1", "yb0", "yb1", "frl0", "frl1"])
    OGS = [list(range(6 * i, min(6 * i + 6, FFB))) for i in range(6)]
    for tt in range(2):
        for ib in range(ICB):
            dma("sync", bf(c_int + ib * 512, 512),
                intscr[ib, :, tt * HT:(tt + 1) * HT],
                reads=[f"int{ib}_{tt}"], writes=[f"ih{ib}"])
        for ogi, og in enumerate(OGS):
            for ib in range(ICB):
                qsl = ib % 2
                qd = bf(c_qd + qsl * 1024, len(og) * 128)
                dma("sync", qd, qdscr[ib][:, og[0] * 128:(og[0] + len(og)) * 128],
                    reads=[f"qd{ib}_{og[0] // 8}", f"qd{ib}_{(og[-1]) // 8}"],
                    writes=[f"qdl{qsl}"])
                for j, oc in enumerate(og):
                    S.op("tensor", lambda e, qd=qd, j=j, ib=ib: e.matmul(
                        out=pf[j][:, 0:512], lhsT=qd[:, j * 128:(j + 1) * 128],
                        rhs=bf(c_int + ib * 512, 512),
                        start=(ib == 0), stop=(ib == ICB - 1)),
                        reads=[f"qdl{qsl}", f"ih{ib}"], writes=[f"pf{j}"])
            for j, oc in enumerate(og):
                fop = bf(c_yb + (oc % 2) * 1024, 512)
                S.op("scalar", lambda e, j=j, fop=fop: e.activation(
                    out=fop, in_=pf[j][:, 0:512], func=AF.Copy, scale=sm_eta),
                    reads=[f"pf{j}", "eta"], writes=[f"yb{oc % 2}"])
                dma("sync", foscr[oc, :, tt * HT:(tt + 1) * HT], fop,
                    reads=[f"yb{oc % 2}"], writes=[f"fo{oc}_{tt}"])
                S.op("tensor", lambda e, oc=oc, fop=fop: e.matmul(
                    out=pbf[0][0:1, 0:512], lhsT=onesb, rhs=fop,
                    start=(oc == 0), stop=(oc == FFB - 1)),
                    reads=[f"yb{oc % 2}", "ob"], writes=["pb0"])
                sq = bf(c_sq + (oc % 2) * 512, 512)
                S.op("scalar", lambda e, fop=fop, sq=sq: e.activation(
                    out=sq, in_=fop, func=AF.Square),
                    reads=[f"yb{oc % 2}"], writes=[f"sq{oc % 2}"])
                S.op("tensor", lambda e, sq=sq, oc=oc: e.matmul(
                    out=pbf[1][0:1, 0:512], lhsT=onesb, rhs=sq,
                    start=(oc == 0), stop=(oc == FFB - 1)),
                    reads=[f"sq{oc % 2}", "ob"], writes=["pb1"])
        postln_scalars(pbf[0][0:1, 0:512], pbf[1][0:1, 0:512], ["pb0"], ["pb1"])
        bcast2(pool("t2", 512), pool("t1", 512), ["p_t2"], ["p_t1"])
        for fc in range(KCB):
            sl = fc % 2
            xt = f32(c_xt + sl * 1024, 512)
            dma("sync", xt, x2scr[fc, :, tt * HT:(tt + 1) * HT],
                reads=[f"x2_{fc}_{tt}"], writes=[f"xt{sl}"])
            frl = bf(c_hdec + (fc % 2) * 1024, 512)
            dma("sync", frl, foscr[fc, :, tt * HT:(tt + 1) * HT],
                reads=[f"fo{fc}_{tt}"], writes=[f"frl{fc % 2}"])
            cvt = f32(c_cv + (fc % 2) * 1024, 512)
            S.op("vector", lambda e, fc=fc, cvt=cvt, frl=frl: e.tensor_tensor(
                out=cvt, in0=frl, in1=pf[2][:, 0:512], op=ALU.mult),
                reads=[f"frl{fc % 2}", "pf2"], writes=[f"cv{fc % 2}"])
            S.op("vector", lambda e, cvt=cvt: e.tensor_tensor(
                out=cvt, in0=cvt, in1=pf[3][:, 0:512], op=ALU.subtract),
                reads=[f"cv{fc % 2}", "pf3"], writes=[f"cv{fc % 2}"])
            S.op("vector", lambda e, cvt=cvt, xt=xt: e.tensor_add(
                out=cvt, in0=cvt, in1=xt),
                reads=[f"cv{fc % 2}", f"xt{sl}"], writes=[f"cv{fc % 2}"])
            dma("sync", out[fc * 128:(fc + 1) * 128, tt * HT:(tt + 1) * HT], cvt,
                reads=[f"cv{fc % 2}"], writes=[f"out{fc}_{tt}"])

    S.op("vector", lambda e: e.engine_nop(),
         reads=[f"out{fc}_{tt}" for fc in range(KCB) for tt in range(2)] +
               (["dscal", "dh", "dg0", "datt", "dgn", "ddts", "dsil",
                 "dxr"] if dbg else []),
         writes=["done"])

    sems = {}
    dsems = {}
    for e in Sched.ENGS:
        sems[e] = []
        dsems[e] = []
        for k in range(4):
            sems[e].append(nc.semaphore(f"sem_{e}_{k}").__enter__())
        if e in ("sync", "gpsimd", "scalar", "vector"):
            for k in range(Sched.NLANE):
                dsems[e].append(nc.semaphore(f"dsem_{e}_{k}").__enter__())
    blk_cm = nc.Block()
    blk = blk_cm.__enter__()
    S.finalize(blk, sems, dsems)
    blk_cm.__exit__(None, None, None)
    return nc


_NC_CACHE = {}


def _prep(inp):
    f = np.float32
    x = np.ascontiguousarray(inp["L0_attn_hidden_states"]).astype(f, copy=False)
    _ = inp["L0_attn_residual"]
    ipw = np.asarray(inp["L0_attn_in_proj_w"], f)
    opw = np.asarray(inp["L0_attn_out_proj_w"], f)
    conv_w = np.asarray(inp["L0_attn_conv_w"], f)
    A_log = np.asarray(inp["L0_attn_A_log"], f)
    dt_bias = np.asarray(inp["L0_attn_dt_bias"], f)
    D = np.asarray(inp["L0_attn_D"], f)
    ln_w = np.asarray(inp["L0_attn_ln_w"], f)
    norm_w = np.asarray(inp["L0_attn_norm_w"], f)
    mup = np.asarray(inp["L0_attn_mup_vector"], f)
    wg = np.asarray(inp["L0_ffn_w_gate"], f)
    wu = np.asarray(inp["L0_ffn_w_up"], f)
    wd = np.asarray(inp["L0_ffn_w_down"], f)

    ipw_p = np.zeros((H, OCB * 128), f)
    ipw_p[:, :PROJ_REAL] = ipw
    Wip = np.ascontiguousarray(
        ipw_p.reshape(KCB, 128, OCB, 128).transpose(2, 1, 0, 3)
    ).reshape(OCB, 128, KCB * 128)
    Wop = np.ascontiguousarray(
        opw.reshape(KCB, 128, FFB, 128).transpose(2, 1, 0, 3)
    ).reshape(FFB, 128, KCB * 128)

    def padT(w):
        wt = np.zeros((H, ICB * 128), f)
        wt[:, :NI] = w.T
        return np.ascontiguousarray(
            wt.reshape(KCB, 128, ICB, 128).transpose(2, 1, 0, 3)
        ).reshape(ICB, 128, KCB * 128)

    WgTt = padT(wg)
    WuTt = padT(wu)
    wdt = np.zeros((ICB * 128, H), f)
    wdt[:NI, :] = wd.T
    WdTt = np.ascontiguousarray(wdt.reshape(ICB, 128, FFB * 128))

    identf = np.eye(128, dtype=f)
    Umat = np.triu(np.ones((128, 128), f))
    maskA = np.where(Umat > 0, np.float32(0.0), np.float32(-1e30)).astype(f)
    lnw_t = np.ascontiguousarray(ln_w.reshape(32, 128).T)
    nw_t = np.ascontiguousarray(norm_w.reshape(32, 128).T)
    mup_p = np.zeros((OCB * 128,), f)
    mup_p[:PROJ_REAL] = mup
    mup_t = np.ascontiguousarray(mup_p.reshape(OCB, 128).T)
    Dexp = np.ascontiguousarray(D.reshape(32, 2)[:, np.repeat([0, 1], 64)].T)
    convw_t = np.ascontiguousarray(
        conv_w.reshape(34, 128, 4).transpose(1, 0, 2).reshape(128, 136))
    dtb_c = np.ascontiguousarray(dt_bias.reshape(64, 1))
    alog_c = np.ascontiguousarray(A_log.reshape(64, 1))

    shared = dict(Wip=Wip, Wop=Wop, WgT=WgTt, WuT=WuTt, WdT=WdTt,
                  identf=identf, Umat=Umat, maskA=maskA, lnw_i=lnw_t,
                  nw_i=nw_t, mup_i=mup_t, dtb_i=dtb_c, alog_i=alog_c,
                  Dexp_i=Dexp, convw_i=convw_t)
    in_maps = []
    zeros_half = np.zeros((H, TPC), f)
    for c in range(8):
        b, hh = c // 2, c % 2
        m = dict(shared)
        m["xT"] = np.ascontiguousarray(x[b, hh * TPC:(hh + 1) * TPC, :].T)
        m["xPT"] = (zeros_half if hh == 0
                    else np.ascontiguousarray(x[b, 0:TPC, :].T))
        in_maps.append(m)
    return in_maps


PROJ_REAL = 8512


def kernel(dbg=False, **inp):
    in_maps = _prep(inp)
    key = ("dbg" if dbg else "plain")
    if key not in _NC_CACHE:
        _NC_CACHE[key] = build_nc(dbg=dbg)
    nc = _NC_CACHE[key]
    res = run_bass_kernel_spmd(nc, in_maps, core_ids=list(range(8)))
    outf = np.empty((4, 2048, H), np.float32)
    for c in range(8):
        b, hh = c // 2, c % 2
        outf[b, hh * TPC:(hh + 1) * TPC, :] = \
            np.asarray(res.results[c]["out"], np.float32).T
    if dbg:
        return outf, res
    return outf


# version 13
# speedup vs baseline: 1.1707x; 1.1707x over previous
import numpy as np
import ml_dtypes
import concourse.bass as bass
import concourse.mybir as mybir
from concourse.bass_utils import run_bass_kernel_spmd

F32 = mybir.dt.float32
F32R = mybir.dt.float32r
BF16 = mybir.dt.bfloat16
AF = mybir.ActivationFunctionType
ALU = mybir.AluOpType

H = 4096
OCB = 67
KCB = 32
ICB = 88
FFB = 32
NI = 11008
EPS = 1e-5
TPC = 1024
HT = 512
NCH = 4
QQ = 128
SEM_CAP = 40000


SAME_ENG_THRESH = 16


class Sched:
    ENGS = ("sync", "gpsimd", "vector", "scalar", "tensor")

    def __init__(self):
        self.ops = {e: [] for e in Sched.ENGS}
        self.last_w = {}
        self.readers = {}

    def op(self, eng, fn, reads=(), writes=(), deps=(), dma=False):
        waits = {}

        def add_dep(h):
            if h is None:
                return
            e, i = h
            kk = (e, self.ops[e][i]["dma"])
            waits[kk] = max(waits.get(kk, -1), i)

        for k in reads:
            if k in self.last_w:
                add_dep(self.last_w[k])
        for k in writes:
            if k in self.last_w:
                add_dep(self.last_w[k])
            for r in self.readers.get(k, ()):
                add_dep(r)
        for d in deps:
            add_dep(d)
        n = len(self.ops[eng])
        kc = (eng, False)
        if kc in waits:
            ti = waits[kc]
            if dma or eng == "tensor" or n - ti > SAME_ENG_THRESH:
                waits.pop(kc)
        idx = n
        self.ops[eng].append(dict(fn=fn, waits=waits, dma=dma))
        h = (eng, idx)
        for k in reads:
            self.readers.setdefault(k, []).append(h)
        for k in writes:
            self.last_w[k] = h
            self.readers[k] = []
        return h

    NLANE = 8

    def finalize(self, blk, sems, dsems):
        targets = {e: set() for e in Sched.ENGS}
        for e in Sched.ENGS:
            for i, o in enumerate(self.ops[e]):
                for (te, _td), ti in o["waits"].items():
                    targets[te].add(ti)
        seg_of, val_of, nseg = {}, {}, {}
        for e in Sched.ENGS:
            seg, val, ndma = 0, 0, 0
            for i, o in enumerate(self.ops[e]):
                if o["dma"]:
                    lane = ndma % Sched.NLANE
                    o["inc"] = ("dma", lane, 16 * (ndma // Sched.NLANE + 1))
                    seg_of[(e, i)] = lane
                    val_of[(e, i)] = 16 * (ndma // Sched.NLANE + 1)
                    ndma += 1
                    continue
                if i in targets[e]:
                    inc = 1
                    if val + inc > SEM_CAP:
                        seg += 1
                        val = 0
                    val += inc
                    o["inc"] = ("eng", seg, inc)
                else:
                    o["inc"] = None
                seg_of[(e, i)] = seg
                val_of[(e, i)] = val
            nseg[e] = seg + 1
        for e in Sched.ENGS:
            assert len(sems[e]) >= nseg[e], (e, nseg[e])

        def make_body(e):
            def body(eng):
                guar = {}
                for i, o in enumerate(self.ops[e]):
                    for (te, _td), ti in sorted(o["waits"].items()):
                        tdma = self.ops[te][ti]["dma"]
                        tseg, tval = seg_of[(te, ti)], val_of[(te, ti)]
                        assert tval > 0, (e, i, te, ti)
                        key = (te, "d" if tdma else "e", tseg)
                        if guar.get(key, -1) >= tval:
                            continue
                        sem = dsems[te][tseg] if tdma else sems[te][tseg]
                        eng.wait_ge(sem, tval)
                        guar[key] = tval
                        if not tdma:
                            for sgg in range(tseg):
                                guar[(te, "e", sgg)] = 1 << 30
                    inst = o["fn"](eng)
                    if o["inc"] is not None:
                        kind, sg, inc_or_val = o["inc"]
                        assert inst is not None, f"{e}:{i} inc w/o inst"
                        if kind == "dma":
                            inst.then_inc(dsems[e][sg], 16)
                        else:
                            inst.then_inc(sems[e][sg], inc_or_val)
            return body

        blk.sync(make_body("sync"))
        blk.gpsimd(make_body("gpsimd"))
        blk.vector(make_body("vector"))
        blk.scalar(make_body("scalar"))
        blk.tensor(make_body("tensor"))


def build_nc(dbg=False):
    nc = bass.Bass("TRN2", target_bir_lowering=False)

    def din(name, shape, dt=F32):
        return nc.dram_tensor(name, shape, dt, kind="ExternalInput").ap()

    xT = din("xT", [H, TPC])
    xPT = din("xPT", [H, TPC])
    Wip = din("Wip", [OCB, 128, KCB * 128])
    Wop = din("Wop", [FFB, 128, KCB * 128])
    WgT = din("WgT", [ICB, 128, KCB * 128])
    WuT = din("WuT", [ICB, 128, KCB * 128])
    WdT = din("WdT", [ICB, 128, FFB * 128])
    identf = din("identf", [128, 128])
    Umat = din("Umat", [128, 128])
    maskA = din("maskA", [128, 128])
    lnw_i = din("lnw_i", [128, 32])
    nw_i = din("nw_i", [128, 32])
    mup_i = din("mup_i", [128, OCB])
    dtb_i = din("dtb_i", [64, 1])
    alog_i = din("alog_i", [64, 1])
    Dexp_i = din("Dexp_i", [128, 32])
    convw_i = din("convw_i", [128, 136])

    out = nc.dram_tensor("out", [H, TPC], F32, kind="ExternalOutput").ap()
    DB = {}
    if dbg:
        for nm, shape in [("d_dts", [64, TPC]), ("d_h", [128, 4096]),
                          ("d_g0", [128, 4096]), ("d_attn", [128, 16384]),
                          ("d_scal", [1, 16]), ("d_gn", [128, 16384]),
                          ("d_xr", [128, 16384]),
                          ("d_sil", [128, 17536]), ("d_gall", [128, 8 * 4096]),
                          ("d_hM", [128, 2 * 4096])]:
            DB[nm] = nc.dram_tensor(nm, shape, F32, kind="ExternalOutput").ap()

    x2scr = nc.dram_tensor("x2scr", [KCB, 128, TPC], F32, kind="Internal").ap()
    intscr = nc.dram_tensor("intscr", [ICB, 128, TPC], BF16, kind="Internal").ap()
    qdscr = nc.dram_tensor("qdscr", [ICB, 128, FFB * 128], BF16, kind="Internal").ap()
    srowscr = nc.dram_tensor("srowscr", [64, 128], F32, kind="Internal").ap()
    zscr = nc.dram_tensor("zscr", [8, 128, 32, 128], BF16, kind="Internal").ap()
    foscr = nc.dram_tensor("foscr", [FFB, 128, TPC], BF16, kind="Internal").ap()

    ones1 = nc.alloc_sbuf_tensor("ones1", [1, 128], F32).ap()
    onesb = nc.alloc_sbuf_tensor("onesb", [128, 1], BF16).ap()
    onesf = nc.alloc_sbuf_tensor("onesf", [128, 1], F32).ap()
    AR = nc.alloc_sbuf_tensor("AR", [128, 106112], BF16).ap()
    ARf = AR.bitcast(F32)
    ARr = AR.bitcast(F32R)
    ptr = [0]

    def rgn(n):
        n = (n + 31) & ~31
        c0 = ptr[0]
        ptr[0] += n
        assert ptr[0] <= 106112, f"arena overflow {ptr[0]}"
        return c0

    def bf(c0, n, r0=0, nr=128):
        return AR[r0:r0 + nr, c0:c0 + n]

    def f32(c0, n, r0=0, nr=128):
        return ARf[r0:r0 + nr, c0 // 2:c0 // 2 + n]

    def f32r(c0, n, r0=0, nr=128):
        return ARr[r0:r0 + nr, c0 // 2:c0 // 2 + n]

    c_hT = rgn(8192)
    c_pool = rgn(12544)
    c_ib = rgn(128)
    c_if = rgn(256)
    c_U = rgn(256)
    c_mask = rgn(256)
    c_cw = rgn(272)
    c_mup = rgn(160)
    c_lnw = rgn(64)
    c_nw = rgn(64)
    c_Dx = rgn(64)
    c_sm = rgn(64)
    c_acc = rgn(256)
    c_halo = rgn(112)
    c_xr = rgn(16384)
    c_xbcs = rgn(17536)
    c_zs = rgn(8192)
    c_Xs = rgn(4096)
    c_g = rgn(4096)
    c_Wb = rgn(8192)
    c_dt = rgn(2048)
    c_ch = rgn(2432)
    c_Dm = rgn(512)
    c_E = rgn(256)
    c_Mt = rgn(256)
    c_Cx = rgn(256)
    c_Bw = rgn(256)
    c_Gt = rgn(128)
    c_hdec = rgn(2048)
    c_yb = rgn(2048)
    c_cv = rgn(2048)
    c_sq = rgn(1024)
    c_xt = rgn(2048)
    c_hbf = rgn(2048)
    c_srw = rgn(4160)
    c_onesA = rgn(256)
    c_abs = rgn(2048)
    assert ptr[0] <= 106112, ptr[0]

    c_gn = c_xr
    c_attn = c_xbcs
    c_ln3 = c_xr
    assert c_xbcs == c_xr + 16384 and c_zs == c_xbcs + 17536
    c_int = c_xr
    c_wq = c_zs
    c_qq = c_Wb
    c_qt = c_g
    c_wdq = c_Xs
    c_wdo = c_hbf
    assert c_g == c_Xs + 4096 and c_Wb == c_g + 4096
    c_qd = c_hbf

    def smc(i, r0=0, nr=128):
        return ARf[r0:r0 + nr, c_sm // 2 + i:c_sm // 2 + i + 1]

    sm_dtb = smc(0, 0, 64)
    sm_alog = smc(1, 0, 64)
    sm_Acol = smc(2, 0, 64)
    sm_T = [smc(3), smc(4), smc(5)]
    sm_negT = [smc(11), smc(12), smc(13)]
    sm_eta = smc(6)

    PL = {"sa": (0, 0), "sb": (0, 512), "t1": (0, 1024), "t2": (0, 1536),
          "al": (0, 2048), "be": (0, 2560),
          "mx": (0, 3200), "r3": (0, 4224), "mr3": (0, 5248)}

    def pool(nm, n=1024, off=0):
        r, cb = PL[nm]
        base = c_pool // 2 + cb + off
        return ARf[r:r + 1, base:base + n]

    def oA(r0):
        return ARf[r0:r0 + 1, c_onesA // 2:c_onesA // 2 + 128]

    def pbase(nm):
        return PL[nm][0]

    def chs(off, n, r0=0, nr=128):
        return ARf[r0:r0 + nr, c_ch // 2 + off:c_ch // 2 + off + n]

    sfm = lambda s: chs(s * 128, 128, 0, 64)
    wfm = lambda s: chs(256 + s * 128, 128, 0, 64)
    dtAf = chs(512, 128, 0, 64)
    lndf = chs(640, 128, 0, 64)
    dtATM = lambda s: chs(768, 64)
    vT = lambda s: chs(832 + s * 64, 64)
    wT = lambda s: chs(960 + s * 64, 64)
    B_TM = lambda s: bf(c_ch + 2176 + s * 128, 128)
    dts_h = f32(c_dt, 512, 0, 64)
    dtraw_h = f32(c_dt + 1024, 512, 0, 64)

    srow = lambda s: ARf[32 * s:32 * s + 1, c_srw // 2:c_srw // 2 + 1024]
    erow = lambda s: ARf[32 * s:32 * s + 1,
                         c_srw // 2 + 1024:c_srw // 2 + 2048]
    crow = lambda s: ARf[32 * s:32 * s + 1,
                         c_srw // 2 + 2048:c_srw // 2 + 2048 + 16]

    pf = [nc.alloc_psum_tensor(f"pf{i}", [128, 512], F32).ap() for i in range(6)]
    pb = [nc.alloc_psum_tensor(f"pb{i}", [128, 1024], BF16).ap() for i in range(2)]
    pbf = [p.bitcast(F32) for p in pb]

    acc = f32(c_acc, 128)
    S = Sched()

    def dma(q, dst, src, reads=(), writes=(), deps=()):
        return S.op(q, lambda e, d=dst, sr=src: e.dma_start(out=d, in_=sr),
                    reads=reads, writes=writes, deps=deps, dma=True)

    dma("sync", f32(c_U, 128), Umat, writes=["cU"])
    dma("sync", f32(c_mask, 128), maskA, writes=["cMk"])
    dma("sync", f32(c_cw, 136), convw_i, writes=["ccw"])
    dma("sync", f32(c_mup, 67), mup_i, writes=["cmup"])
    dma("sync", f32(c_lnw, 32), lnw_i, writes=["clnw"])
    dma("sync", f32(c_nw, 32), nw_i, writes=["cnw"])
    dma("sync", f32(c_Dx, 32), Dexp_i, writes=["cD"])
    dma("sync", sm_dtb, dtb_i, writes=["cdtb"])
    dma("sync", sm_alog, alog_i, writes=["cal"])
    dma("sync", f32(c_if, 128), identf, writes=["cIf"])
    dma("gpsimd", bf(c_ib, 128), identf, writes=["cIb"])
    eps_ap = smc(14)
    m1_ap = smc(15)
    nc.const_aps.register(F32, EPS, eps_ap)
    nc.const_aps.register(F32, -1.0, m1_ap)
    S.op("vector", lambda e: e.memset(eps_ap, EPS), writes=["constz"])
    S.op("vector", lambda e: e.memset(m1_ap, -1.0), writes=["constz"])
    S.op("vector", lambda e: e.memset(ones1, 1.0), writes=["o1"])
    S.op("vector", lambda e: e.memset(f32(c_onesA, 128), 1.0), writes=["o1"])
    S.op("vector", lambda e: e.memset(onesb, 1.0), writes=["ob"])
    S.op("vector", lambda e: e.memset(onesf, 1.0), writes=["of"])
    S.op("vector", lambda e: e.memset(f32(c_hT, 4096), 0.0),
         writes=[f"hT{g}" for g in range(8)])
    S.op("vector", lambda e: e.memset(bf(c_halo, 102), 0.0), writes=["halo"])
    S.op("scalar", lambda e: e.activation(out=sm_Acol, in_=sm_alog, func=AF.Exp),
         reads=["cal", "constz"], writes=["cA"])
    S.op("vector", lambda e: e.tensor_scalar_mul(out=sm_Acol, in0=sm_Acol,
                                                 scalar1=-1.0),
         reads=["cA"], writes=["cA"])

    absin = [f32(c_abs, 512), f32(c_abs + 1024, 512)]
    absout = bf(c_abs + 2048, 512)
    accn = [0]
    bg1, bg2 = [], []

    def emit_abs(mat, ib, j, si):
        sl = accn[0] % 2
        col = accn[0] % 88
        accn[0] += 1
        dma("sync", absin[sl], mat[ib][:, j * 512:(j + 1) * 512],
            writes=[f"ai{sl}"])
        S.op("scalar",
             lambda e, sl=sl, col=col: e.activation(
                 out=absout, in_=absin[sl], func=AF.Abs,
                 accum_out=acc[:, col:col + 1]),
             reads=[f"ai{sl}"], writes=["ao", f"acc{col}"])
        if accn[0] % 88 == 0:
            S.op("vector", lambda e, si=si: e.reduce_sum(
                out=acc[:, 88 + si:89 + si], in_=acc[:, 0:88],
                axis=mybir.AxisListType.X),
                reads=[f"acc{k}" for k in range(88)], writes=[f"sum{si}"])

    _si = [0]
    for mat in (WgT, WuT, WdT):
        for ib in range(ICB):
            for j in range(8):
                bg1.append((emit_abs, mat, ib, j, (_si[0]) // 88))
                _si[0] += 1

    def emit_wdq(ib, j):
        sl = ib % 2
        wqa = f32(c_wdq + sl * 2048, 1024)
        qta = bf(c_wdo, 1024)
        wk = [f"wdqa{sl}"] + (["XsT"] if (ib == 0 and j == 0) else [])
        dma("sync", wqa, WdT[ib][:, j * 1024:(j + 1) * 1024], writes=wk)
        wk2 = ["wdqo"] + (["hbf0", "hbf1"] if (ib == 0 and j == 0) else [])
        S.op("scalar", lambda e, wqa=wqa, qta=qta: e.activation(
            out=qta, in_=wqa, func=AF.Sign, bias=sm_T[2]),
            reads=[f"wdqa{sl}", "Tc2"], writes=wk2)
        S.op("scalar", lambda e, wqa=wqa: e.activation(
            out=bf(c_qt, 1024), in_=wqa, func=AF.Sign, bias=sm_negT[2]),
            reads=[f"wdqa{sl}", "nTc2"],
            writes=["qt"] + (["gbuf"] if (ib == 0 and j == 0) else []))
        S.op("vector", lambda e, qta=qta: e.tensor_add(
            out=qta, in0=qta, in1=bf(c_qt, 1024)),
            reads=["wdqo", "qt"], writes=["wdqo"])
        dma("sync", qdscr[ib][:, j * 1024:(j + 1) * 1024], qta,
            reads=["wdqo"], writes=[f"qd{ib}_{j}"])

    def pump(n=1):
        for _ in range(n):
            if bg1:
                fn, *a = bg1.pop(0)
                fn(*a)
            elif bg2:
                fn, *a = bg2.pop(0)
                fn(*a)

    def stats_half(xsrc, hh, with_rms, statkey):
        for fc in range(KCB):
            sl = fc % 2
            xt = f32(c_xt + sl * 1024, 512)
            dma("sync", xt, xsrc[fc * 128:(fc + 1) * 128, hh * HT:(hh + 1) * HT],
                writes=[f"xt{sl}"])
            S.op("tensor", lambda e, xt=xt, fc=fc: e.matmul(
                out=pf[4][0:1, 0:512], lhsT=onesf, rhs=xt,
                start=(fc == 0), stop=(fc == KCB - 1)),
                reads=[f"xt{sl}", "of"], writes=["pf4"])
            sq = bf(c_sq + (fc % 2) * 512, 512)
            S.op("scalar", lambda e, xt=xt, sq=sq: e.activation(
                out=sq, in_=xt, func=AF.Square),
                reads=[f"xt{sl}"], writes=[f"sq{fc % 2}"])
            S.op("tensor", lambda e, sq=sq, fc=fc: e.matmul(
                out=pf[5][0:1, 0:512], lhsT=onesb, rhs=sq,
                start=(fc == 0), stop=(fc == KCB - 1)),
                reads=[f"sq{fc % 2}", "ob"], writes=["pf5"])
        S.op("vector", lambda e: e.tensor_copy(out=pool("sa", 512),
                                               in_=pf[4][0:1, 0:512]),
             reads=["pf4"], writes=["p_sa"])
        S.op("vector", lambda e: e.tensor_copy(out=pool("sb", 512),
                                               in_=pf[5][0:1, 0:512]),
             reads=["pf5"], writes=["p_sb"])
        S.op("vector", lambda e: e.tensor_scalar_mul(out=pool("t1", 512),
                                                     in0=pool("sa", 512),
                                                     scalar1=1.0 / H),
             reads=["p_sa"], writes=["p_t1"])
        S.op("vector", lambda e: e.tensor_scalar_mul(out=pool("sb", 512),
                                                     in0=pool("sb", 512),
                                                     scalar1=1.0 / H),
             reads=["p_sb"], writes=["p_sb"])
        S.op("vector", lambda e: e.tensor_mul(out=pool("t2", 512),
                                              in0=pool("t1", 512),
                                              in1=pool("t1", 512)),
             reads=["p_t1"], writes=["p_t2"])
        S.op("vector", lambda e: e.tensor_sub(out=pool("sb", 512),
                                              in0=pool("sb", 512),
                                              in1=pool("t2", 512)),
             reads=["p_sb", "p_t2"], writes=["p_sb"])
        S.op("scalar", lambda e: e.activation(out=pool("t2", 512),
                                              in_=pool("sb", 512),
                                              func=AF.Sqrt, bias=EPS),
             reads=["p_sb"], writes=["p_t2"])
        S.op("vector", lambda e: e.reciprocal(out=pool("t2", 512),
                                              in_=pool("t2", 512)),
             reads=["p_t2"], writes=["p_t2"])
        if with_rms:
            S.op("vector", lambda e: e.tensor_mul(out=pool("sa", 512),
                                                  in0=pool("t2", 512),
                                                  in1=pool("t2", 512)),
                 reads=["p_t2"], writes=["p_sa"])
            S.op("vector", lambda e: e.tensor_mul(out=pool("sa", 512),
                                                  in0=pool("sa", 512),
                                                  in1=pool("sb", 512)),
                 reads=["p_sa", "p_sb"], writes=["p_sa"])
            S.op("scalar", lambda e: e.activation(out=pool("sa", 512),
                                                  in_=pool("sa", 512),
                                                  func=AF.Sqrt, bias=EPS),
                 reads=["p_sa"], writes=["p_sa"])
            S.op("vector", lambda e: e.reciprocal(out=pool("sa", 512),
                                                  in_=pool("sa", 512)),
                 reads=["p_sa"], writes=["p_sa"])
            S.op("vector", lambda e: e.tensor_mul(out=pool("al", 512),
                                                  in0=pool("t2", 512),
                                                  in1=pool("sa", 512)),
                 reads=["p_t2", "p_sa"], writes=["p_al"])
        else:
            S.op("vector", lambda e: e.tensor_copy(out=pool("al", 512),
                                                   in_=pool("t2", 512)),
                 reads=["p_t2"], writes=["p_al"])
        S.op("vector", lambda e: e.tensor_mul(out=pool("be", 512),
                                              in0=pool("t1", 512),
                                              in1=pool("al", 512)),
             reads=["p_t1", "p_al"], writes=["p_be"])
        if statkey == "mx":
            S.op("vector", lambda e, hh=hh: e.tensor_copy(
                out=pool("mx", 512, hh * 512), in_=pool("t1", 512)),
                reads=["p_t1"], writes=[f"p_mx{hh}"])

    def bcast2(rowA, rowB, readsA, readsB, baseA=0, baseB=0):
        S.op("tensor", lambda e: e.matmul(out=pf[2][:, 0:512], lhsT=oA(baseA),
                                          rhs=rowA, start=True, stop=True),
             reads=list(readsA) + ["o1"], writes=["pf2"])
        S.op("tensor", lambda e: e.matmul(out=pf[3][:, 0:512], lhsT=oA(baseB),
                                          rhs=rowB, start=True, stop=True),
             reads=list(readsB) + ["o1"], writes=["pf3"])

    def xr_half(xsrc, hh):
        bcast2(pool("al", 512), pool("be", 512), ["p_al"], ["p_be"])
        for fc in range(KCB):
            sl = fc % 2
            xt = f32(c_xt + sl * 1024, 512)
            dma("sync", xt, xsrc[fc * 128:(fc + 1) * 128, hh * HT:(hh + 1) * HT],
                writes=[f"xt{sl}"])
            cvt = f32(c_cv + (fc % 2) * 1024, 512)
            S.op("vector", lambda e, xt=xt, cvt=cvt: e.tensor_tensor(
                out=cvt, in0=xt, in1=pf[2][:, 0:512], op=ALU.mult),
                reads=[f"xt{sl}", "pf2"], writes=[f"cv{fc % 2}"])
            S.op("vector", lambda e, cvt=cvt: e.tensor_tensor(
                out=cvt, in0=cvt, in1=pf[3][:, 0:512], op=ALU.subtract),
                reads=[f"cv{fc % 2}", "pf3"], writes=[f"cv{fc % 2}"])
            wx = ["xr"] + (["gn"] if fc == 0 else [])
            S.op("vector", lambda e, cvt=cvt, fc=fc: e.tensor_scalar_mul(
                out=bf(c_xr + fc * 512, 512), in0=cvt,
                scalar1=f32(c_lnw, 32)[:, fc:fc + 1]),
                reads=[f"cv{fc % 2}", "clnw"], writes=wx)
            pump(1)

    def sweep_inproj(blocks, hh):
        for bi, oc in enumerate(blocks):
            wsl = bi % 2
            wb = bf(c_Wb + wsl * 4096, 4096)
            dma("gpsimd", wb, Wip[oc], writes=[f"wb{wsl}"])
            bank = bi % 2
            for kc in range(KCB):
                S.op("tensor", lambda e, wb=wb, kc=kc, bank=bank: e.matmul(
                    out=pf[bank][:, 0:512], lhsT=wb[:, kc * 128:(kc + 1) * 128],
                    rhs=bf(c_xr + kc * 512, 512), start=(kc == 0),
                    stop=(kc == KCB - 1)),
                    reads=[f"wb{wsl}", "xr"], writes=[f"pf{bank}"])
            mupc = f32(c_mup, 67)[:, oc:oc + 1]
            if oc < 32:
                zev = bf(c_zs + (oc % 2) * 512, 512)
                S.op("scalar", lambda e, bank=bank, mupc=mupc, zev=zev: e.activation(
                    out=zev, in_=pf[bank][:, 0:512],
                    func=AF.Silu, scale=mupc),
                    reads=[f"pf{bank}", "cmup"], writes=[f"zev{oc % 2}", "zsl0"])
                for c2 in range(NCH):
                    dma("sync", zscr[hh * NCH + c2, :, oc, :],
                        zev[:, c2 * 128:(c2 + 1) * 128],
                        reads=[f"zev{oc % 2}"], writes=[f"zd{hh}_{c2}"])
            elif oc < 66:
                cc = oc - 32
                S.op("scalar", lambda e, cc=cc, bank=bank, mupc=mupc: e.activation(
                    out=bf(c_xbcs + cc * 515 + 3, 512), in_=pf[bank][:, 0:512],
                    func=AF.Copy, scale=mupc),
                    reads=[f"pf{bank}", "cmup"],
                    writes=[f"xbc{cc}", f"sil{cc}"] + (["attn"] if cc == 0 else []))
            else:
                S.op("scalar", lambda e, bank=bank: e.activation(
                    out=dtraw_h, in_=pf[bank][0:64, 0:512], func=AF.Copy,
                    scale=ARf[0:64, c_mup // 2 + oc:c_mup // 2 + oc + 1]),
                    reads=[f"pf{bank}", "cmup"], writes=["dtraw"])
            pump(1)

    def conv_half(hh):
        xb3 = AR[:, c_xbcs:c_xbcs + 17510].rearrange("p (c t) -> p c t", t=515)
        hl3 = bf(c_halo, 102).rearrange("p (c t) -> p c t", t=3)
        S.op("vector", lambda e: e.tensor_copy(out=xb3[:, :, 0:3], in_=hl3),
             reads=["halo"] + [f"xbc{cc}" for cc in range(34)],
             writes=[f"xbch{cc}" for cc in range(34)])
        S.op("vector", lambda e: e.tensor_copy(out=hl3, in_=xb3[:, :, 512:515]),
             reads=[f"xbc{cc}" for cc in range(34)], writes=["halo"])
        for cc in range(34):
            sl = cc % 2
            cvt = f32(c_cv + sl * 1024, 512)

            def stripe(k, cc=cc):
                return bf(c_xbcs + cc * 515 + k, 512)

            def wcol(k, cc=cc):
                return f32(c_cw, 136)[:, cc * 4 + k:cc * 4 + k + 1]

            S.op("vector", lambda e, cvt=cvt, cc=cc: e.tensor_scalar_mul(
                out=cvt, in0=stripe(0, cc), scalar1=wcol(0, cc)),
                reads=[f"xbc{cc}", f"xbch{cc}", "ccw"], writes=[f"cv{sl}"])
            for k in range(1, 4):
                S.op("vector", lambda e, cvt=cvt, cc=cc, k=k: e.scalar_tensor_tensor(
                    out=cvt, in0=stripe(k, cc), scalar=wcol(k, cc), in1=cvt,
                    op0=ALU.mult, op1=ALU.add),
                    reads=[f"xbc{cc}", f"xbch{cc}", "ccw", f"cv{sl}"],
                    writes=[f"cv{sl}"])
            S.op("scalar", lambda e, cvt=cvt, cc=cc: e.activation(
                out=bf(c_xbcs + cc * 515 + 3, 512), in_=cvt, func=AF.Silu),
                reads=[f"cv{sl}"], writes=[f"sil{cc}", f"xbc{cc}"])
            pump(1)

    def dt_half(hh):
        ta = f32(c_cv, 512, 0, 64)
        tb = f32(c_cv + 1024, 512, 0, 64)
        tc = f32(c_yb, 512, 0, 64)
        S.op("vector", lambda e: e.tensor_scalar_add(out=ta, in0=dtraw_h,
                                                     scalar1=sm_dtb),
             reads=["dtraw", "cdtb"], writes=["cv0"])
        S.op("vector", lambda e: e.tensor_scalar_max(out=tb, in0=ta, scalar1=0.0),
             reads=["cv0"], writes=["cv1"])
        S.op("scalar", lambda e: e.activation(out=tc, in_=ta, func=AF.Abs),
             reads=["cv0"], writes=["yb0"])
        S.op("scalar", lambda e: e.activation(out=tc, in_=tc, func=AF.Exp,
                                              scale=-1.0),
             reads=["yb0"], writes=["yb0"])
        S.op("scalar", lambda e: e.activation(out=tc, in_=tc, func=AF.Ln, bias=1.0),
             reads=["yb0"], writes=["yb0"])
        S.op("vector", lambda e: e.tensor_add(out=dts_h, in0=tb, in1=tc),
             reads=["cv1", "yb0"], writes=["dts"])

    def chunk2(hh, c2, is_R):
        t0 = c2 * QQ
        dts = dts_h[:, t0:t0 + QQ]
        csl = c2 % 2
        ta = f32(c_cv, 512, 0, 64)[:, 0:128]
        tb = f32(c_cv + 1024, 512, 0, 64)[:, 0:128]
        tc = f32(c_yb, 512, 0, 64)[:, 0:128]

        S.op("vector", lambda e: e.tensor_scalar_mul(out=dtAf, in0=dts,
                                                     scalar1=sm_Acol),
             reads=["dts", "cA"], writes=["dtA"])
        S.op("tensor", lambda e: e.transpose(out=pf[5][:, 0:64], in_=dtAf,
                                             identity=f32(c_if, 128)[0:64, 0:64]),
             reads=["dtA", "cIf", "pf5"], writes=["pf5"])
        S.op("vector", lambda e, csl=csl: e.tensor_copy(out=dtATM(csl),
                                                        in_=pf[5][:, 0:64]),
             reads=["pf5"], writes=["dtATM"])
        S.op("tensor", lambda e, csl=csl: e.matmul(
            out=pf[5][0:64, 64:192], lhsT=dtATM(csl), rhs=f32(c_U, 128),
            start=True, stop=True),
            reads=["dtATM", "cU", "pf5"], writes=["pf5"])
        S.op("vector", lambda e, csl=csl: e.tensor_copy(out=sfm(csl),
                                                        in_=pf[5][0:64, 64:192]),
             reads=["pf5"], writes=["sfm"])
        dma("sync", srowscr, sfm(csl), reads=["sfm"], writes=["srowd"])
        S.op("vector", lambda e, csl=csl: e.tensor_scalar(
            out=ta, in0=sfm(csl), scalar1=sfm(csl)[:, 127:128], scalar2=None,
            op0=ALU.subtract),
            reads=["sfm"], writes=["cv0"])
        S.op("scalar", lambda e: e.activation(out=tb, in_=ta, func=AF.Exp,
                                              scale=-1.0),
             reads=["cv0"], writes=["cv1"])
        S.op("vector", lambda e, csl=csl: e.tensor_mul(out=wfm(csl), in0=tb,
                                                       in1=dts),
             reads=["cv1", "dts"], writes=["wfm"])
        S.op("tensor", lambda e, csl=csl: e.transpose(
            out=pf[5][:, 192:256], in_=wfm(csl),
            identity=f32(c_if, 128)[0:64, 0:64]),
            reads=["wfm", "cIf", "pf5"], writes=["pf5"])
        S.op("vector", lambda e, csl=csl: e.tensor_copy(out=wT(csl),
                                                        in_=pf[5][:, 192:256]),
             reads=["pf5"], writes=["wTk"])
        if not is_R:
            S.op("scalar", lambda e, dts=dts: e.activation(out=lndf, in_=dts,
                                                           func=AF.Ln),
                 reads=["dts"], writes=["lndtk"])
            S.op("vector", lambda e, csl=csl: e.tensor_sub(out=tc, in0=lndf,
                                                           in1=sfm(csl)),
                 reads=["lndtk", "sfm"], writes=["yb0"])
            S.op("tensor", lambda e: e.transpose(
                out=pf[5][:, 256:320], in_=tc,
                identity=f32(c_if, 128)[0:64, 0:64]),
                reads=["yb0", "cIf", "pf5"], writes=["pf5"])
            S.op("vector", lambda e, csl=csl: e.tensor_copy(out=vT(csl),
                                                            in_=pf[5][:, 256:320]),
                 reads=["pf5"], writes=["vTk"])
        if not is_R:
            zsl_ = bf(c_zs + csl * 4096, 4096)
            dma("sync", zsl_, zscr[hh * NCH + c2],
                reads=[f"zd{hh}_{c2}"], writes=[f"zsl{csl}"])
        Bfm = bf(c_xbcs + 32 * 515 + 3 + t0, 128)
        Cfm = bf(c_xbcs + 33 * 515 + 3 + t0, 128)
        S.op("tensor", lambda e, Bfm=Bfm: e.transpose(
            out=pb[0][:, 0:128], in_=Bfm, identity=bf(c_ib, 128)),
            reads=["sil32", "cIb", "pb0"], writes=["pb0"])
        S.op("vector", lambda e, csl=csl: e.tensor_copy(out=B_TM(csl),
                                                        in_=pb[0][:, 0:128]),
             reads=["pb0"], writes=["BTM"])
        if not is_R:
            S.op("tensor", lambda e, Bfm=Bfm, Cfm=Cfm: e.matmul(
                out=pf[0][:, 0:128], lhsT=Bfm, rhs=Cfm, start=True, stop=True),
                reads=["sil32", "sil33", "pf0"], writes=["pf0"])
            S.op("vector", lambda e, csl=csl: e.tensor_copy(
                out=bf(c_Gt, 128), in_=pf[0][:, 0:128]),
                reads=["pf0"], writes=["GtB"])
        for fc in range(KCB):
            pbank = (fc // 8) % 2
            S.op("tensor", lambda e, fc=fc, pbank=pbank, t0=t0: e.transpose(
                out=pb[pbank][:, (fc % 8) * 128:(fc % 8 + 1) * 128],
                in_=bf(c_xbcs + fc * 515 + 3 + t0, 128), identity=bf(c_ib, 128)),
                reads=[f"sil{fc}", "cIb", f"pb{pbank}"], writes=[f"pb{pbank}"])
            if fc % 8 == 7:
                S.op("vector", lambda e, fc=fc, pbank=pbank: e.tensor_copy(
                    out=bf(c_Xs + (fc - 7) * 128, 1024),
                    in_=pb[pbank][:, 0:1024]),
                    reads=[f"pb{pbank}"], writes=["XsT"])

        for g8 in range(8):
            ssl = g8 % 2
            sr = srow(ssl)
            cr = crow(ssl)
            dma("sync", sr, srowscr[g8 * 8:(g8 + 1) * 8, :],
                reads=["srowd"], writes=[f"srow{ssl}"])
            S.op("scalar", lambda e, sr=sr, cr=cr: e.activation(
                out=cr[0:1, 0:8], in_=sr[0:1, 127::128], func=AF.Exp),
                reads=[f"srow{ssl}"], writes=[f"crow{ssl}"])
            S.op("tensor", lambda e, cr=cr, ssl=ssl: e.matmul(
                out=pf[5][:, 0:512], lhsT=oA(32 * ssl),
                rhs=cr[0:1, 0:8].rearrange("a (b c) -> a b c", c=1)
                .broadcast_to([1, 8, 64]),
                start=True, stop=True),
                reads=[f"crow{ssl}", "o1", "pf5"], writes=["pf5"])
            S.op("vector", lambda e, g8=g8: e.tensor_tensor(
                out=f32(c_hdec + (g8 % 2) * 1024, 512),
                in0=f32(c_hT, 4096)[:, g8 * 512:(g8 + 1) * 512],
                in1=pf[5][:, 0:512], op=ALU.mult),
                reads=[f"hT{g8}", "pf5"], writes=[f"hdec{g8 % 2}"])
            ybank = g8 % 2
            if not is_R:
                S.op("scalar", lambda e, sr=sr, ssl=ssl: e.activation(
                    out=erow(ssl), in_=sr, func=AF.Exp),
                    reads=[f"srow{ssl}"], writes=[f"erow{ssl}"])
                S.op("vector", lambda e, g8=g8, ssl=ssl: e.tensor_copy(
                    out=bf(c_hbf + ssl * 1024, 512),
                    in_=f32(c_hT, 4096)[:, g8 * 512:(g8 + 1) * 512]),
                    reads=[f"hT{g8}"], writes=[f"hbf{ssl}"])
                for j4 in range(2):
                    S.op("tensor", lambda e, sr=sr, j4=j4, ssl=ssl: e.matmul(
                        out=pf[3][:, 0:512], lhsT=oA(32 * ssl),
                        rhs=sr[0:1, j4 * 512:(j4 + 1) * 512],
                        start=True, stop=True),
                        reads=[f"srow{ssl}", "o1"], writes=["pf3"])
                    S.op("tensor", lambda e, j4=j4, ssl=ssl: e.matmul(
                        out=pf[4][:, 0:512], lhsT=oA(32 * ssl),
                        rhs=erow(ssl)[0:1, j4 * 512:(j4 + 1) * 512],
                        start=True, stop=True),
                        reads=[f"erow{ssl}", "o1"], writes=["pf4"])
                    for jj in range(4):
                        j8 = j4 * 4 + jj
                        r = g8 * 8 + j8
                        q = j8 // 2
                        pcol = (j8 % 2) * 64
                        msl = j8 % 2
                        scol = jj * 128
                        S.op("vector", lambda e, r=r, csl=csl, msl=msl,
                             scol=scol: e.scalar_tensor_tensor(
                            out=f32(c_Dm + msl * 256, 128),
                            in0=pf[3][:, scol:scol + 128],
                            scalar=vT(csl)[:, r:r + 1], in1=f32(c_mask, 128),
                            op0=ALU.add, op1=ALU.add),
                            reads=["pf3", "vTk", "cMk"], writes=[f"Dm{msl}"])
                        S.op("scalar", lambda e, msl=msl: e.activation(
                            out=bf(c_E + msl * 128, 128),
                            in_=f32(c_Dm + msl * 256, 128), func=AF.Exp),
                            reads=[f"Dm{msl}"], writes=[f"E{msl}"])
                        S.op("vector", lambda e, msl=msl: e.tensor_mul(
                            out=bf(c_Mt + msl * 128, 128),
                            in0=bf(c_E + msl * 128, 128),
                            in1=bf(c_Gt, 128)),
                            reads=[f"E{msl}", "GtB"], writes=[f"Mt{msl}"])
                        S.op("vector", lambda e, msl=msl, scol=scol,
                             Cfm=Cfm: e.tensor_tensor(
                            out=bf(c_Cx + msl * 128, 128), in0=Cfm,
                            in1=pf[4][:, scol:scol + 128], op=ALU.mult),
                            reads=["sil33", "pf4"], writes=[f"Cx{msl}"])
                        S.op("tensor", lambda e, r=r, msl=msl, q=q, pcol=pcol,
                             ybank=ybank: e.matmul(
                            out=pf[ybank][pcol:pcol + 64, q * 128:(q + 1) * 128],
                            lhsT=bf(c_Xs + r * 64, 64),
                            rhs=bf(c_Mt + msl * 128, 128),
                            start=True, stop=False, tile_position=(0, pcol)),
                            reads=["XsT", f"Mt{msl}"], writes=[f"pf{ybank}"])
                        S.op("tensor", lambda e, j8=j8, msl=msl, q=q, pcol=pcol,
                             ssl=ssl, ybank=ybank: e.matmul(
                            out=pf[ybank][pcol:pcol + 64, q * 128:(q + 1) * 128],
                            lhsT=bf(c_hbf + ssl * 1024, 512)[:, j8 * 64:(j8 + 1) * 64],
                            rhs=bf(c_Cx + msl * 128, 128),
                            start=False, stop=(j8 == 7), tile_position=(0, pcol)),
                            reads=[f"hbf{ssl}", f"Cx{msl}"], writes=[f"pf{ybank}"])
            S.op("tensor", lambda e, g8=g8: e.matmul(
                out=pf[2][:, 0:512], lhsT=f32(c_if, 128),
                rhs=f32(c_hdec + (g8 % 2) * 1024, 512),
                start=True, stop=False),
                reads=[f"hdec{g8 % 2}", "cIf", "pf2kk"], writes=["pf2"])
            for j8 in range(8):
                r = g8 * 8 + j8
                msl = j8 % 2
                S.op("vector", lambda e, r=r, msl=msl, csl=csl: e.tensor_scalar_mul(
                    out=bf(c_Bw + msl * 128, 128), in0=B_TM(csl),
                    scalar1=wT(csl)[:, r:r + 1]),
                    reads=["BTM", "wTk"], writes=[f"Bw{msl}"])
                S.op("tensor", lambda e, r=r, msl=msl, j8=j8, g8=g8: e.matmul(
                    out=pf[2][:, j8 * 64:(j8 + 1) * 64],
                    lhsT=bf(c_Bw + msl * 128, 128),
                    rhs=bf(c_Xs + r * 64, 64),
                    start=False, stop=(j8 == 7)),
                    reads=[f"Bw{msl}", "XsT"], writes=["pf2"])
            S.op("vector", lambda e, g8=g8: e.tensor_copy(
                out=f32(c_hT, 4096)[:, g8 * 512:(g8 + 1) * 512],
                in_=pf[2][:, 0:512]),
                reads=["pf2"], writes=[f"hT{g8}", "pf2kk"])
            if not is_R:
                for q in range(4):
                    fc = g8 * 4 + q
                    S.op("vector", lambda e, fc=fc, q=q, t0=t0, g8=g8,
                         ybank=ybank: e.scalar_tensor_tensor(
                        out=f32(c_yb + (g8 % 2) * 1024, 512)[:, q * 128:(q + 1) * 128],
                        in0=bf(c_xbcs + fc * 515 + 3 + t0, 128),
                        scalar=f32(c_Dx, 32)[:, fc:fc + 1],
                        in1=pf[ybank][:, q * 128:(q + 1) * 128],
                        op0=ALU.mult, op1=ALU.add),
                        reads=[f"sil{fc}", "cD", f"pf{ybank}"],
                        writes=[f"yb{g8 % 2}"])
                S.op("vector", lambda e, g8=g8, csl=csl: e.tensor_tensor(
                    out=bf(c_g + g8 * 512, 512),
                    in0=f32(c_yb + (g8 % 2) * 1024, 512),
                    in1=bf(c_zs + csl * 4096 + g8 * 512, 512), op=ALU.mult),
                    reads=[f"yb{g8 % 2}", f"zsl{csl}"], writes=["gbuf"])
            pump(1)
        if is_R:
            return
        for i8 in range(8):
            sq = bf(c_sq + (i8 % 2) * 512, 512)
            S.op("scalar", lambda e, i8=i8, sq=sq: e.activation(
                out=sq, in_=bf(c_g + i8 * 512, 512), func=AF.Square),
                reads=["gbuf"], writes=[f"sq{i8 % 2}"])
            S.op("tensor", lambda e, sq=sq, i8=i8: e.matmul(
                out=pf[0][0:1, 0:512], lhsT=onesb, rhs=sq,
                start=(i8 == 0), stop=(i8 == 7)),
                reads=[f"sq{i8 % 2}", "ob"], writes=["pf0"])
        r2 = pool("be", 128, 512)
        S.op("vector", lambda e: e.tensor_copy(out=pool("t1", 512),
                                               in_=pf[0][0:1, 0:512]),
             reads=["pf0"], writes=["p_t1c"])
        S.op("vector", lambda e: e.tensor_add(out=pool("t2", 128),
                                              in0=pool("t1", 128),
                                              in1=pool("t1", 128, 128)),
             reads=["p_t1c"], writes=["p_t2c"])
        S.op("vector", lambda e: e.tensor_add(out=pool("t2", 128, 128),
                                              in0=pool("t1", 128, 256),
                                              in1=pool("t1", 128, 384)),
             reads=["p_t1c"], writes=["p_t2c"])
        S.op("vector", lambda e, r2=r2: e.tensor_add(out=r2,
                                                     in0=pool("t2", 128),
                                                     in1=pool("t2", 128, 128)),
             reads=["p_t2c"], writes=["p_r2"])
        S.op("vector", lambda e, r2=r2: e.tensor_scalar_mul(out=r2, in0=r2,
                                                            scalar1=1.0 / H),
             reads=["p_r2"], writes=["p_r2"])
        S.op("scalar", lambda e, r2=r2: e.activation(out=r2, in_=r2,
                                                     func=AF.Sqrt, bias=EPS),
             reads=["p_r2"], writes=["p_r2"])
        S.op("vector", lambda e, r2=r2: e.reciprocal(out=r2, in_=r2),
             reads=["p_r2"], writes=["p_r2"])
        S.op("tensor", lambda e, r2=r2: e.matmul(out=pf[0][:, 0:128],
                                                 lhsT=ones1, rhs=r2,
                                                 start=True, stop=True),
             reads=["p_r2", "o1"], writes=["pf0"])
        for fc in range(KCB):
            S.op("vector", lambda e, fc=fc: e.tensor_tensor(
                out=f32(c_cv + (fc % 2) * 1024, 512)[:, 0:128],
                in0=bf(c_g + fc * 128, 128), in1=pf[0][:, 0:128], op=ALU.mult),
                reads=["gbuf", "pf0"], writes=[f"cv{fc % 2}"])
            wg_ = ["gn"] + (["xr"] if (fc == 0 and c2 == 0) else [])
            S.op("vector", lambda e, fc=fc, t0=t0: e.tensor_scalar_mul(
                out=bf(c_gn + fc * 512 + t0, 128),
                in0=f32(c_cv + (fc % 2) * 1024, 512)[:, 0:128],
                scalar1=f32(c_nw, 32)[:, fc:fc + 1]),
                reads=[f"cv{fc % 2}", "cnw"], writes=wg_)

    def postln_scalars(s1_src, s2_src, s1_reads, s2_reads):
        S.op("vector", lambda e: e.tensor_copy(out=pool("sa", 512), in_=s1_src),
             reads=s1_reads, writes=["p_sa"])
        S.op("vector", lambda e: e.tensor_copy(out=pool("sb", 512), in_=s2_src),
             reads=s2_reads, writes=["p_sb"])
        S.op("vector", lambda e: e.tensor_scalar_mul(out=pool("t1", 512),
                                                     in0=pool("sa", 512),
                                                     scalar1=1.0 / H),
             reads=["p_sa"], writes=["p_t1"])
        S.op("vector", lambda e: e.tensor_scalar_mul(out=pool("sb", 512),
                                                     in0=pool("sb", 512),
                                                     scalar1=1.0 / H),
             reads=["p_sb"], writes=["p_sb"])
        S.op("vector", lambda e: e.tensor_mul(out=pool("t2", 512),
                                              in0=pool("t1", 512),
                                              in1=pool("t1", 512)),
             reads=["p_t1"], writes=["p_t2"])
        S.op("vector", lambda e: e.tensor_sub(out=pool("sb", 512),
                                              in0=pool("sb", 512),
                                              in1=pool("t2", 512)),
             reads=["p_sb", "p_t2"], writes=["p_sb"])
        S.op("scalar", lambda e: e.activation(out=pool("t2", 512),
                                              in_=pool("sb", 512),
                                              func=AF.Sqrt, bias=EPS),
             reads=["p_sb"], writes=["p_t2"])
        S.op("vector", lambda e: e.reciprocal(out=pool("t2", 512),
                                              in_=pool("t2", 512)),
             reads=["p_t2"], writes=["p_t2"])
        S.op("vector", lambda e: e.tensor_mul(out=pool("t1", 512),
                                              in0=pool("t1", 512),
                                              in1=pool("t2", 512)),
             reads=["p_t1", "p_t2"], writes=["p_t1"])

    def outproj_half(hh):
        for oc in range(FFB):
            wsl = oc % 2
            wb = bf(c_Wb + wsl * 4096, 4096)
            dma("gpsimd", wb, Wop[oc], writes=[f"wb{wsl}"])
            bank = oc % 2
            for kc in range(KCB):
                S.op("tensor", lambda e, wb=wb, kc=kc, bank=bank: e.matmul(
                    out=pf[bank][:, 0:512], lhsT=wb[:, kc * 128:(kc + 1) * 128],
                    rhs=bf(c_gn + kc * 512, 512), start=(kc == 0),
                    stop=(kc == KCB - 1)),
                    reads=[f"wb{wsl}", "gn"], writes=[f"pf{bank}"])
            wka = ["attn"]
            if oc == 0:
                wka += [f"sil{cc}" for cc in range(34)] +                        [f"xbc{cc}" for cc in range(34)] +                        [f"xbch{cc}" for cc in range(34)]
            S.op("scalar", lambda e, oc=oc, bank=bank: e.activation(
                out=bf(c_attn + oc * 512, 512), in_=pf[bank][:, 0:512],
                func=AF.Copy),
                reads=[f"pf{bank}"], writes=wka)
            S.op("tensor", lambda e, oc=oc: e.matmul(
                out=pf[4][0:1, 0:512], lhsT=onesb,
                rhs=bf(c_attn + oc * 512, 512),
                start=(oc == 0), stop=(oc == FFB - 1)),
                reads=["attn", "ob"], writes=["pf4"])
            sq = bf(c_sq + (oc % 2) * 512, 512)
            S.op("scalar", lambda e, oc=oc, sq=sq: e.activation(
                out=sq, in_=bf(c_attn + oc * 512, 512), func=AF.Square),
                reads=["attn"], writes=[f"sq{oc % 2}"])
            S.op("tensor", lambda e, sq=sq, oc=oc: e.matmul(
                out=pf[5][0:1, 0:512], lhsT=onesb, rhs=sq,
                start=(oc == 0), stop=(oc == FFB - 1)),
                reads=[f"sq{oc % 2}", "ob"], writes=["pf5"])
            pump(1)
        postln_scalars(pf[4][0:1, 0:512], pf[5][0:1, 0:512], ["pf4"], ["pf5"])
        bcast2(pool("t2", 512), pool("t1", 512), ["p_t2"], ["p_t1"])
        for fc in range(KCB):
            sl = fc % 2
            xt = f32(c_xt + sl * 1024, 512)
            dma("sync", xt, xT[fc * 128:(fc + 1) * 128, hh * HT:(hh + 1) * HT],
                writes=[f"xt{sl}"])
            cvt = f32(c_cv + (fc % 2) * 1024, 512)
            S.op("vector", lambda e, fc=fc, cvt=cvt: e.tensor_tensor(
                out=cvt, in0=bf(c_attn + fc * 512, 512), in1=pf[2][:, 0:512],
                op=ALU.mult),
                reads=["attn", "pf2"], writes=[f"cv{fc % 2}"])
            S.op("vector", lambda e, cvt=cvt: e.tensor_tensor(
                out=cvt, in0=cvt, in1=pf[3][:, 0:512], op=ALU.subtract),
                reads=[f"cv{fc % 2}", "pf3"], writes=[f"cv{fc % 2}"])
            S.op("vector", lambda e, cvt=cvt, xt=xt: e.tensor_add(
                out=cvt, in0=cvt, in1=xt),
                reads=[f"cv{fc % 2}", f"xt{sl}"], writes=[f"cv{fc % 2}"])
            dma("sync", x2scr[fc, :, hh * HT:(hh + 1) * HT], cvt,
                reads=[f"cv{fc % 2}"], writes=[f"x2_{fc}_{hh}"])
            sq = bf(c_sq + (fc % 2) * 512, 512)
            S.op("scalar", lambda e, cvt=cvt, sq=sq: e.activation(
                out=sq, in_=cvt, func=AF.Square),
                reads=[f"cv{fc % 2}"], writes=[f"sq{fc % 2}"])
            S.op("tensor", lambda e, sq=sq, fc=fc: e.matmul(
                out=pf[4][0:1, 0:512], lhsT=onesb, rhs=sq,
                start=(fc == 0), stop=(fc == KCB - 1)),
                reads=[f"sq{fc % 2}", "ob"], writes=["pf4"])
            pump(1)
        r3 = pool("r3", 512, hh * 512)
        mr3 = pool("mr3", 512, hh * 512)
        S.op("vector", lambda e: e.tensor_copy(out=pool("sa", 512),
                                               in_=pf[4][0:1, 0:512]),
             reads=["pf4"], writes=["p_sa"])
        S.op("vector", lambda e: e.tensor_scalar_mul(out=pool("t1", 512),
                                                     in0=pool("sa", 512),
                                                     scalar1=1.0 / H),
             reads=["p_sa"], writes=["p_t1"])
        S.op("vector", lambda e, hh=hh: e.tensor_mul(
            out=pool("t2", 512), in0=pool("mx", 512, hh * 512),
            in1=pool("mx", 512, hh * 512)),
            reads=[f"p_mx{hh}"], writes=["p_t2"])
        S.op("vector", lambda e: e.tensor_sub(out=pool("t1", 512),
                                              in0=pool("t1", 512),
                                              in1=pool("t2", 512)),
             reads=["p_t1", "p_t2"], writes=["p_t1"])
        S.op("scalar", lambda e, r3=r3: e.activation(out=r3, in_=pool("t1", 512),
                                                     func=AF.Sqrt, bias=EPS),
             reads=["p_t1"], writes=["p_r3"])
        S.op("vector", lambda e, r3=r3: e.reciprocal(out=r3, in_=r3),
             reads=["p_r3"], writes=["p_r3"])
        S.op("vector", lambda e, r3=r3, mr3=mr3, hh=hh: e.tensor_mul(
            out=mr3, in0=pool("mx", 512, hh * 512), in1=r3),
            reads=["p_r3", f"p_mx{hh}"], writes=["p_mr3"])

    def emit_scales():
        for mi in range(3):
            red = ARf[:, c_acc // 2 + 116 + mi:c_acc // 2 + 117 + mi]
            S.op("vector", lambda e, red=red, mi=mi: e.reduce_sum(
                out=red, in_=acc[:, 88 + 8 * mi:88 + 8 * mi + 8],
                axis=mybir.AxisListType.X),
                reads=[f"sum{k}" for k in range(8 * mi, 8 * mi + 8)],
                writes=[f"red{mi}"])
            S.op("tensor", lambda e, red=red, mi=mi: e.matmul(
                out=pf[4][0:1, mi:mi + 1], lhsT=red, rhs=onesf,
                start=(mi == 0), stop=(mi == 2)),
                reads=[f"red{mi}", "of"], writes=["pf4"])
        S.op("vector", lambda e: e.tensor_copy(out=pool("t1", 3),
                                               in_=pf[4][0:1, 0:3]),
             reads=["pf4"], writes=["p_sc"])
        S.op("vector", lambda e: e.tensor_scalar_mul(
            out=pool("t1", 3, 4), in0=pool("t1", 3), scalar1=1.0 / (NI * H)),
            reads=["p_sc"], writes=["p_sc"])
        S.op("vector", lambda e: e.tensor_scalar(
            out=pool("t1", 3, 8), in0=pool("t1", 3, 4), scalar1=EPS, scalar2=0.5,
            op0=ALU.add, op1=ALU.mult),
            reads=["p_sc"], writes=["p_sc"])
        S.op("vector", lambda e: e.tensor_mul(
            out=pool("t1", 1, 12), in0=pool("t1", 1, 4), in1=pool("t1", 1, 5)),
            reads=["p_sc"], writes=["p_sc"])
        S.op("vector", lambda e: e.tensor_mul(
            out=pool("t1", 1, 12), in0=pool("t1", 1, 12), in1=pool("t1", 1, 6)),
            reads=["p_sc"], writes=["p_sc"])
        S.op("vector", lambda e: e.tensor_scalar_mul(
            out=pool("t1", 1, 12), in0=pool("t1", 1, 12), scalar1=1.0 / 8.0),
            reads=["p_sc"], writes=["p_sc"])
        for mi in range(3):
            S.op("tensor", lambda e, mi=mi: e.matmul(
                out=pf[4][:, 8 + mi:9 + mi], lhsT=ones1,
                rhs=pool("t1", 1, 8 + mi), start=True, stop=True),
                reads=["p_sc", "o1"], writes=["pf4"])
        S.op("tensor", lambda e: e.matmul(
            out=pf[4][:, 12:13], lhsT=ones1, rhs=pool("t1", 1, 12),
            start=True, stop=True),
            reads=["p_sc", "o1"], writes=["pf4"])
        for mi in range(3):
            S.op("vector", lambda e, mi=mi: e.tensor_copy(
                out=sm_T[mi], in_=pf[4][:, 8 + mi:9 + mi]),
                reads=["pf4"], writes=[f"Tc{mi}", "Tcs"])
            S.op("vector", lambda e, mi=mi: e.tensor_scalar_mul(
                out=sm_negT[mi], in0=sm_T[mi], scalar1=-1.0),
                reads=[f"Tc{mi}"], writes=[f"nTc{mi}", "Tcs"])
        S.op("vector", lambda e: e.tensor_copy(out=sm_eta, in_=pf[4][:, 12:13]),
             reads=["pf4"], writes=["eta"])
        if dbg:
            dma("sync", DB["d_scal"], pool("t1", 16), reads=["p_sc"],
                writes=["dscal"])

    for hh in range(2):
        stats_half(xPT, hh, True, "")
        xr_half(xPT, hh)
        sweep_inproj(list(range(32, OCB)), hh)
        conv_half(hh)
        dt_half(hh)
        for c2 in range(NCH):
            chunk2(hh, c2, is_R=True)
    if dbg:
        dma("sync", DB["d_h"], f32(c_hT, 4096),
            reads=[f"hT{g}" for g in range(8)], writes=["dh"])

    for hh in range(2):
        stats_half(xT, hh, True, "mx")
        xr_half(xT, hh)
        sweep_inproj(list(range(32, OCB)), hh)
        conv_half(hh)
        dt_half(hh)
        sweep_inproj(list(range(0, 32)), hh)
        if dbg and hh == 0:
            dma("gpsimd", DB["d_sil"], bf(c_xbcs, 17536),
                reads=[f"sil{cc}" for cc in range(34)], writes=["dsil"])
            dma("gpsimd", DB["d_xr"], bf(c_xr, 16384), reads=["xr"], writes=["dxr"])
        for c2 in range(NCH):
            chunk2(hh, c2, is_R=False)
            if dbg and hh == 0 and c2 == 0:
                dma("gpsimd", DB["d_g0"], bf(c_g, 4096), reads=["gbuf"],
                    writes=["dg0"])
            if dbg:
                dma("gpsimd", DB["d_gall"][:, (hh * NCH + c2) * 4096:
                                           (hh * NCH + c2 + 1) * 4096],
                    bf(c_g, 4096), reads=["gbuf"], writes=["dgall"])
        if dbg:
            dma("sync", DB["d_hM"][:, hh * 4096:(hh + 1) * 4096],
                f32(c_hT, 4096), reads=[f"hT{g}" for g in range(8)],
                writes=["dhM"])
        outproj_half(hh)
        if dbg and hh == 0:
            dma("gpsimd", DB["d_attn"], bf(c_attn, 16384), reads=["attn"],
                writes=["datt"])
            dma("gpsimd", DB["d_gn"], bf(c_gn, 16384), reads=["gn"], writes=["dgn"])
    if dbg:
        dma("sync", DB["d_dts"][:, 512:1024], dts_h, reads=["dts"],
            writes=["ddts"])

    while bg1:
        pump(4)
    emit_scales()
    S.op("vector", lambda e: e.engine_nop(),
         writes=["qt", "gbuf", "wdqa0", "wdqa1", "XsT", "wdqo", "hbf0", "hbf1",
                 "wqg0", "wqg1", "wqu0", "wqu1", "zev0", "zev1", "zsl0", "zsl1",
                 "q_wqg", "q_wqu", "wb0", "wb1", "xr", "attn", "gn", "halo", "ln3",
                 "cv0", "cv1", "it0", "it1", "yb0", "yb1"] +
                [f"sil{cc}" for cc in range(34)] +
                [f"xbc{cc}" for cc in range(34)] +
                [f"xbch{cc}" for cc in range(34)])
    for ib in range(ICB):
        for j in range(4):
            bg2.append((emit_wdq, ib, j))

    for hh in range(2):
        S.op("tensor", lambda e, hh=hh: e.matmul(
            out=pf[2][:, 0:512], lhsT=ones1,
            rhs=pool("r3", 512, hh * 512), start=True, stop=True),
            reads=["p_r3", "o1"], writes=["pf2"])
        S.op("tensor", lambda e, hh=hh: e.matmul(
            out=pf[3][:, 0:512], lhsT=ones1,
            rhs=pool("mr3", 512, hh * 512), start=True, stop=True),
            reads=["p_mr3", "o1"], writes=["pf3"])
        for fc in range(KCB):
            sl = fc % 2
            xt = f32(c_xt + sl * 1024, 512)
            dma("sync", xt, x2scr[fc, :, hh * HT:(hh + 1) * HT],
                reads=[f"x2_{fc}_{hh}"], writes=[f"xt{sl}"])
            cvt = f32(c_cv + (fc % 2) * 1024, 512)
            S.op("vector", lambda e, xt=xt, cvt=cvt: e.tensor_tensor(
                out=cvt, in0=xt, in1=pf[2][:, 0:512], op=ALU.mult),
                reads=[f"xt{sl}", "pf2"], writes=[f"cv{fc % 2}"])
            wkeys = ["ln3"]
            S.op("vector", lambda e, cvt=cvt, fc=fc, hh=hh: e.tensor_tensor(
                out=bf(c_ln3 + fc * 1024 + hh * 512, 512), in0=cvt,
                in1=pf[3][:, 0:512], op=ALU.subtract),
                reads=[f"cv{fc % 2}", "pf3"], writes=wkeys)
            pump(1)

    def quant_into(dst, WT, ib, Tb, nTb, rk):
        base = c_wq if rk == "wqg" else c_wq + 4096
        for j in range(4):
            wp = f32(base + (j % 2) * 2048, 1024)
            dma("sync", wp, WT[ib][:, j * 1024:(j + 1) * 1024],
                writes=[f"{rk}{j % 2}"])
            qtp = bf(c_yb, 1024)
            S.op("scalar", lambda e, wp=wp, j=j, dst=dst, Tb=Tb: e.activation(
                out=dst[:, j * 1024:(j + 1) * 1024], in_=wp, func=AF.Sign,
                bias=Tb),
                reads=[f"{rk}{j % 2}", "Tcs"], writes=["q_" + rk])
            S.op("scalar", lambda e, wp=wp, qtp=qtp, nTb=nTb: e.activation(
                out=qtp, in_=wp, func=AF.Sign, bias=nTb),
                reads=[f"{rk}{j % 2}", "Tcs"], writes=["qt"])
            S.op("vector", lambda e, j=j, dst=dst, qtp=qtp: e.tensor_add(
                out=dst[:, j * 1024:(j + 1) * 1024],
                in0=dst[:, j * 1024:(j + 1) * 1024], in1=qtp),
                reads=["q_" + rk, "qt"], writes=["q_" + rk])

    qg = bf(c_qq, 4096)
    qu = bf(c_qq + 4096, 4096)
    for ib in range(ICB):
        quant_into(qg, WgT, ib, sm_T[0], sm_negT[0], "wqg")
        for tt in range(2):
            for kc in range(KCB):
                S.op("tensor", lambda e, kc=kc, tt=tt: e.matmul(
                    out=pf[tt][:, 0:512], lhsT=qg[:, kc * 128:(kc + 1) * 128],
                    rhs=bf(c_ln3 + kc * 1024 + tt * 512, 512),
                    start=(kc == 0), stop=(kc == KCB - 1)),
                    reads=["q_wqg", "ln3"], writes=[f"pf{tt}"])
        quant_into(qu, WuT, ib, sm_T[1], sm_negT[1], "wqu")
        for tt in range(2):
            for kc in range(KCB):
                S.op("tensor", lambda e, kc=kc, tt=tt: e.matmul(
                    out=pf[2 + tt][:, 0:512], lhsT=qu[:, kc * 128:(kc + 1) * 128],
                    rhs=bf(c_ln3 + kc * 1024 + tt * 512, 512),
                    start=(kc == 0), stop=(kc == KCB - 1)),
                    reads=["q_wqu", "ln3"], writes=[f"pf{2 + tt}"])
            gt = f32(c_cv + tt * 1024, 512)
            S.op("scalar", lambda e, gt=gt, tt=tt: e.activation(
                out=gt, in_=pf[tt][:, 0:512], func=AF.Copy),
                reads=[f"pf{tt}"], writes=[f"cv{tt}"])
            it = bf(c_hdec + tt * 1024, 512)
            S.op("vector", lambda e, gt=gt, it=it, tt=tt: e.tensor_tensor(
                out=it, in0=gt, in1=pf[2 + tt][:, 0:512], op=ALU.mult),
                reads=[f"cv{tt}", f"pf{2 + tt}"], writes=[f"it{tt}"])
            dma("sync", intscr[ib, :, tt * HT:(tt + 1) * HT], it,
                reads=[f"it{tt}"], writes=[f"int{ib}_{tt}"])
        pump(2)
    while bg2:
        pump(4)

    S.op("vector", lambda e: e.engine_nop(),
         writes=["wqg0", "wqg1", "wqu0", "wqu1", "q_wqg", "q_wqu", "qt", "ln3",
                 "XsT", "gbuf", "wb0", "wb1", "wdqa0", "wdqa1", "wdqo",
                 "hbf0", "hbf1", "zev0", "zev1", "zsl0", "zsl1", "xr", "attn",
                 "gn", "cv0", "cv1", "it0", "it1", "yb0", "yb1", "frl0", "frl1",
                 "qdl0", "qdl1", "qdl2", "qdl3", "srow0", "srow1",
                 "erow0", "erow1", "crow0", "crow1"])
    OGS = [list(range(6 * i, min(6 * i + 6, FFB))) for i in range(6)]
    for tt in range(2):
        for ib in range(ICB):
            dma("sync" if ib % 2 == 0 else "scalar", bf(c_int + ib * 512, 512),
                intscr[ib, :, tt * HT:(tt + 1) * HT],
                reads=[f"int{ib}_{tt}", "ln3"], writes=[f"ih{ib}"])
        for ogi, og in enumerate(OGS):
            for ib in range(ICB):
                qsl = ib % 4
                qd = bf(c_qd + qsl * 1024, len(og) * 128)
                dma("sync" if ib % 2 == 0 else "scalar",
                    qd, qdscr[ib][:, og[0] * 128:(og[0] + len(og)) * 128],
                    reads=[f"qd{ib}_{og[0] // 8}", f"qd{ib}_{(og[-1]) // 8}"],
                    writes=[f"qdl{qsl}"])
                for j, oc in enumerate(og):
                    S.op("tensor", lambda e, qd=qd, j=j, ib=ib: e.matmul(
                        out=pf[j][:, 0:512], lhsT=qd[:, j * 128:(j + 1) * 128],
                        rhs=bf(c_int + ib * 512, 512),
                        start=(ib == 0), stop=(ib == ICB - 1)),
                        reads=[f"qdl{qsl}", f"ih{ib}"], writes=[f"pf{j}"])
            for j, oc in enumerate(og):
                fop = bf(c_yb + (oc % 2) * 1024, 512)
                S.op("scalar", lambda e, j=j, fop=fop: e.activation(
                    out=fop, in_=pf[j][:, 0:512], func=AF.Copy, scale=sm_eta),
                    reads=[f"pf{j}", "eta"], writes=[f"yb{oc % 2}"])
                dma("sync", foscr[oc, :, tt * HT:(tt + 1) * HT], fop,
                    reads=[f"yb{oc % 2}"], writes=[f"fo{oc}_{tt}"])
                S.op("tensor", lambda e, oc=oc, fop=fop: e.matmul(
                    out=pbf[0][0:1, 0:512], lhsT=onesb, rhs=fop,
                    start=(oc == 0), stop=(oc == FFB - 1)),
                    reads=[f"yb{oc % 2}", "ob"], writes=["pb0"])
                sq = bf(c_sq + (oc % 2) * 512, 512)
                S.op("scalar", lambda e, fop=fop, sq=sq: e.activation(
                    out=sq, in_=fop, func=AF.Square),
                    reads=[f"yb{oc % 2}"], writes=[f"sq{oc % 2}"])
                S.op("tensor", lambda e, sq=sq, oc=oc: e.matmul(
                    out=pbf[1][0:1, 0:512], lhsT=onesb, rhs=sq,
                    start=(oc == 0), stop=(oc == FFB - 1)),
                    reads=[f"sq{oc % 2}", "ob"], writes=["pb1"])
        postln_scalars(pbf[0][0:1, 0:512], pbf[1][0:1, 0:512], ["pb0"], ["pb1"])
        bcast2(pool("t2", 512), pool("t1", 512), ["p_t2"], ["p_t1"])
        for fc in range(KCB):
            sl = fc % 2
            xt = f32(c_xt + sl * 1024, 512)
            dma("sync", xt, x2scr[fc, :, tt * HT:(tt + 1) * HT],
                reads=[f"x2_{fc}_{tt}"], writes=[f"xt{sl}"])
            frl = bf(c_hdec + (fc % 2) * 1024, 512)
            dma("sync", frl, foscr[fc, :, tt * HT:(tt + 1) * HT],
                reads=[f"fo{fc}_{tt}"], writes=[f"frl{fc % 2}"])
            cvt = f32(c_cv + (fc % 2) * 1024, 512)
            S.op("vector", lambda e, fc=fc, cvt=cvt, frl=frl: e.tensor_tensor(
                out=cvt, in0=frl, in1=pf[2][:, 0:512], op=ALU.mult),
                reads=[f"frl{fc % 2}", "pf2"], writes=[f"cv{fc % 2}"])
            S.op("vector", lambda e, cvt=cvt: e.tensor_tensor(
                out=cvt, in0=cvt, in1=pf[3][:, 0:512], op=ALU.subtract),
                reads=[f"cv{fc % 2}", "pf3"], writes=[f"cv{fc % 2}"])
            S.op("vector", lambda e, cvt=cvt, xt=xt: e.tensor_add(
                out=cvt, in0=cvt, in1=xt),
                reads=[f"cv{fc % 2}", f"xt{sl}"], writes=[f"cv{fc % 2}"])
            dma("sync", out[fc * 128:(fc + 1) * 128, tt * HT:(tt + 1) * HT], cvt,
                reads=[f"cv{fc % 2}"], writes=[f"out{fc}_{tt}"])

    S.op("vector", lambda e: e.engine_nop(),
         reads=[f"out{fc}_{tt}" for fc in range(KCB) for tt in range(2)] +
               (["dscal", "dh", "dg0", "datt", "dgn", "ddts", "dsil",
                 "dxr"] if dbg else []),
         writes=["done"])

    sems = {}
    dsems = {}
    for e in Sched.ENGS:
        sems[e] = []
        dsems[e] = []
        for k in range(4):
            sems[e].append(nc.semaphore(f"sem_{e}_{k}").__enter__())
        if e in ("sync", "gpsimd", "scalar", "vector"):
            for k in range(Sched.NLANE):
                dsems[e].append(nc.semaphore(f"dsem_{e}_{k}").__enter__())
    blk_cm = nc.Block()
    blk = blk_cm.__enter__()
    S.finalize(blk, sems, dsems)
    blk_cm.__exit__(None, None, None)
    return nc


_NC_CACHE = {}


def _prep(inp):
    f = np.float32
    x = np.ascontiguousarray(inp["L0_attn_hidden_states"]).astype(f, copy=False)
    _ = inp["L0_attn_residual"]
    ipw = np.asarray(inp["L0_attn_in_proj_w"], f)
    opw = np.asarray(inp["L0_attn_out_proj_w"], f)
    conv_w = np.asarray(inp["L0_attn_conv_w"], f)
    A_log = np.asarray(inp["L0_attn_A_log"], f)
    dt_bias = np.asarray(inp["L0_attn_dt_bias"], f)
    D = np.asarray(inp["L0_attn_D"], f)
    ln_w = np.asarray(inp["L0_attn_ln_w"], f)
    norm_w = np.asarray(inp["L0_attn_norm_w"], f)
    mup = np.asarray(inp["L0_attn_mup_vector"], f)
    wg = np.asarray(inp["L0_ffn_w_gate"], f)
    wu = np.asarray(inp["L0_ffn_w_up"], f)
    wd = np.asarray(inp["L0_ffn_w_down"], f)

    ipw_p = np.zeros((H, OCB * 128), f)
    ipw_p[:, :PROJ_REAL] = ipw
    Wip = np.ascontiguousarray(
        ipw_p.reshape(KCB, 128, OCB, 128).transpose(2, 1, 0, 3)
    ).reshape(OCB, 128, KCB * 128)
    Wop = np.ascontiguousarray(
        opw.reshape(KCB, 128, FFB, 128).transpose(2, 1, 0, 3)
    ).reshape(FFB, 128, KCB * 128)

    def padT(w):
        wt = np.zeros((H, ICB * 128), f)
        wt[:, :NI] = w.T
        return np.ascontiguousarray(
            wt.reshape(KCB, 128, ICB, 128).transpose(2, 1, 0, 3)
        ).reshape(ICB, 128, KCB * 128)

    WgTt = padT(wg)
    WuTt = padT(wu)
    wdt = np.zeros((ICB * 128, H), f)
    wdt[:NI, :] = wd.T
    WdTt = np.ascontiguousarray(wdt.reshape(ICB, 128, FFB * 128))

    identf = np.eye(128, dtype=f)
    Umat = np.triu(np.ones((128, 128), f))
    maskA = np.where(Umat > 0, np.float32(0.0), np.float32(-1e30)).astype(f)
    lnw_t = np.ascontiguousarray(ln_w.reshape(32, 128).T)
    nw_t = np.ascontiguousarray(norm_w.reshape(32, 128).T)
    mup_p = np.zeros((OCB * 128,), f)
    mup_p[:PROJ_REAL] = mup
    mup_t = np.ascontiguousarray(mup_p.reshape(OCB, 128).T)
    Dexp = np.ascontiguousarray(D.reshape(32, 2)[:, np.repeat([0, 1], 64)].T)
    convw_t = np.ascontiguousarray(
        conv_w.reshape(34, 128, 4).transpose(1, 0, 2).reshape(128, 136))
    dtb_c = np.ascontiguousarray(dt_bias.reshape(64, 1))
    alog_c = np.ascontiguousarray(A_log.reshape(64, 1))

    shared = dict(Wip=Wip, Wop=Wop, WgT=WgTt, WuT=WuTt, WdT=WdTt,
                  identf=identf, Umat=Umat, maskA=maskA, lnw_i=lnw_t,
                  nw_i=nw_t, mup_i=mup_t, dtb_i=dtb_c, alog_i=alog_c,
                  Dexp_i=Dexp, convw_i=convw_t)
    in_maps = []
    zeros_half = np.zeros((H, TPC), f)
    for c in range(8):
        b, hh = c // 2, c % 2
        m = dict(shared)
        m["xT"] = np.ascontiguousarray(x[b, hh * TPC:(hh + 1) * TPC, :].T)
        m["xPT"] = (zeros_half if hh == 0
                    else np.ascontiguousarray(x[b, 0:TPC, :].T))
        in_maps.append(m)
    return in_maps


PROJ_REAL = 8512


def kernel(dbg=False, **inp):
    in_maps = _prep(inp)
    key = ("dbg" if dbg else "plain")
    if key not in _NC_CACHE:
        _NC_CACHE[key] = build_nc(dbg=dbg)
    nc = _NC_CACHE[key]
    res = run_bass_kernel_spmd(nc, in_maps, core_ids=list(range(8)))
    outf = np.empty((4, 2048, H), np.float32)
    for c in range(8):
        b, hh = c // 2, c % 2
        outf[b, hh * TPC:(hh + 1) * TPC, :] = \
            np.asarray(res.results[c]["out"], np.float32).T
    if dbg:
        return outf, res
    return outf


# version 15 (final)
# speedup vs baseline: 1.1789x; 1.0071x over previous
import numpy as np
import ml_dtypes
import concourse.bass as bass
import concourse.mybir as mybir
from concourse.bass_utils import run_bass_kernel_spmd

F32 = mybir.dt.float32
F32R = mybir.dt.float32r
BF16 = mybir.dt.bfloat16
AF = mybir.ActivationFunctionType
ALU = mybir.AluOpType

H = 4096
OCB = 67
KCB = 32
ICB = 88
FFB = 32
NI = 11008
EPS = 1e-5
TPC = 1024
HT = 512
NCH = 4
QQ = 128
SEM_CAP = 40000


SAME_ENG_THRESH = 16


class Sched:
    ENGS = ("sync", "gpsimd", "vector", "scalar", "tensor")

    def __init__(self):
        self.ops = {e: [] for e in Sched.ENGS}
        self.last_w = {}
        self.readers = {}

    def op(self, eng, fn, reads=(), writes=(), deps=(), dma=False):
        waits = {}

        def add_dep(h):
            if h is None:
                return
            e, i = h
            kk = (e, self.ops[e][i]["dma"])
            waits[kk] = max(waits.get(kk, -1), i)

        for k in reads:
            if k in self.last_w:
                add_dep(self.last_w[k])
        for k in writes:
            if k in self.last_w:
                add_dep(self.last_w[k])
            for r in self.readers.get(k, ()):
                add_dep(r)
        for d in deps:
            add_dep(d)
        n = len(self.ops[eng])
        kc = (eng, False)
        if kc in waits:
            ti = waits[kc]
            if (not dma) and (eng == "tensor" or n - ti > SAME_ENG_THRESH):
                waits.pop(kc)
        idx = n
        self.ops[eng].append(dict(fn=fn, waits=waits, dma=dma))
        h = (eng, idx)
        for k in reads:
            self.readers.setdefault(k, []).append(h)
        for k in writes:
            self.last_w[k] = h
            self.readers[k] = []
        return h

    NLANE = 8

    def finalize(self, blk, sems, dsems):
        targets = {e: set() for e in Sched.ENGS}
        for e in Sched.ENGS:
            for i, o in enumerate(self.ops[e]):
                for (te, _td), ti in o["waits"].items():
                    targets[te].add(ti)
        seg_of, val_of, nseg = {}, {}, {}
        for e in Sched.ENGS:
            seg, val, ndma = 0, 0, 0
            for i, o in enumerate(self.ops[e]):
                if o["dma"]:
                    lane = ndma % Sched.NLANE
                    o["inc"] = ("dma", lane, 16 * (ndma // Sched.NLANE + 1))
                    seg_of[(e, i)] = lane
                    val_of[(e, i)] = 16 * (ndma // Sched.NLANE + 1)
                    ndma += 1
                    continue
                if i in targets[e]:
                    inc = 1
                    if val + inc > SEM_CAP:
                        seg += 1
                        val = 0
                    val += inc
                    o["inc"] = ("eng", seg, inc)
                else:
                    o["inc"] = None
                seg_of[(e, i)] = seg
                val_of[(e, i)] = val
            nseg[e] = seg + 1
        for e in Sched.ENGS:
            assert len(sems[e]) >= nseg[e], (e, nseg[e])

        def make_body(e):
            def body(eng):
                guar = {}
                for i, o in enumerate(self.ops[e]):
                    for (te, _td), ti in sorted(o["waits"].items()):
                        tdma = self.ops[te][ti]["dma"]
                        tseg, tval = seg_of[(te, ti)], val_of[(te, ti)]
                        assert tval > 0, (e, i, te, ti)
                        key = (te, "d" if tdma else "e", tseg)
                        if guar.get(key, -1) >= tval:
                            continue
                        sem = dsems[te][tseg] if tdma else sems[te][tseg]
                        eng.wait_ge(sem, tval)
                        guar[key] = tval
                        if not tdma:
                            for sgg in range(tseg):
                                guar[(te, "e", sgg)] = 1 << 30
                    inst = o["fn"](eng)
                    if o["inc"] is not None:
                        kind, sg, inc_or_val = o["inc"]
                        assert inst is not None, f"{e}:{i} inc w/o inst"
                        if kind == "dma":
                            inst.then_inc(dsems[e][sg], 16)
                        else:
                            inst.then_inc(sems[e][sg], inc_or_val)
            return body

        blk.sync(make_body("sync"))
        blk.gpsimd(make_body("gpsimd"))
        blk.vector(make_body("vector"))
        blk.scalar(make_body("scalar"))
        blk.tensor(make_body("tensor"))


def build_nc(dbg=False):
    nc = bass.Bass("TRN2", target_bir_lowering=False)

    def din(name, shape, dt=F32):
        return nc.dram_tensor(name, shape, dt, kind="ExternalInput").ap()

    xT = din("xT", [H, TPC])
    xPT = din("xPT", [H, TPC])
    Wip = din("Wip", [OCB, 128, KCB * 128])
    Wop = din("Wop", [FFB, 128, KCB * 128])
    WgT = din("WgT", [ICB, 128, KCB * 128])
    WuT = din("WuT", [ICB, 128, KCB * 128])
    WdT = din("WdT", [ICB, 128, FFB * 128])
    identf = din("identf", [128, 128])
    Umat = din("Umat", [128, 128])
    maskA = din("maskA", [128, 128])
    lnw_i = din("lnw_i", [128, 32])
    nw_i = din("nw_i", [128, 32])
    mup_i = din("mup_i", [128, OCB])
    dtb_i = din("dtb_i", [64, 1])
    alog_i = din("alog_i", [64, 1])
    Dexp_i = din("Dexp_i", [128, 32])
    convw_i = din("convw_i", [128, 136])

    out = nc.dram_tensor("out", [H, TPC], F32, kind="ExternalOutput").ap()
    DB = {}
    if dbg:
        for nm, shape in [("d_dts", [64, TPC]), ("d_h", [128, 4096]),
                          ("d_g0", [128, 4096]), ("d_attn", [128, 16384]),
                          ("d_scal", [1, 16]), ("d_gn", [128, 16384]),
                          ("d_xr", [128, 16384]),
                          ("d_sil", [128, 17536]), ("d_gall", [128, 8 * 4096]),
                          ("d_hM", [128, 2 * 4096])]:
            DB[nm] = nc.dram_tensor(nm, shape, F32, kind="ExternalOutput").ap()

    x2scr = nc.dram_tensor("x2scr", [KCB, 128, TPC], F32, kind="Internal").ap()
    intscr = nc.dram_tensor("intscr", [ICB, 128, TPC], BF16, kind="Internal").ap()
    qdscr = nc.dram_tensor("qdscr", [ICB, 128, FFB * 128], BF16, kind="Internal").ap()
    srowscr = nc.dram_tensor("srowscr", [64, 128], F32, kind="Internal").ap()
    zscr = nc.dram_tensor("zscr", [8, 128, 32, 128], BF16, kind="Internal").ap()
    foscr = nc.dram_tensor("foscr", [FFB, 128, TPC], BF16, kind="Internal").ap()

    ones1 = nc.alloc_sbuf_tensor("ones1", [1, 128], F32).ap()
    onesb = nc.alloc_sbuf_tensor("onesb", [128, 1], BF16).ap()
    onesf = nc.alloc_sbuf_tensor("onesf", [128, 1], F32).ap()
    AR = nc.alloc_sbuf_tensor("AR", [128, 106112], BF16).ap()
    ARf = AR.bitcast(F32)
    ARr = AR.bitcast(F32R)
    ptr = [0]

    def rgn(n):
        n = (n + 31) & ~31
        c0 = ptr[0]
        ptr[0] += n
        assert ptr[0] <= 106112, f"arena overflow {ptr[0]}"
        return c0

    def bf(c0, n, r0=0, nr=128):
        return AR[r0:r0 + nr, c0:c0 + n]

    def f32(c0, n, r0=0, nr=128):
        return ARf[r0:r0 + nr, c0 // 2:c0 // 2 + n]

    def f32r(c0, n, r0=0, nr=128):
        return ARr[r0:r0 + nr, c0 // 2:c0 // 2 + n]

    c_hT = rgn(8192)
    c_pool = rgn(12544)
    c_ib = rgn(128)
    c_if = rgn(256)
    c_U = rgn(256)
    c_mask = rgn(256)
    c_cw = rgn(272)
    c_mup = rgn(160)
    c_lnw = rgn(64)
    c_nw = rgn(64)
    c_Dx = rgn(64)
    c_sm = rgn(64)
    c_acc = rgn(256)
    c_halo = rgn(112)
    c_xr = rgn(16384)
    c_xbcs = rgn(17536)
    c_zs = rgn(8192)
    c_Xs = rgn(4096)
    c_g = rgn(4096)
    c_Wb = rgn(8192)
    c_dt = rgn(2048)
    c_ch = rgn(2432)
    c_Dm = rgn(512)
    c_E = rgn(256)
    c_Mt = rgn(256)
    c_Cx = rgn(256)
    c_Bw = rgn(256)
    c_Gt = rgn(128)
    c_hdec = rgn(2048)
    c_yb = rgn(2048)
    c_cv = rgn(2048)
    c_sq = rgn(1024)
    c_xt = rgn(2048)
    c_hbf = rgn(2048)
    c_srw = rgn(4160)
    c_onesA = rgn(256)
    c_abs = rgn(2048)
    assert ptr[0] <= 106112, ptr[0]

    c_gn = c_xr
    c_attn = c_xbcs
    c_ln3 = c_xr
    assert c_xbcs == c_xr + 16384 and c_zs == c_xbcs + 17536
    c_int = c_xr
    c_wq = c_zs
    c_qq = c_Wb
    c_qt = c_g
    c_wdq = c_Xs
    c_wdo = c_hbf
    assert c_g == c_Xs + 4096 and c_Wb == c_g + 4096
    c_qd = c_hbf

    def smc(i, r0=0, nr=128):
        return ARf[r0:r0 + nr, c_sm // 2 + i:c_sm // 2 + i + 1]

    sm_dtb = smc(0, 0, 64)
    sm_alog = smc(1, 0, 64)
    sm_Acol = smc(2, 0, 64)
    sm_T = [smc(3), smc(4), smc(5)]
    sm_negT = [smc(11), smc(12), smc(13)]
    sm_eta = smc(6)

    PL = {"sa": (0, 0), "sb": (0, 512), "t1": (0, 1024), "t2": (0, 1536),
          "al": (0, 2048), "be": (0, 2560),
          "mx": (0, 3200), "r3": (0, 4224), "mr3": (0, 5248)}

    def pool(nm, n=1024, off=0):
        r, cb = PL[nm]
        base = c_pool // 2 + cb + off
        return ARf[r:r + 1, base:base + n]

    def oA(r0):
        return ARf[r0:r0 + 1, c_onesA // 2:c_onesA // 2 + 128]

    def pbase(nm):
        return PL[nm][0]

    def chs(off, n, r0=0, nr=128):
        return ARf[r0:r0 + nr, c_ch // 2 + off:c_ch // 2 + off + n]

    sfm = lambda s: chs(s * 128, 128, 0, 64)
    wfm = lambda s: chs(256 + s * 128, 128, 0, 64)
    dtAf = chs(512, 128, 0, 64)
    lndf = chs(640, 128, 0, 64)
    dtATM = lambda s: chs(768, 64)
    vT = lambda s: chs(832 + s * 64, 64)
    wT = lambda s: chs(960 + s * 64, 64)
    B_TM = lambda s: bf(c_ch + 2176 + s * 128, 128)
    dts_h = f32(c_dt, 512, 0, 64)
    dtraw_h = f32(c_dt + 1024, 512, 0, 64)

    srow = lambda s: ARf[32 * s:32 * s + 1, c_srw // 2:c_srw // 2 + 1024]
    erow = lambda s: ARf[32 * s:32 * s + 1,
                         c_srw // 2 + 1024:c_srw // 2 + 2048]
    crow = lambda s: ARf[32 * s:32 * s + 1,
                         c_srw // 2 + 2048:c_srw // 2 + 2048 + 16]

    pf = [nc.alloc_psum_tensor(f"pf{i}", [128, 512], F32).ap() for i in range(6)]
    pb = [nc.alloc_psum_tensor(f"pb{i}", [128, 1024], BF16).ap() for i in range(2)]
    pbf = [p.bitcast(F32) for p in pb]

    acc = f32(c_acc, 128)
    S = Sched()

    def dma(q, dst, src, reads=(), writes=(), deps=()):
        return S.op(q, lambda e, d=dst, sr=src: e.dma_start(out=d, in_=sr),
                    reads=reads, writes=writes, deps=deps, dma=True)

    dma("sync", f32(c_U, 128), Umat, writes=["cU"])
    dma("sync", f32(c_mask, 128), maskA, writes=["cMk"])
    dma("sync", f32(c_cw, 136), convw_i, writes=["ccw"])
    dma("sync", f32(c_mup, 67), mup_i, writes=["cmup"])
    dma("sync", f32(c_lnw, 32), lnw_i, writes=["clnw"])
    dma("sync", f32(c_nw, 32), nw_i, writes=["cnw"])
    dma("sync", f32(c_Dx, 32), Dexp_i, writes=["cD"])
    dma("sync", sm_dtb, dtb_i, writes=["cdtb"])
    dma("sync", sm_alog, alog_i, writes=["cal"])
    dma("sync", f32(c_if, 128), identf, writes=["cIf"])
    dma("gpsimd", bf(c_ib, 128), identf, writes=["cIb"])
    eps_ap = smc(14)
    m1_ap = smc(15)
    nc.const_aps.register(F32, EPS, eps_ap)
    nc.const_aps.register(F32, -1.0, m1_ap)
    S.op("vector", lambda e: e.memset(eps_ap, EPS), writes=["constz"])
    S.op("vector", lambda e: e.memset(m1_ap, -1.0), writes=["constz"])
    S.op("vector", lambda e: e.memset(ones1, 1.0), writes=["o1"])
    S.op("vector", lambda e: e.memset(f32(c_onesA, 128), 1.0), writes=["o1"])
    S.op("vector", lambda e: e.memset(onesb, 1.0), writes=["ob"])
    S.op("vector", lambda e: e.memset(onesf, 1.0), writes=["of"])
    S.op("vector", lambda e: e.memset(f32(c_hT, 4096), 0.0),
         writes=[f"hT{g}" for g in range(8)])
    S.op("vector", lambda e: e.memset(bf(c_halo, 102), 0.0), writes=["halo"])
    S.op("scalar", lambda e: e.activation(out=sm_Acol, in_=sm_alog, func=AF.Exp),
         reads=["cal", "constz"], writes=["cA"])
    S.op("vector", lambda e: e.tensor_scalar_mul(out=sm_Acol, in0=sm_Acol,
                                                 scalar1=-1.0),
         reads=["cA"], writes=["cA"])

    absin = [f32(c_abs, 512), f32(c_abs + 1024, 512)]
    absout = bf(c_abs + 2048, 512)
    accn = [0]
    bg1, bg2 = [], []

    def emit_abs(mat, ib, j, si):
        sl = accn[0] % 2
        col = accn[0] % 88
        accn[0] += 1
        dma("sync", absin[sl], mat[ib][:, j * 512:(j + 1) * 512],
            writes=[f"ai{sl}"])
        S.op("scalar",
             lambda e, sl=sl, col=col: e.activation(
                 out=absout, in_=absin[sl], func=AF.Abs,
                 accum_out=acc[:, col:col + 1]),
             reads=[f"ai{sl}"], writes=["ao", f"acc{col}"])
        if accn[0] % 88 == 0:
            S.op("vector", lambda e, si=si: e.reduce_sum(
                out=acc[:, 88 + si:89 + si], in_=acc[:, 0:88],
                axis=mybir.AxisListType.X),
                reads=[f"acc{k}" for k in range(88)], writes=[f"sum{si}"])

    _si = [0]
    for mat in (WgT, WuT, WdT):
        for ib in range(ICB):
            for j in range(8):
                bg1.append((emit_abs, mat, ib, j, (_si[0]) // 88))
                _si[0] += 1

    def emit_wdq(ib, j):
        sl = ib % 2
        wqa = f32(c_wdq + sl * 2048, 1024)
        qta = bf(c_wdo, 1024)
        wk = [f"wdqa{sl}"] + (["XsT"] if (ib == 0 and j == 0) else [])
        dma("sync" if ib % 2 == 0 else "scalar",
            wqa, WdT[ib][:, j * 1024:(j + 1) * 1024], writes=wk)
        wk2 = ["wdqo"] + (["hbf0", "hbf1"] if (ib == 0 and j == 0) else [])
        S.op("scalar", lambda e, wqa=wqa, qta=qta: e.activation(
            out=qta, in_=wqa, func=AF.Sign, bias=sm_T[2]),
            reads=[f"wdqa{sl}", "Tc2"], writes=wk2)
        S.op("scalar", lambda e, wqa=wqa: e.activation(
            out=bf(c_qt, 1024), in_=wqa, func=AF.Sign, bias=sm_negT[2]),
            reads=[f"wdqa{sl}", "nTc2"],
            writes=["qt"] + (["gbuf"] if (ib == 0 and j == 0) else []))
        S.op("vector", lambda e, qta=qta: e.tensor_add(
            out=qta, in0=qta, in1=bf(c_qt, 1024)),
            reads=["wdqo", "qt"], writes=["wdqo"])
        dma("sync", qdscr[ib][:, j * 1024:(j + 1) * 1024], qta,
            reads=["wdqo"], writes=[f"qd{ib}_{j}"])

    def pump(n=1):
        for _ in range(n):
            if bg1:
                fn, *a = bg1.pop(0)
                fn(*a)
            elif bg2:
                fn, *a = bg2.pop(0)
                fn(*a)

    def stats_half(xsrc, hh, with_rms, statkey):
        for fc in range(KCB):
            sl = fc % 2
            xt = f32(c_xt + sl * 1024, 512)
            dma("sync", xt, xsrc[fc * 128:(fc + 1) * 128, hh * HT:(hh + 1) * HT],
                writes=[f"xt{sl}"])
            S.op("tensor", lambda e, xt=xt, fc=fc: e.matmul(
                out=pf[4][0:1, 0:512], lhsT=onesf, rhs=xt,
                start=(fc == 0), stop=(fc == KCB - 1)),
                reads=[f"xt{sl}", "of"], writes=["pf4"])
            sq = bf(c_sq + (fc % 2) * 512, 512)
            S.op("scalar", lambda e, xt=xt, sq=sq: e.activation(
                out=sq, in_=xt, func=AF.Square),
                reads=[f"xt{sl}"], writes=[f"sq{fc % 2}"])
            S.op("tensor", lambda e, sq=sq, fc=fc: e.matmul(
                out=pf[5][0:1, 0:512], lhsT=onesb, rhs=sq,
                start=(fc == 0), stop=(fc == KCB - 1)),
                reads=[f"sq{fc % 2}", "ob"], writes=["pf5"])
        S.op("vector", lambda e: e.tensor_copy(out=pool("sa", 512),
                                               in_=pf[4][0:1, 0:512]),
             reads=["pf4"], writes=["p_sa"])
        S.op("vector", lambda e: e.tensor_copy(out=pool("sb", 512),
                                               in_=pf[5][0:1, 0:512]),
             reads=["pf5"], writes=["p_sb"])
        S.op("vector", lambda e: e.tensor_scalar_mul(out=pool("t1", 512),
                                                     in0=pool("sa", 512),
                                                     scalar1=1.0 / H),
             reads=["p_sa"], writes=["p_t1"])
        S.op("vector", lambda e: e.tensor_scalar_mul(out=pool("sb", 512),
                                                     in0=pool("sb", 512),
                                                     scalar1=1.0 / H),
             reads=["p_sb"], writes=["p_sb"])
        S.op("vector", lambda e: e.tensor_mul(out=pool("t2", 512),
                                              in0=pool("t1", 512),
                                              in1=pool("t1", 512)),
             reads=["p_t1"], writes=["p_t2"])
        S.op("vector", lambda e: e.tensor_sub(out=pool("sb", 512),
                                              in0=pool("sb", 512),
                                              in1=pool("t2", 512)),
             reads=["p_sb", "p_t2"], writes=["p_sb"])
        S.op("scalar", lambda e: e.activation(out=pool("t2", 512),
                                              in_=pool("sb", 512),
                                              func=AF.Sqrt, bias=EPS),
             reads=["p_sb"], writes=["p_t2"])
        S.op("vector", lambda e: e.reciprocal(out=pool("t2", 512),
                                              in_=pool("t2", 512)),
             reads=["p_t2"], writes=["p_t2"])
        if with_rms:
            S.op("vector", lambda e: e.tensor_mul(out=pool("sa", 512),
                                                  in0=pool("t2", 512),
                                                  in1=pool("t2", 512)),
                 reads=["p_t2"], writes=["p_sa"])
            S.op("vector", lambda e: e.tensor_mul(out=pool("sa", 512),
                                                  in0=pool("sa", 512),
                                                  in1=pool("sb", 512)),
                 reads=["p_sa", "p_sb"], writes=["p_sa"])
            S.op("scalar", lambda e: e.activation(out=pool("sa", 512),
                                                  in_=pool("sa", 512),
                                                  func=AF.Sqrt, bias=EPS),
                 reads=["p_sa"], writes=["p_sa"])
            S.op("vector", lambda e: e.reciprocal(out=pool("sa", 512),
                                                  in_=pool("sa", 512)),
                 reads=["p_sa"], writes=["p_sa"])
            S.op("vector", lambda e: e.tensor_mul(out=pool("al", 512),
                                                  in0=pool("t2", 512),
                                                  in1=pool("sa", 512)),
                 reads=["p_t2", "p_sa"], writes=["p_al"])
        else:
            S.op("vector", lambda e: e.tensor_copy(out=pool("al", 512),
                                                   in_=pool("t2", 512)),
                 reads=["p_t2"], writes=["p_al"])
        S.op("vector", lambda e: e.tensor_mul(out=pool("be", 512),
                                              in0=pool("t1", 512),
                                              in1=pool("al", 512)),
             reads=["p_t1", "p_al"], writes=["p_be"])
        if statkey == "mx":
            S.op("vector", lambda e, hh=hh: e.tensor_copy(
                out=pool("mx", 512, hh * 512), in_=pool("t1", 512)),
                reads=["p_t1"], writes=[f"p_mx{hh}"])

    def bcast2(rowA, rowB, readsA, readsB, baseA=0, baseB=0):
        S.op("tensor", lambda e: e.matmul(out=pf[2][:, 0:512], lhsT=oA(baseA),
                                          rhs=rowA, start=True, stop=True),
             reads=list(readsA) + ["o1"], writes=["pf2"])
        S.op("tensor", lambda e: e.matmul(out=pf[3][:, 0:512], lhsT=oA(baseB),
                                          rhs=rowB, start=True, stop=True),
             reads=list(readsB) + ["o1"], writes=["pf3"])

    def xr_half(xsrc, hh):
        bcast2(pool("al", 512), pool("be", 512), ["p_al"], ["p_be"])
        for fc in range(KCB):
            sl = fc % 2
            xt = f32(c_xt + sl * 1024, 512)
            dma("sync", xt, xsrc[fc * 128:(fc + 1) * 128, hh * HT:(hh + 1) * HT],
                writes=[f"xt{sl}"])
            cvt = f32(c_cv + (fc % 2) * 1024, 512)
            S.op("vector", lambda e, xt=xt, cvt=cvt: e.tensor_tensor(
                out=cvt, in0=xt, in1=pf[2][:, 0:512], op=ALU.mult),
                reads=[f"xt{sl}", "pf2"], writes=[f"cv{fc % 2}"])
            S.op("vector", lambda e, cvt=cvt: e.tensor_tensor(
                out=cvt, in0=cvt, in1=pf[3][:, 0:512], op=ALU.subtract),
                reads=[f"cv{fc % 2}", "pf3"], writes=[f"cv{fc % 2}"])
            wx = ["xr"] + (["gn"] if fc == 0 else [])
            S.op("vector", lambda e, cvt=cvt, fc=fc: e.tensor_scalar_mul(
                out=bf(c_xr + fc * 512, 512), in0=cvt,
                scalar1=f32(c_lnw, 32)[:, fc:fc + 1]),
                reads=[f"cv{fc % 2}", "clnw"], writes=wx)
            pump(1)

    def sweep_inproj(blocks, hh):
        for bi, oc in enumerate(blocks):
            wsl = bi % 2
            wb = bf(c_Wb + wsl * 4096, 4096)
            dma("gpsimd", wb, Wip[oc], writes=[f"wb{wsl}"])
            bank = bi % 2
            for kc in range(KCB):
                S.op("tensor", lambda e, wb=wb, kc=kc, bank=bank: e.matmul(
                    out=pf[bank][:, 0:512], lhsT=wb[:, kc * 128:(kc + 1) * 128],
                    rhs=bf(c_xr + kc * 512, 512), start=(kc == 0),
                    stop=(kc == KCB - 1)),
                    reads=[f"wb{wsl}", "xr"], writes=[f"pf{bank}"])
            mupc = f32(c_mup, 67)[:, oc:oc + 1]
            if oc < 32:
                zev = bf(c_zs + (oc % 2) * 512, 512)
                S.op("scalar", lambda e, bank=bank, mupc=mupc, zev=zev: e.activation(
                    out=zev, in_=pf[bank][:, 0:512],
                    func=AF.Silu, scale=mupc),
                    reads=[f"pf{bank}", "cmup"], writes=[f"zev{oc % 2}", "zsl0"])
                for c2 in range(NCH):
                    dma("sync", zscr[hh * NCH + c2, :, oc, :],
                        zev[:, c2 * 128:(c2 + 1) * 128],
                        reads=[f"zev{oc % 2}"], writes=[f"zd{hh}_{c2}"])
            elif oc < 66:
                cc = oc - 32
                S.op("scalar", lambda e, cc=cc, bank=bank, mupc=mupc: e.activation(
                    out=bf(c_xbcs + cc * 515 + 3, 512), in_=pf[bank][:, 0:512],
                    func=AF.Copy, scale=mupc),
                    reads=[f"pf{bank}", "cmup"],
                    writes=[f"xbc{cc}", f"sil{cc}"] + (["attn"] if cc == 0 else []))
            else:
                S.op("scalar", lambda e, bank=bank: e.activation(
                    out=dtraw_h, in_=pf[bank][0:64, 0:512], func=AF.Copy,
                    scale=ARf[0:64, c_mup // 2 + oc:c_mup // 2 + oc + 1]),
                    reads=[f"pf{bank}", "cmup"], writes=["dtraw"])
            pump(1)

    def conv_half(hh):
        xb3 = AR[:, c_xbcs:c_xbcs + 17510].rearrange("p (c t) -> p c t", t=515)
        hl3 = bf(c_halo, 102).rearrange("p (c t) -> p c t", t=3)
        S.op("vector", lambda e: e.tensor_copy(out=xb3[:, :, 0:3], in_=hl3),
             reads=["halo"] + [f"xbc{cc}" for cc in range(34)],
             writes=[f"xbch{cc}" for cc in range(34)])
        S.op("vector", lambda e: e.tensor_copy(out=hl3, in_=xb3[:, :, 512:515]),
             reads=[f"xbc{cc}" for cc in range(34)], writes=["halo"])
        for cc in range(34):
            sl = cc % 2
            cvt = f32(c_cv + sl * 1024, 512)

            def stripe(k, cc=cc):
                return bf(c_xbcs + cc * 515 + k, 512)

            def wcol(k, cc=cc):
                return f32(c_cw, 136)[:, cc * 4 + k:cc * 4 + k + 1]

            S.op("vector", lambda e, cvt=cvt, cc=cc: e.tensor_scalar_mul(
                out=cvt, in0=stripe(0, cc), scalar1=wcol(0, cc)),
                reads=[f"xbc{cc}", f"xbch{cc}", "ccw"], writes=[f"cv{sl}"])
            for k in range(1, 4):
                S.op("vector", lambda e, cvt=cvt, cc=cc, k=k: e.scalar_tensor_tensor(
                    out=cvt, in0=stripe(k, cc), scalar=wcol(k, cc), in1=cvt,
                    op0=ALU.mult, op1=ALU.add),
                    reads=[f"xbc{cc}", f"xbch{cc}", "ccw", f"cv{sl}"],
                    writes=[f"cv{sl}"])
            S.op("scalar", lambda e, cvt=cvt, cc=cc: e.activation(
                out=bf(c_xbcs + cc * 515 + 3, 512), in_=cvt, func=AF.Silu),
                reads=[f"cv{sl}"], writes=[f"sil{cc}", f"xbc{cc}"])
            pump(1)

    def dt_half(hh):
        ta = f32(c_cv, 512, 0, 64)
        tb = f32(c_cv + 1024, 512, 0, 64)
        tc = f32(c_yb, 512, 0, 64)
        S.op("vector", lambda e: e.tensor_scalar_add(out=ta, in0=dtraw_h,
                                                     scalar1=sm_dtb),
             reads=["dtraw", "cdtb"], writes=["cv0"])
        S.op("vector", lambda e: e.tensor_scalar_max(out=tb, in0=ta, scalar1=0.0),
             reads=["cv0"], writes=["cv1"])
        S.op("scalar", lambda e: e.activation(out=tc, in_=ta, func=AF.Abs),
             reads=["cv0"], writes=["yb0"])
        S.op("scalar", lambda e: e.activation(out=tc, in_=tc, func=AF.Exp,
                                              scale=-1.0),
             reads=["yb0"], writes=["yb0"])
        S.op("scalar", lambda e: e.activation(out=tc, in_=tc, func=AF.Ln, bias=1.0),
             reads=["yb0"], writes=["yb0"])
        S.op("vector", lambda e: e.tensor_add(out=dts_h, in0=tb, in1=tc),
             reads=["cv1", "yb0"], writes=["dts"])

    def chunk2(hh, c2, is_R):
        t0 = c2 * QQ
        dts = dts_h[:, t0:t0 + QQ]
        csl = c2 % 2
        ta = f32(c_cv, 512, 0, 64)[:, 0:128]
        tb = f32(c_cv + 1024, 512, 0, 64)[:, 0:128]
        tc = f32(c_yb, 512, 0, 64)[:, 0:128]

        S.op("vector", lambda e: e.tensor_scalar_mul(out=dtAf, in0=dts,
                                                     scalar1=sm_Acol),
             reads=["dts", "cA"], writes=["dtA"])
        S.op("tensor", lambda e: e.transpose(out=pf[5][:, 0:64], in_=dtAf,
                                             identity=f32(c_if, 128)[0:64, 0:64]),
             reads=["dtA", "cIf", "pf5"], writes=["pf5"])
        S.op("vector", lambda e, csl=csl: e.tensor_copy(out=dtATM(csl),
                                                        in_=pf[5][:, 0:64]),
             reads=["pf5"], writes=["dtATM"])
        S.op("tensor", lambda e, csl=csl: e.matmul(
            out=pf[5][0:64, 64:192], lhsT=dtATM(csl), rhs=f32(c_U, 128),
            start=True, stop=True),
            reads=["dtATM", "cU", "pf5"], writes=["pf5"])
        S.op("vector", lambda e, csl=csl: e.tensor_copy(out=sfm(csl),
                                                        in_=pf[5][0:64, 64:192]),
             reads=["pf5"], writes=["sfm"])
        dma("sync", srowscr, sfm(csl), reads=["sfm"], writes=["srowd"])
        S.op("vector", lambda e, csl=csl: e.tensor_scalar(
            out=ta, in0=sfm(csl), scalar1=sfm(csl)[:, 127:128], scalar2=None,
            op0=ALU.subtract),
            reads=["sfm"], writes=["cv0"])
        S.op("scalar", lambda e: e.activation(out=tb, in_=ta, func=AF.Exp,
                                              scale=-1.0),
             reads=["cv0"], writes=["cv1"])
        S.op("vector", lambda e, csl=csl: e.tensor_mul(out=wfm(csl), in0=tb,
                                                       in1=dts),
             reads=["cv1", "dts"], writes=["wfm"])
        S.op("tensor", lambda e, csl=csl: e.transpose(
            out=pf[5][:, 192:256], in_=wfm(csl),
            identity=f32(c_if, 128)[0:64, 0:64]),
            reads=["wfm", "cIf", "pf5"], writes=["pf5"])
        S.op("vector", lambda e, csl=csl: e.tensor_copy(out=wT(csl),
                                                        in_=pf[5][:, 192:256]),
             reads=["pf5"], writes=["wTk"])
        if not is_R:
            S.op("scalar", lambda e, dts=dts: e.activation(out=lndf, in_=dts,
                                                           func=AF.Ln),
                 reads=["dts"], writes=["lndtk"])
            S.op("vector", lambda e, csl=csl: e.tensor_sub(out=tc, in0=lndf,
                                                           in1=sfm(csl)),
                 reads=["lndtk", "sfm"], writes=["yb0"])
            S.op("tensor", lambda e: e.transpose(
                out=pf[5][:, 256:320], in_=tc,
                identity=f32(c_if, 128)[0:64, 0:64]),
                reads=["yb0", "cIf", "pf5"], writes=["pf5"])
            S.op("vector", lambda e, csl=csl: e.tensor_copy(out=vT(csl),
                                                            in_=pf[5][:, 256:320]),
                 reads=["pf5"], writes=["vTk"])
        if not is_R:
            zsl_ = bf(c_zs + csl * 4096, 4096)
            dma("sync", zsl_, zscr[hh * NCH + c2],
                reads=[f"zd{hh}_{c2}"], writes=[f"zsl{csl}"])
        Bfm = bf(c_xbcs + 32 * 515 + 3 + t0, 128)
        Cfm = bf(c_xbcs + 33 * 515 + 3 + t0, 128)
        S.op("tensor", lambda e, Bfm=Bfm: e.transpose(
            out=pb[0][:, 0:128], in_=Bfm, identity=bf(c_ib, 128)),
            reads=["sil32", "cIb", "pb0"], writes=["pb0"])
        S.op("vector", lambda e, csl=csl: e.tensor_copy(out=B_TM(csl),
                                                        in_=pb[0][:, 0:128]),
             reads=["pb0"], writes=["BTM"])
        if not is_R:
            S.op("tensor", lambda e, Bfm=Bfm, Cfm=Cfm: e.matmul(
                out=pf[0][:, 0:128], lhsT=Bfm, rhs=Cfm, start=True, stop=True),
                reads=["sil32", "sil33", "pf0"], writes=["pf0"])
            S.op("vector", lambda e, csl=csl: e.tensor_copy(
                out=bf(c_Gt, 128), in_=pf[0][:, 0:128]),
                reads=["pf0"], writes=["GtB"])
        for fc in range(KCB):
            pbank = (fc // 8) % 2
            S.op("tensor", lambda e, fc=fc, pbank=pbank, t0=t0: e.transpose(
                out=pb[pbank][:, (fc % 8) * 128:(fc % 8 + 1) * 128],
                in_=bf(c_xbcs + fc * 515 + 3 + t0, 128), identity=bf(c_ib, 128)),
                reads=[f"sil{fc}", "cIb", f"pb{pbank}"], writes=[f"pb{pbank}"])
            if fc % 8 == 7:
                S.op("vector", lambda e, fc=fc, pbank=pbank: e.tensor_copy(
                    out=bf(c_Xs + (fc - 7) * 128, 1024),
                    in_=pb[pbank][:, 0:1024]),
                    reads=[f"pb{pbank}"], writes=["XsT"])

        for g8 in range(8):
            ssl = g8 % 2
            sr = srow(ssl)
            cr = crow(ssl)
            dma("sync", sr, srowscr[g8 * 8:(g8 + 1) * 8, :],
                reads=["srowd"], writes=[f"srow{ssl}"])
            S.op("scalar", lambda e, sr=sr, cr=cr: e.activation(
                out=cr[0:1, 0:8], in_=sr[0:1, 127::128], func=AF.Exp),
                reads=[f"srow{ssl}"], writes=[f"crow{ssl}"])
            S.op("tensor", lambda e, cr=cr, ssl=ssl: e.matmul(
                out=pf[5][:, 0:512], lhsT=oA(32 * ssl),
                rhs=cr[0:1, 0:8].rearrange("a (b c) -> a b c", c=1)
                .broadcast_to([1, 8, 64]),
                start=True, stop=True),
                reads=[f"crow{ssl}", "o1", "pf5"], writes=["pf5"])
            S.op("vector", lambda e, g8=g8: e.tensor_tensor(
                out=f32(c_hdec + (g8 % 2) * 1024, 512),
                in0=f32(c_hT, 4096)[:, g8 * 512:(g8 + 1) * 512],
                in1=pf[5][:, 0:512], op=ALU.mult),
                reads=[f"hT{g8}", "pf5"], writes=[f"hdec{g8 % 2}"])
            ybank = g8 % 2
            if not is_R:
                S.op("scalar", lambda e, sr=sr, ssl=ssl: e.activation(
                    out=erow(ssl), in_=sr, func=AF.Exp),
                    reads=[f"srow{ssl}"], writes=[f"erow{ssl}"])
                S.op("vector", lambda e, g8=g8, ssl=ssl: e.tensor_copy(
                    out=bf(c_hbf + ssl * 1024, 512),
                    in_=f32(c_hT, 4096)[:, g8 * 512:(g8 + 1) * 512]),
                    reads=[f"hT{g8}"], writes=[f"hbf{ssl}"])
                for j4 in range(2):
                    S.op("tensor", lambda e, sr=sr, j4=j4, ssl=ssl: e.matmul(
                        out=pf[3][:, 0:512], lhsT=oA(32 * ssl),
                        rhs=sr[0:1, j4 * 512:(j4 + 1) * 512],
                        start=True, stop=True),
                        reads=[f"srow{ssl}", "o1"], writes=["pf3"])
                    S.op("tensor", lambda e, j4=j4, ssl=ssl: e.matmul(
                        out=pf[4][:, 0:512], lhsT=oA(32 * ssl),
                        rhs=erow(ssl)[0:1, j4 * 512:(j4 + 1) * 512],
                        start=True, stop=True),
                        reads=[f"erow{ssl}", "o1"], writes=["pf4"])
                    for jj in range(4):
                        j8 = j4 * 4 + jj
                        r = g8 * 8 + j8
                        q = j8 // 2
                        pcol = (j8 % 2) * 64
                        msl = j8 % 2
                        scol = jj * 128
                        S.op("vector", lambda e, r=r, csl=csl, msl=msl,
                             scol=scol: e.scalar_tensor_tensor(
                            out=f32(c_Dm + msl * 256, 128),
                            in0=pf[3][:, scol:scol + 128],
                            scalar=vT(csl)[:, r:r + 1], in1=f32(c_mask, 128),
                            op0=ALU.add, op1=ALU.add),
                            reads=["pf3", "vTk", "cMk"], writes=[f"Dm{msl}"])
                        S.op("scalar", lambda e, msl=msl: e.activation(
                            out=bf(c_E + msl * 128, 128),
                            in_=f32(c_Dm + msl * 256, 128), func=AF.Exp),
                            reads=[f"Dm{msl}"], writes=[f"E{msl}"])
                        S.op("vector", lambda e, msl=msl: e.tensor_mul(
                            out=bf(c_Mt + msl * 128, 128),
                            in0=bf(c_E + msl * 128, 128),
                            in1=bf(c_Gt, 128)),
                            reads=[f"E{msl}", "GtB"], writes=[f"Mt{msl}"])
                        S.op("vector", lambda e, msl=msl, scol=scol,
                             Cfm=Cfm: e.tensor_tensor(
                            out=bf(c_Cx + msl * 128, 128), in0=Cfm,
                            in1=pf[4][:, scol:scol + 128], op=ALU.mult),
                            reads=["sil33", "pf4"], writes=[f"Cx{msl}"])
                        S.op("tensor", lambda e, r=r, msl=msl, q=q, pcol=pcol,
                             ybank=ybank: e.matmul(
                            out=pf[ybank][pcol:pcol + 64, q * 128:(q + 1) * 128],
                            lhsT=bf(c_Xs + r * 64, 64),
                            rhs=bf(c_Mt + msl * 128, 128),
                            start=True, stop=False, tile_position=(0, pcol)),
                            reads=["XsT", f"Mt{msl}"], writes=[f"pf{ybank}"])
                        S.op("tensor", lambda e, j8=j8, msl=msl, q=q, pcol=pcol,
                             ssl=ssl, ybank=ybank: e.matmul(
                            out=pf[ybank][pcol:pcol + 64, q * 128:(q + 1) * 128],
                            lhsT=bf(c_hbf + ssl * 1024, 512)[:, j8 * 64:(j8 + 1) * 64],
                            rhs=bf(c_Cx + msl * 128, 128),
                            start=False, stop=(j8 == 7), tile_position=(0, pcol)),
                            reads=[f"hbf{ssl}", f"Cx{msl}"], writes=[f"pf{ybank}"])
            S.op("tensor", lambda e, g8=g8: e.matmul(
                out=pf[2][:, 0:512], lhsT=f32(c_if, 128),
                rhs=f32(c_hdec + (g8 % 2) * 1024, 512),
                start=True, stop=False),
                reads=[f"hdec{g8 % 2}", "cIf", "pf2kk"], writes=["pf2"])
            for j8 in range(8):
                r = g8 * 8 + j8
                msl = j8 % 2
                S.op("vector", lambda e, r=r, msl=msl, csl=csl: e.tensor_scalar_mul(
                    out=bf(c_Bw + msl * 128, 128), in0=B_TM(csl),
                    scalar1=wT(csl)[:, r:r + 1]),
                    reads=["BTM", "wTk"], writes=[f"Bw{msl}"])
                S.op("tensor", lambda e, r=r, msl=msl, j8=j8, g8=g8: e.matmul(
                    out=pf[2][:, j8 * 64:(j8 + 1) * 64],
                    lhsT=bf(c_Bw + msl * 128, 128),
                    rhs=bf(c_Xs + r * 64, 64),
                    start=False, stop=(j8 == 7)),
                    reads=[f"Bw{msl}", "XsT"], writes=["pf2"])
            S.op("vector", lambda e, g8=g8: e.tensor_copy(
                out=f32(c_hT, 4096)[:, g8 * 512:(g8 + 1) * 512],
                in_=pf[2][:, 0:512]),
                reads=["pf2"], writes=[f"hT{g8}", "pf2kk"])
            if not is_R:
                for q in range(4):
                    fc = g8 * 4 + q
                    S.op("vector", lambda e, fc=fc, q=q, t0=t0, g8=g8,
                         ybank=ybank: e.scalar_tensor_tensor(
                        out=f32(c_yb + (g8 % 2) * 1024, 512)[:, q * 128:(q + 1) * 128],
                        in0=bf(c_xbcs + fc * 515 + 3 + t0, 128),
                        scalar=f32(c_Dx, 32)[:, fc:fc + 1],
                        in1=pf[ybank][:, q * 128:(q + 1) * 128],
                        op0=ALU.mult, op1=ALU.add),
                        reads=[f"sil{fc}", "cD", f"pf{ybank}"],
                        writes=[f"yb{g8 % 2}"])
                S.op("vector", lambda e, g8=g8, csl=csl: e.tensor_tensor(
                    out=bf(c_g + g8 * 512, 512),
                    in0=f32(c_yb + (g8 % 2) * 1024, 512),
                    in1=bf(c_zs + csl * 4096 + g8 * 512, 512), op=ALU.mult),
                    reads=[f"yb{g8 % 2}", f"zsl{csl}"], writes=["gbuf"])
            pump(1)
        if is_R:
            return
        for i8 in range(8):
            sq = bf(c_sq + (i8 % 2) * 512, 512)
            S.op("scalar", lambda e, i8=i8, sq=sq: e.activation(
                out=sq, in_=bf(c_g + i8 * 512, 512), func=AF.Square),
                reads=["gbuf"], writes=[f"sq{i8 % 2}"])
            S.op("tensor", lambda e, sq=sq, i8=i8: e.matmul(
                out=pf[0][0:1, 0:512], lhsT=onesb, rhs=sq,
                start=(i8 == 0), stop=(i8 == 7)),
                reads=[f"sq{i8 % 2}", "ob"], writes=["pf0"])
        r2 = pool("be", 128, 512)
        S.op("vector", lambda e: e.tensor_copy(out=pool("t1", 512),
                                               in_=pf[0][0:1, 0:512]),
             reads=["pf0"], writes=["p_t1c"])
        S.op("vector", lambda e: e.tensor_add(out=pool("t2", 128),
                                              in0=pool("t1", 128),
                                              in1=pool("t1", 128, 128)),
             reads=["p_t1c"], writes=["p_t2c"])
        S.op("vector", lambda e: e.tensor_add(out=pool("t2", 128, 128),
                                              in0=pool("t1", 128, 256),
                                              in1=pool("t1", 128, 384)),
             reads=["p_t1c"], writes=["p_t2c"])
        S.op("vector", lambda e, r2=r2: e.tensor_add(out=r2,
                                                     in0=pool("t2", 128),
                                                     in1=pool("t2", 128, 128)),
             reads=["p_t2c"], writes=["p_r2"])
        S.op("vector", lambda e, r2=r2: e.tensor_scalar_mul(out=r2, in0=r2,
                                                            scalar1=1.0 / H),
             reads=["p_r2"], writes=["p_r2"])
        S.op("scalar", lambda e, r2=r2: e.activation(out=r2, in_=r2,
                                                     func=AF.Sqrt, bias=EPS),
             reads=["p_r2"], writes=["p_r2"])
        S.op("vector", lambda e, r2=r2: e.reciprocal(out=r2, in_=r2),
             reads=["p_r2"], writes=["p_r2"])
        S.op("tensor", lambda e, r2=r2: e.matmul(out=pf[0][:, 0:128],
                                                 lhsT=ones1, rhs=r2,
                                                 start=True, stop=True),
             reads=["p_r2", "o1"], writes=["pf0"])
        for fc in range(KCB):
            S.op("vector", lambda e, fc=fc: e.tensor_tensor(
                out=f32(c_cv + (fc % 2) * 1024, 512)[:, 0:128],
                in0=bf(c_g + fc * 128, 128), in1=pf[0][:, 0:128], op=ALU.mult),
                reads=["gbuf", "pf0"], writes=[f"cv{fc % 2}"])
            wg_ = ["gn"] + (["xr"] if (fc == 0 and c2 == 0) else [])
            S.op("vector", lambda e, fc=fc, t0=t0: e.tensor_scalar_mul(
                out=bf(c_gn + fc * 512 + t0, 128),
                in0=f32(c_cv + (fc % 2) * 1024, 512)[:, 0:128],
                scalar1=f32(c_nw, 32)[:, fc:fc + 1]),
                reads=[f"cv{fc % 2}", "cnw"], writes=wg_)

    def postln_scalars(s1_src, s2_src, s1_reads, s2_reads):
        S.op("vector", lambda e: e.tensor_copy(out=pool("sa", 512), in_=s1_src),
             reads=s1_reads, writes=["p_sa"])
        S.op("vector", lambda e: e.tensor_copy(out=pool("sb", 512), in_=s2_src),
             reads=s2_reads, writes=["p_sb"])
        S.op("vector", lambda e: e.tensor_scalar_mul(out=pool("t1", 512),
                                                     in0=pool("sa", 512),
                                                     scalar1=1.0 / H),
             reads=["p_sa"], writes=["p_t1"])
        S.op("vector", lambda e: e.tensor_scalar_mul(out=pool("sb", 512),
                                                     in0=pool("sb", 512),
                                                     scalar1=1.0 / H),
             reads=["p_sb"], writes=["p_sb"])
        S.op("vector", lambda e: e.tensor_mul(out=pool("t2", 512),
                                              in0=pool("t1", 512),
                                              in1=pool("t1", 512)),
             reads=["p_t1"], writes=["p_t2"])
        S.op("vector", lambda e: e.tensor_sub(out=pool("sb", 512),
                                              in0=pool("sb", 512),
                                              in1=pool("t2", 512)),
             reads=["p_sb", "p_t2"], writes=["p_sb"])
        S.op("scalar", lambda e: e.activation(out=pool("t2", 512),
                                              in_=pool("sb", 512),
                                              func=AF.Sqrt, bias=EPS),
             reads=["p_sb"], writes=["p_t2"])
        S.op("vector", lambda e: e.reciprocal(out=pool("t2", 512),
                                              in_=pool("t2", 512)),
             reads=["p_t2"], writes=["p_t2"])
        S.op("vector", lambda e: e.tensor_mul(out=pool("t1", 512),
                                              in0=pool("t1", 512),
                                              in1=pool("t2", 512)),
             reads=["p_t1", "p_t2"], writes=["p_t1"])

    def outproj_half(hh):
        for oc in range(FFB):
            wsl = oc % 2
            wb = bf(c_Wb + wsl * 4096, 4096)
            dma("gpsimd", wb, Wop[oc], writes=[f"wb{wsl}"])
            bank = oc % 2
            for kc in range(KCB):
                S.op("tensor", lambda e, wb=wb, kc=kc, bank=bank: e.matmul(
                    out=pf[bank][:, 0:512], lhsT=wb[:, kc * 128:(kc + 1) * 128],
                    rhs=bf(c_gn + kc * 512, 512), start=(kc == 0),
                    stop=(kc == KCB - 1)),
                    reads=[f"wb{wsl}", "gn"], writes=[f"pf{bank}"])
            wka = ["attn"]
            if oc == 0:
                wka += [f"sil{cc}" for cc in range(34)] +                        [f"xbc{cc}" for cc in range(34)] +                        [f"xbch{cc}" for cc in range(34)]
            S.op("scalar", lambda e, oc=oc, bank=bank: e.activation(
                out=bf(c_attn + oc * 512, 512), in_=pf[bank][:, 0:512],
                func=AF.Copy),
                reads=[f"pf{bank}"], writes=wka)
            S.op("tensor", lambda e, oc=oc: e.matmul(
                out=pf[4][0:1, 0:512], lhsT=onesb,
                rhs=bf(c_attn + oc * 512, 512),
                start=(oc == 0), stop=(oc == FFB - 1)),
                reads=["attn", "ob"], writes=["pf4"])
            sq = bf(c_sq + (oc % 2) * 512, 512)
            S.op("scalar", lambda e, oc=oc, sq=sq: e.activation(
                out=sq, in_=bf(c_attn + oc * 512, 512), func=AF.Square),
                reads=["attn"], writes=[f"sq{oc % 2}"])
            S.op("tensor", lambda e, sq=sq, oc=oc: e.matmul(
                out=pf[5][0:1, 0:512], lhsT=onesb, rhs=sq,
                start=(oc == 0), stop=(oc == FFB - 1)),
                reads=[f"sq{oc % 2}", "ob"], writes=["pf5"])
            pump(1)
        postln_scalars(pf[4][0:1, 0:512], pf[5][0:1, 0:512], ["pf4"], ["pf5"])
        bcast2(pool("t2", 512), pool("t1", 512), ["p_t2"], ["p_t1"])
        for fc in range(KCB):
            sl = fc % 2
            xt = f32(c_xt + sl * 1024, 512)
            dma("sync", xt, xT[fc * 128:(fc + 1) * 128, hh * HT:(hh + 1) * HT],
                writes=[f"xt{sl}"])
            cvt = f32(c_cv + (fc % 2) * 1024, 512)
            S.op("vector", lambda e, fc=fc, cvt=cvt: e.tensor_tensor(
                out=cvt, in0=bf(c_attn + fc * 512, 512), in1=pf[2][:, 0:512],
                op=ALU.mult),
                reads=["attn", "pf2"], writes=[f"cv{fc % 2}"])
            S.op("vector", lambda e, cvt=cvt: e.tensor_tensor(
                out=cvt, in0=cvt, in1=pf[3][:, 0:512], op=ALU.subtract),
                reads=[f"cv{fc % 2}", "pf3"], writes=[f"cv{fc % 2}"])
            S.op("vector", lambda e, cvt=cvt, xt=xt: e.tensor_add(
                out=cvt, in0=cvt, in1=xt),
                reads=[f"cv{fc % 2}", f"xt{sl}"], writes=[f"cv{fc % 2}"])
            dma("sync", x2scr[fc, :, hh * HT:(hh + 1) * HT], cvt,
                reads=[f"cv{fc % 2}"], writes=[f"x2_{fc}_{hh}"])
            sq = bf(c_sq + (fc % 2) * 512, 512)
            S.op("scalar", lambda e, cvt=cvt, sq=sq: e.activation(
                out=sq, in_=cvt, func=AF.Square),
                reads=[f"cv{fc % 2}"], writes=[f"sq{fc % 2}"])
            S.op("tensor", lambda e, sq=sq, fc=fc: e.matmul(
                out=pf[4][0:1, 0:512], lhsT=onesb, rhs=sq,
                start=(fc == 0), stop=(fc == KCB - 1)),
                reads=[f"sq{fc % 2}", "ob"], writes=["pf4"])
            pump(1)
        r3 = pool("r3", 512, hh * 512)
        mr3 = pool("mr3", 512, hh * 512)
        S.op("vector", lambda e: e.tensor_copy(out=pool("sa", 512),
                                               in_=pf[4][0:1, 0:512]),
             reads=["pf4"], writes=["p_sa"])
        S.op("vector", lambda e: e.tensor_scalar_mul(out=pool("t1", 512),
                                                     in0=pool("sa", 512),
                                                     scalar1=1.0 / H),
             reads=["p_sa"], writes=["p_t1"])
        S.op("vector", lambda e, hh=hh: e.tensor_mul(
            out=pool("t2", 512), in0=pool("mx", 512, hh * 512),
            in1=pool("mx", 512, hh * 512)),
            reads=[f"p_mx{hh}"], writes=["p_t2"])
        S.op("vector", lambda e: e.tensor_sub(out=pool("t1", 512),
                                              in0=pool("t1", 512),
                                              in1=pool("t2", 512)),
             reads=["p_t1", "p_t2"], writes=["p_t1"])
        S.op("scalar", lambda e, r3=r3: e.activation(out=r3, in_=pool("t1", 512),
                                                     func=AF.Sqrt, bias=EPS),
             reads=["p_t1"], writes=["p_r3"])
        S.op("vector", lambda e, r3=r3: e.reciprocal(out=r3, in_=r3),
             reads=["p_r3"], writes=["p_r3"])
        S.op("vector", lambda e, r3=r3, mr3=mr3, hh=hh: e.tensor_mul(
            out=mr3, in0=pool("mx", 512, hh * 512), in1=r3),
            reads=["p_r3", f"p_mx{hh}"], writes=["p_mr3"])

    def emit_scales():
        for mi in range(3):
            red = ARf[:, c_acc // 2 + 116 + mi:c_acc // 2 + 117 + mi]
            S.op("vector", lambda e, red=red, mi=mi: e.reduce_sum(
                out=red, in_=acc[:, 88 + 8 * mi:88 + 8 * mi + 8],
                axis=mybir.AxisListType.X),
                reads=[f"sum{k}" for k in range(8 * mi, 8 * mi + 8)],
                writes=[f"red{mi}"])
            S.op("tensor", lambda e, red=red, mi=mi: e.matmul(
                out=pf[4][0:1, mi:mi + 1], lhsT=red, rhs=onesf,
                start=(mi == 0), stop=(mi == 2)),
                reads=[f"red{mi}", "of"], writes=["pf4"])
        S.op("vector", lambda e: e.tensor_copy(out=pool("t1", 3),
                                               in_=pf[4][0:1, 0:3]),
             reads=["pf4"], writes=["p_sc"])
        S.op("vector", lambda e: e.tensor_scalar_mul(
            out=pool("t1", 3, 4), in0=pool("t1", 3), scalar1=1.0 / (NI * H)),
            reads=["p_sc"], writes=["p_sc"])
        S.op("vector", lambda e: e.tensor_scalar(
            out=pool("t1", 3, 8), in0=pool("t1", 3, 4), scalar1=EPS, scalar2=0.5,
            op0=ALU.add, op1=ALU.mult),
            reads=["p_sc"], writes=["p_sc"])
        S.op("vector", lambda e: e.tensor_mul(
            out=pool("t1", 1, 12), in0=pool("t1", 1, 4), in1=pool("t1", 1, 5)),
            reads=["p_sc"], writes=["p_sc"])
        S.op("vector", lambda e: e.tensor_mul(
            out=pool("t1", 1, 12), in0=pool("t1", 1, 12), in1=pool("t1", 1, 6)),
            reads=["p_sc"], writes=["p_sc"])
        S.op("vector", lambda e: e.tensor_scalar_mul(
            out=pool("t1", 1, 12), in0=pool("t1", 1, 12), scalar1=1.0 / 8.0),
            reads=["p_sc"], writes=["p_sc"])
        for mi in range(3):
            S.op("tensor", lambda e, mi=mi: e.matmul(
                out=pf[4][:, 8 + mi:9 + mi], lhsT=ones1,
                rhs=pool("t1", 1, 8 + mi), start=True, stop=True),
                reads=["p_sc", "o1"], writes=["pf4"])
        S.op("tensor", lambda e: e.matmul(
            out=pf[4][:, 12:13], lhsT=ones1, rhs=pool("t1", 1, 12),
            start=True, stop=True),
            reads=["p_sc", "o1"], writes=["pf4"])
        for mi in range(3):
            S.op("vector", lambda e, mi=mi: e.tensor_copy(
                out=sm_T[mi], in_=pf[4][:, 8 + mi:9 + mi]),
                reads=["pf4"], writes=[f"Tc{mi}", "Tcs"])
            S.op("vector", lambda e, mi=mi: e.tensor_scalar_mul(
                out=sm_negT[mi], in0=sm_T[mi], scalar1=-1.0),
                reads=[f"Tc{mi}"], writes=[f"nTc{mi}", "Tcs"])
        S.op("vector", lambda e: e.tensor_copy(out=sm_eta, in_=pf[4][:, 12:13]),
             reads=["pf4"], writes=["eta"])
        if dbg:
            dma("sync", DB["d_scal"], pool("t1", 16), reads=["p_sc"],
                writes=["dscal"])

    for hh in range(2):
        stats_half(xPT, hh, True, "")
        xr_half(xPT, hh)
        sweep_inproj(list(range(32, OCB)), hh)
        conv_half(hh)
        dt_half(hh)
        for c2 in range(NCH):
            chunk2(hh, c2, is_R=True)
    if dbg:
        dma("sync", DB["d_h"], f32(c_hT, 4096),
            reads=[f"hT{g}" for g in range(8)], writes=["dh"])

    for hh in range(2):
        stats_half(xT, hh, True, "mx")
        xr_half(xT, hh)
        sweep_inproj(list(range(32, OCB)), hh)
        conv_half(hh)
        dt_half(hh)
        sweep_inproj(list(range(0, 32)), hh)
        if dbg and hh == 0:
            dma("gpsimd", DB["d_sil"], bf(c_xbcs, 17536),
                reads=[f"sil{cc}" for cc in range(34)], writes=["dsil"])
            dma("gpsimd", DB["d_xr"], bf(c_xr, 16384), reads=["xr"], writes=["dxr"])
        for c2 in range(NCH):
            chunk2(hh, c2, is_R=False)
            if dbg and hh == 0 and c2 == 0:
                dma("gpsimd", DB["d_g0"], bf(c_g, 4096), reads=["gbuf"],
                    writes=["dg0"])
            if dbg:
                dma("gpsimd", DB["d_gall"][:, (hh * NCH + c2) * 4096:
                                           (hh * NCH + c2 + 1) * 4096],
                    bf(c_g, 4096), reads=["gbuf"], writes=["dgall"])
        if dbg:
            dma("sync", DB["d_hM"][:, hh * 4096:(hh + 1) * 4096],
                f32(c_hT, 4096), reads=[f"hT{g}" for g in range(8)],
                writes=["dhM"])
        outproj_half(hh)
        if dbg and hh == 0:
            dma("gpsimd", DB["d_attn"], bf(c_attn, 16384), reads=["attn"],
                writes=["datt"])
            dma("gpsimd", DB["d_gn"], bf(c_gn, 16384), reads=["gn"], writes=["dgn"])
    if dbg:
        dma("sync", DB["d_dts"][:, 512:1024], dts_h, reads=["dts"],
            writes=["ddts"])

    while bg1:
        pump(4)
    emit_scales()
    S.op("vector", lambda e: e.engine_nop(),
         writes=["qt", "gbuf", "wdqa0", "wdqa1", "XsT", "wdqo", "hbf0", "hbf1",
                 "wqg0", "wqg1", "wqu0", "wqu1", "zev0", "zev1", "zsl0", "zsl1",
                 "q_wqg", "q_wqu", "wb0", "wb1", "xr", "attn", "gn", "halo", "ln3",
                 "cv0", "cv1", "it0", "it1", "yb0", "yb1"] +
                [f"sil{cc}" for cc in range(34)] +
                [f"xbc{cc}" for cc in range(34)] +
                [f"xbch{cc}" for cc in range(34)])
    for ib in range(ICB):
        for j in range(4):
            bg2.append((emit_wdq, ib, j))

    for hh in range(2):
        S.op("tensor", lambda e, hh=hh: e.matmul(
            out=pf[2][:, 0:512], lhsT=ones1,
            rhs=pool("r3", 512, hh * 512), start=True, stop=True),
            reads=["p_r3", "o1"], writes=["pf2"])
        S.op("tensor", lambda e, hh=hh: e.matmul(
            out=pf[3][:, 0:512], lhsT=ones1,
            rhs=pool("mr3", 512, hh * 512), start=True, stop=True),
            reads=["p_mr3", "o1"], writes=["pf3"])
        for fc in range(KCB):
            sl = fc % 2
            xt = f32(c_xt + sl * 1024, 512)
            dma("sync", xt, x2scr[fc, :, hh * HT:(hh + 1) * HT],
                reads=[f"x2_{fc}_{hh}"], writes=[f"xt{sl}"])
            cvt = f32(c_cv + (fc % 2) * 1024, 512)
            S.op("vector", lambda e, xt=xt, cvt=cvt: e.tensor_tensor(
                out=cvt, in0=xt, in1=pf[2][:, 0:512], op=ALU.mult),
                reads=[f"xt{sl}", "pf2"], writes=[f"cv{fc % 2}"])
            wkeys = ["ln3"]
            S.op("vector", lambda e, cvt=cvt, fc=fc, hh=hh: e.tensor_tensor(
                out=bf(c_ln3 + fc * 1024 + hh * 512, 512), in0=cvt,
                in1=pf[3][:, 0:512], op=ALU.subtract),
                reads=[f"cv{fc % 2}", "pf3"], writes=wkeys)
            pump(1)

    def quant_into(dst, WT, ib, Tb, nTb, rk):
        base = c_wq if rk == "wqg" else c_wq + 4096
        q = "sync" if rk == "wqg" else "scalar"
        for j in range(4):
            wp = f32(base + (j % 2) * 2048, 1024)
            dma(q, wp, WT[ib][:, j * 1024:(j + 1) * 1024],
                writes=[f"{rk}{j % 2}"])
            qtp = bf(c_yb, 1024)
            S.op("scalar", lambda e, wp=wp, j=j, dst=dst, Tb=Tb: e.activation(
                out=dst[:, j * 1024:(j + 1) * 1024], in_=wp, func=AF.Sign,
                bias=Tb),
                reads=[f"{rk}{j % 2}", "Tcs"], writes=["q_" + rk])
            S.op("scalar", lambda e, wp=wp, qtp=qtp, nTb=nTb: e.activation(
                out=qtp, in_=wp, func=AF.Sign, bias=nTb),
                reads=[f"{rk}{j % 2}", "Tcs"], writes=["qt"])
            S.op("vector", lambda e, j=j, dst=dst, qtp=qtp: e.tensor_add(
                out=dst[:, j * 1024:(j + 1) * 1024],
                in0=dst[:, j * 1024:(j + 1) * 1024], in1=qtp),
                reads=["q_" + rk, "qt"], writes=["q_" + rk])

    qg = bf(c_qq, 4096)
    qu = bf(c_qq + 4096, 4096)
    for ib in range(ICB):
        quant_into(qg, WgT, ib, sm_T[0], sm_negT[0], "wqg")
        for tt in range(2):
            for kc in range(KCB):
                S.op("tensor", lambda e, kc=kc, tt=tt: e.matmul(
                    out=pf[tt][:, 0:512], lhsT=qg[:, kc * 128:(kc + 1) * 128],
                    rhs=bf(c_ln3 + kc * 1024 + tt * 512, 512),
                    start=(kc == 0), stop=(kc == KCB - 1)),
                    reads=["q_wqg", "ln3"], writes=[f"pf{tt}"])
        quant_into(qu, WuT, ib, sm_T[1], sm_negT[1], "wqu")
        for tt in range(2):
            for kc in range(KCB):
                S.op("tensor", lambda e, kc=kc, tt=tt: e.matmul(
                    out=pf[2 + tt][:, 0:512], lhsT=qu[:, kc * 128:(kc + 1) * 128],
                    rhs=bf(c_ln3 + kc * 1024 + tt * 512, 512),
                    start=(kc == 0), stop=(kc == KCB - 1)),
                    reads=["q_wqu", "ln3"], writes=[f"pf{2 + tt}"])
            gt = f32(c_cv + tt * 1024, 512)
            S.op("scalar", lambda e, gt=gt, tt=tt: e.activation(
                out=gt, in_=pf[tt][:, 0:512], func=AF.Copy),
                reads=[f"pf{tt}"], writes=[f"cv{tt}"])
            it = bf(c_hdec + tt * 1024, 512)
            S.op("vector", lambda e, gt=gt, it=it, tt=tt: e.tensor_tensor(
                out=it, in0=gt, in1=pf[2 + tt][:, 0:512], op=ALU.mult),
                reads=[f"cv{tt}", f"pf{2 + tt}"], writes=[f"it{tt}"])
            dma("sync", intscr[ib, :, tt * HT:(tt + 1) * HT], it,
                reads=[f"it{tt}"], writes=[f"int{ib}_{tt}"])
        pump(4)
    while bg2:
        pump(4)

    S.op("vector", lambda e: e.engine_nop(),
         writes=["wqg0", "wqg1", "wqu0", "wqu1", "q_wqg", "q_wqu", "qt", "ln3",
                 "XsT", "gbuf", "wb0", "wb1", "wdqa0", "wdqa1", "wdqo",
                 "hbf0", "hbf1", "zev0", "zev1", "zsl0", "zsl1", "xr", "attn",
                 "gn", "cv0", "cv1", "it0", "it1", "yb0", "yb1", "frl0", "frl1",
                 "qdl0", "qdl1", "qdl2", "qdl3", "srow0", "srow1",
                 "erow0", "erow1", "crow0", "crow1"])
    OGS = [list(range(6 * i, min(6 * i + 6, FFB))) for i in range(6)]
    for tt in range(2):
        for ib in range(ICB):
            dma("sync" if ib % 2 == 0 else "scalar", bf(c_int + ib * 512, 512),
                intscr[ib, :, tt * HT:(tt + 1) * HT],
                reads=[f"int{ib}_{tt}", "ln3"], writes=[f"ih{ib}"])
        for ogi, og in enumerate(OGS):
            for ib in range(ICB):
                qsl = ib % 4
                qd = bf(c_qd + qsl * 1024, len(og) * 128)
                dma("sync" if ib % 2 == 0 else "scalar",
                    qd, qdscr[ib][:, og[0] * 128:(og[0] + len(og)) * 128],
                    reads=[f"qd{ib}_{og[0] // 8}", f"qd{ib}_{(og[-1]) // 8}"],
                    writes=[f"qdl{qsl}"])
                for j, oc in enumerate(og):
                    S.op("tensor", lambda e, qd=qd, j=j, ib=ib: e.matmul(
                        out=pf[j][:, 0:512], lhsT=qd[:, j * 128:(j + 1) * 128],
                        rhs=bf(c_int + ib * 512, 512),
                        start=(ib == 0), stop=(ib == ICB - 1)),
                        reads=[f"qdl{qsl}", f"ih{ib}"], writes=[f"pf{j}"])
            for j, oc in enumerate(og):
                fop = bf(c_yb + (oc % 2) * 1024, 512)
                S.op("scalar", lambda e, j=j, fop=fop: e.activation(
                    out=fop, in_=pf[j][:, 0:512], func=AF.Copy, scale=sm_eta),
                    reads=[f"pf{j}", "eta"], writes=[f"yb{oc % 2}"])
                dma("sync", foscr[oc, :, tt * HT:(tt + 1) * HT], fop,
                    reads=[f"yb{oc % 2}"], writes=[f"fo{oc}_{tt}"])
                S.op("tensor", lambda e, oc=oc, fop=fop: e.matmul(
                    out=pbf[0][0:1, 0:512], lhsT=onesb, rhs=fop,
                    start=(oc == 0), stop=(oc == FFB - 1)),
                    reads=[f"yb{oc % 2}", "ob"], writes=["pb0"])
                sq = bf(c_sq + (oc % 2) * 512, 512)
                S.op("scalar", lambda e, fop=fop, sq=sq: e.activation(
                    out=sq, in_=fop, func=AF.Square),
                    reads=[f"yb{oc % 2}"], writes=[f"sq{oc % 2}"])
                S.op("tensor", lambda e, sq=sq, oc=oc: e.matmul(
                    out=pbf[1][0:1, 0:512], lhsT=onesb, rhs=sq,
                    start=(oc == 0), stop=(oc == FFB - 1)),
                    reads=[f"sq{oc % 2}", "ob"], writes=["pb1"])
        postln_scalars(pbf[0][0:1, 0:512], pbf[1][0:1, 0:512], ["pb0"], ["pb1"])
        bcast2(pool("t2", 512), pool("t1", 512), ["p_t2"], ["p_t1"])
        for fc in range(KCB):
            sl = fc % 2
            xt = f32(c_xt + sl * 1024, 512)
            dma("sync", xt, x2scr[fc, :, tt * HT:(tt + 1) * HT],
                reads=[f"x2_{fc}_{tt}"], writes=[f"xt{sl}"])
            frl = bf(c_hdec + (fc % 2) * 1024, 512)
            dma("sync", frl, foscr[fc, :, tt * HT:(tt + 1) * HT],
                reads=[f"fo{fc}_{tt}"], writes=[f"frl{fc % 2}"])
            cvt = f32(c_cv + (fc % 2) * 1024, 512)
            S.op("vector", lambda e, fc=fc, cvt=cvt, frl=frl: e.tensor_tensor(
                out=cvt, in0=frl, in1=pf[2][:, 0:512], op=ALU.mult),
                reads=[f"frl{fc % 2}", "pf2"], writes=[f"cv{fc % 2}"])
            S.op("vector", lambda e, cvt=cvt: e.tensor_tensor(
                out=cvt, in0=cvt, in1=pf[3][:, 0:512], op=ALU.subtract),
                reads=[f"cv{fc % 2}", "pf3"], writes=[f"cv{fc % 2}"])
            S.op("vector", lambda e, cvt=cvt, xt=xt: e.tensor_add(
                out=cvt, in0=cvt, in1=xt),
                reads=[f"cv{fc % 2}", f"xt{sl}"], writes=[f"cv{fc % 2}"])
            dma("sync", out[fc * 128:(fc + 1) * 128, tt * HT:(tt + 1) * HT], cvt,
                reads=[f"cv{fc % 2}"], writes=[f"out{fc}_{tt}"])

    S.op("vector", lambda e: e.engine_nop(),
         reads=[f"out{fc}_{tt}" for fc in range(KCB) for tt in range(2)] +
               (["dscal", "dh", "dg0", "datt", "dgn", "ddts", "dsil",
                 "dxr"] if dbg else []),
         writes=["done"])

    sems = {}
    dsems = {}
    for e in Sched.ENGS:
        sems[e] = []
        dsems[e] = []
        for k in range(4):
            sems[e].append(nc.semaphore(f"sem_{e}_{k}").__enter__())
        if e in ("sync", "gpsimd", "scalar", "vector"):
            for k in range(Sched.NLANE):
                dsems[e].append(nc.semaphore(f"dsem_{e}_{k}").__enter__())
    blk_cm = nc.Block()
    blk = blk_cm.__enter__()
    S.finalize(blk, sems, dsems)
    blk_cm.__exit__(None, None, None)
    return nc


_NC_CACHE = {}


def _prep(inp):
    f = np.float32
    x = np.ascontiguousarray(inp["L0_attn_hidden_states"]).astype(f, copy=False)
    _ = inp["L0_attn_residual"]
    ipw = np.asarray(inp["L0_attn_in_proj_w"], f)
    opw = np.asarray(inp["L0_attn_out_proj_w"], f)
    conv_w = np.asarray(inp["L0_attn_conv_w"], f)
    A_log = np.asarray(inp["L0_attn_A_log"], f)
    dt_bias = np.asarray(inp["L0_attn_dt_bias"], f)
    D = np.asarray(inp["L0_attn_D"], f)
    ln_w = np.asarray(inp["L0_attn_ln_w"], f)
    norm_w = np.asarray(inp["L0_attn_norm_w"], f)
    mup = np.asarray(inp["L0_attn_mup_vector"], f)
    wg = np.asarray(inp["L0_ffn_w_gate"], f)
    wu = np.asarray(inp["L0_ffn_w_up"], f)
    wd = np.asarray(inp["L0_ffn_w_down"], f)

    ipw_p = np.zeros((H, OCB * 128), f)
    ipw_p[:, :PROJ_REAL] = ipw
    Wip = np.ascontiguousarray(
        ipw_p.reshape(KCB, 128, OCB, 128).transpose(2, 1, 0, 3)
    ).reshape(OCB, 128, KCB * 128)
    Wop = np.ascontiguousarray(
        opw.reshape(KCB, 128, FFB, 128).transpose(2, 1, 0, 3)
    ).reshape(FFB, 128, KCB * 128)

    def padT(w):
        wt = np.zeros((H, ICB * 128), f)
        wt[:, :NI] = w.T
        return np.ascontiguousarray(
            wt.reshape(KCB, 128, ICB, 128).transpose(2, 1, 0, 3)
        ).reshape(ICB, 128, KCB * 128)

    WgTt = padT(wg)
    WuTt = padT(wu)
    wdt = np.zeros((ICB * 128, H), f)
    wdt[:NI, :] = wd.T
    WdTt = np.ascontiguousarray(wdt.reshape(ICB, 128, FFB * 128))

    identf = np.eye(128, dtype=f)
    Umat = np.triu(np.ones((128, 128), f))
    maskA = np.where(Umat > 0, np.float32(0.0), np.float32(-1e30)).astype(f)
    lnw_t = np.ascontiguousarray(ln_w.reshape(32, 128).T)
    nw_t = np.ascontiguousarray(norm_w.reshape(32, 128).T)
    mup_p = np.zeros((OCB * 128,), f)
    mup_p[:PROJ_REAL] = mup
    mup_t = np.ascontiguousarray(mup_p.reshape(OCB, 128).T)
    Dexp = np.ascontiguousarray(D.reshape(32, 2)[:, np.repeat([0, 1], 64)].T)
    convw_t = np.ascontiguousarray(
        conv_w.reshape(34, 128, 4).transpose(1, 0, 2).reshape(128, 136))
    dtb_c = np.ascontiguousarray(dt_bias.reshape(64, 1))
    alog_c = np.ascontiguousarray(A_log.reshape(64, 1))

    shared = dict(Wip=Wip, Wop=Wop, WgT=WgTt, WuT=WuTt, WdT=WdTt,
                  identf=identf, Umat=Umat, maskA=maskA, lnw_i=lnw_t,
                  nw_i=nw_t, mup_i=mup_t, dtb_i=dtb_c, alog_i=alog_c,
                  Dexp_i=Dexp, convw_i=convw_t)
    in_maps = []
    zeros_half = np.zeros((H, TPC), f)
    for c in range(8):
        b, hh = c // 2, c % 2
        m = dict(shared)
        m["xT"] = np.ascontiguousarray(x[b, hh * TPC:(hh + 1) * TPC, :].T)
        m["xPT"] = (zeros_half if hh == 0
                    else np.ascontiguousarray(x[b, 0:TPC, :].T))
        in_maps.append(m)
    return in_maps


PROJ_REAL = 8512


def kernel(dbg=False, **inp):
    in_maps = _prep(inp)
    key = ("dbg" if dbg else "plain")
    if key not in _NC_CACHE:
        _NC_CACHE[key] = build_nc(dbg=dbg)
    nc = _NC_CACHE[key]
    res = run_bass_kernel_spmd(nc, in_maps, core_ids=list(range(8)))
    outf = np.empty((4, 2048, H), np.float32)
    for c in range(8):
        b, hh = c // 2, c % 2
        outf[b, hh * TPC:(hh + 1) * TPC, :] = \
            np.asarray(res.results[c]["out"], np.float32).T
    if dbg:
        return outf, res
    return outf
